# version 03
# baseline (speedup 1.0000x reference)
import numpy as np
import ml_dtypes
import contextlib
import concourse.bass as bass
import concourse.mybir as mybir
from concourse.bass_utils import run_bass_kernel_spmd

F32 = mybir.dt.float32
BF16 = mybir.dt.bfloat16
AF = mybir.ActivationFunctionType
ALU = mybir.AluOpType
BF = ml_dtypes.bfloat16

NFFT = 8192
N1 = 64
N2 = 128
NCHUNK = 16
CH = 64
DM = 1024
T_FULL = 4096
T_OWN = 2048
PAIRS = 32
SPEC = 4096


def _tables():
    n2 = np.arange(N2); k2 = np.arange(N2)
    n1 = np.arange(N1); k1 = np.arange(N1)
    a1 = -2 * np.pi * np.outer(n2, k2) / N2
    dft128_c = np.cos(a1); dft128_s = np.sin(a1)
    at = -2 * np.pi * np.outer(k2, n1) / NFFT
    tw_c = np.cos(at); tw_s = np.sin(at)
    a2 = -2 * np.pi * np.outer(n1, k1) / N1
    dft64_c = np.cos(a2); dft64_s = np.sin(a2)
    ai1 = 2 * np.pi * np.outer(k1, n1) / N1
    idft64_c = np.cos(ai1); idft64_s = np.sin(ai1)
    ait = 2 * np.pi * np.outer(n1, k2) / NFFT
    itw_c = np.cos(ait); itw_s = np.sin(ait)
    ai2 = 2 * np.pi * np.outer(k2, n2) / N2
    idft128_c = np.cos(ai2) / NFFT; idft128_s = np.sin(ai2) / NFFT
    dup = lambda m: np.ascontiguousarray(np.concatenate([m[:64], m[:64]], 0))
    bfc = lambda m: np.ascontiguousarray(m, np.float32).astype(BF)
    t = {}
    t['d128c'] = bfc(dup(dft128_c)); t['d128s'] = bfc(dup(dft128_s))
    t['twc'] = bfc(tw_c); t['tws'] = bfc(tw_s)
    t['d64c'] = bfc(np.concatenate([dft64_c] * 2, 0))
    t['d64s'] = bfc(np.concatenate([dft64_s] * 2, 0))
    t['d64sn'] = bfc(np.concatenate([-dft64_s] * 2, 0))
    t['i64c'] = bfc(np.concatenate([idft64_c] * 2, 0))
    t['i64s'] = bfc(np.concatenate([idft64_s] * 2, 0))
    t['i64sn'] = bfc(np.concatenate([-idft64_s] * 2, 0))
    t['itwc'] = bfc(np.concatenate([itw_c] * 2, 0))
    t['itws'] = bfc(np.concatenate([itw_s] * 2, 0))
    t['i128c_full'] = bfc(idft128_c[:, :64])
    t['i128sn_full'] = bfc(-idft128_s[:, :64])
    t['i128c_h0'] = bfc(idft128_c[:, 0:32]); t['i128sn_h0'] = bfc(-idft128_s[:, 0:32])
    t['i128c_h1'] = bfc(idft128_c[:, 32:64]); t['i128sn_h1'] = bfc(-idft128_s[:, 32:64])
    t['ident_bf'] = bfc(np.eye(128))
    t['ident_f32'] = np.ascontiguousarray(np.eye(128), np.float32)
    return t


ENG_SEM = {'pe': 'pe', 'dve': 'dve', 'act': 'act', 'pool': 'pool', 'sp': None}


class Sched:
    def __init__(self):
        self.q = {'pe': [], 'dve': [], 'act': [], 'pool': [], 'sp': []}
        self.cnt = {s: 0 for s in ['pe', 'dve', 'act', 'pool', 'sp16', 'gd16']}
        self.lastw = {e: {} for e in self.q}
        self.pending = {}

    def _waits(self, eng, waits):
        best = {}
        for t in waits:
            if t is None:
                continue
            s, v = t
            if v > best.get(s, -1):
                best[s] = v
        out = []
        for s, v in best.items():
            if self.lastw[eng].get(s, -1) >= v:
                continue
            self.lastw[eng][s] = v
            out.append((s, v))
        return out

    def barrier(self):
        toks = []
        for s, v in self.cnt.items():
            if v > 0:
                toks.append((s, v))
        self.pending = {e: list(toks) for e in self.q}

    def op(self, eng, fn, waits=(), inc=True):
        w = self._waits(eng, list(waits) + self.pending.pop(eng, []))
        sem = ENG_SEM[eng]
        tok = None
        if inc and sem:
            self.cnt[sem] += 1
            tok = (sem, self.cnt[sem])
        self.q[eng].append(('op', w, fn, tok))
        return tok

    def dma(self, eng, out_ap, in_ap, waits=()):
        w = self._waits(eng, list(waits) + self.pending.pop(eng, []))
        sem = 'sp16' if eng == 'sp' else 'gd16'
        self.cnt[sem] += 16
        tok = (sem, self.cnt[sem])
        self.q[eng].append(('dma', w, (out_ap, in_ap), tok))
        return tok


def _maxper(toks):
    best = {}
    for t in toks:
        if t is None:
            continue
        s, v = t
        if v > best.get(s, -1):
            best[s] = v
    return [(s, v) for s, v in best.items()]


def build_nc(debug=False):
    nc = bass.Bass("TRN2", target_bir_lowering=False)
    T = _tables()
    S = Sched()
    stack = contextlib.ExitStack()

    def dram_in(name, shape, dt=F32):
        return nc.dram_tensor(name, list(shape), dt, kind="ExternalInput").ap()

    u_b = dram_in("u_b", [T_FULL, DM])
    u_own = dram_in("u_own", [T_OWN, DM])
    WqT = dram_in("WqT", [DM, DM]); WkT = dram_in("WkT", [DM, DM])
    WvT = dram_in("WvT", [DM, DM]); WoT = dram_in("WoT", [DM, DM])
    kker = dram_in("kker", [DM, 2048]); sker = dram_in("sker", [DM, 2048])
    Wr = dram_in("Wr", [DM, 8])
    W1 = dram_in("W1", [8, DM, 4096]); W2 = dram_in("W2", [8, 4096, DM])
    b2d = dram_in("b2d", [8, DM])
    bq_c = dram_in("bq_c", [128, 8]); bk_c = dram_in("bk_c", [128, 8]); bv_c = dram_in("bv_c", [128, 8])
    br_c = dram_in("br_c", [8, 1]); bo_rep = dram_in("bo_rep", [128, DM])
    b1cd = dram_in("b1cd", [128, 256])
    kd_pr = dram_in("kd_pr", [128, 512], BF16)
    d_pr = dram_in("d_pr", [128, 512], BF16)
    tabs = {k: dram_in("tab_" + k, list(v.shape), BF16 if v.dtype == BF else F32)
            for k, v in T.items()}
    i128c_own = dram_in("i128c_own", [128, 32], BF16)
    i128sn_own = dram_in("i128sn_own", [128, 32], BF16)

    scrkind = "ExternalOutput" if debug else "Internal"
    kscr = nc.dram_tensor("kscr", [DM, T_FULL], F32, kind=scrkind).ap()
    vscr = nc.dram_tensor("vscr", [DM, T_FULL], F32, kind=scrkind).ap()
    if debug:
        dbg_xn = nc.dram_tensor("dbg_xn", [8, 128, T_FULL], BF16, kind="ExternalOutput").ap()
        dbg_ycm = nc.dram_tensor("dbg_ycm", [8, 128, T_OWN], BF16, kind="ExternalOutput").ap()
        dbg_lg = nc.dram_tensor("dbg_lg", [8, 1024], F32, kind="ExternalOutput").ap()
        dbg_gate = nc.dram_tensor("dbg_gate", [128, 64], F32, kind="ExternalOutput").ap()
        dbg_yacc = nc.dram_tensor("dbg_yacc", [8, 128, DM], F32, kind="ExternalOutput").ap()
        dbg_xno = nc.dram_tensor("dbg_xno", [8, 128, 1024], BF16, kind="ExternalOutput").ap()
        dbg_kv = nc.dram_tensor("dbg_kv", [128, 2048], BF16, kind="ExternalOutput").ap()
        dbg_kf = nc.dram_tensor("dbg_kf", [128, 2048], BF16, kind="ExternalOutput").ap()
        dbg_sR = nc.dram_tensor("dbg_sR", [128, SPEC], BF16, kind="ExternalOutput").ap()
        dbg_prb = nc.dram_tensor("dbg_prb", [128, 8], F32, kind="ExternalOutput").ap()
        dbg_Sb = nc.dram_tensor("dbg_Sb", [128, 8], F32, kind="ExternalOutput").ap()
        dbg_kR = nc.dram_tensor("dbg_kR", [128, SPEC], BF16, kind="ExternalOutput").ap()
    out_own = nc.dram_tensor("out_own", [T_OWN, DM], F32, kind="ExternalOutput").ap()

    stack_ab = contextlib.ExitStack()

    def sb(name, shape, dt):
        return stack.enter_context(nc.sbuf_tensor(name, list(shape), dt))

    def sb_ab(name, shape, dt):
        return stack_ab.enter_context(nc.sbuf_tensor(name, list(shape), dt))

    tab_s = {k: sb("t_" + k, v.shape, BF16 if v.dtype == BF else F32) for k, v in T.items()}
    i128c_own_s = sb("i128co", [128, 32], BF16)
    i128sn_own_s = sb("i128so", [128, 32], BF16)
    bq_s = sb("bq_s", [128, 8], F32); bk_s = sb("bk_s", [128, 8], F32); bv_s = sb("bv_s", [128, 8], F32)
    kd_s = sb("kd_s", [128, 512], BF16); dd_s = sb("dd_s", [128, 512], BF16)
    ycm = [sb(f"ycm{i}", [128, T_OWN], BF16) for i in range(8)]
    tmp1 = sb("tmp1", [128, 1024], F32); tmp2 = sb("tmp2", [128, 1024], F32)
    ut = [sb(f"ut{i}", [128, DM], F32) for i in range(2)]
    usq = sb("usq", [128, DM], F32)
    st_s = sb("st_s", [128, 8], F32)
    xnt = sb("xnt", [128, DM], BF16)
    xn = [sb_ab(f"xn{i}", [128, T_FULL], BF16) for i in range(8)]
    cm_slab = sb_ab("cm_slab", [128, 1024], F32)
    wk_s = sb_ab("wk_s", [128, 8 * 128], BF16)
    wv_s = sb_ab("wv_s", [128, 8 * 128], BF16)
    kfft = sb_ab("kfft", [128, 2048], BF16)
    vfft = sb_ab("vfft", [128, 2048], BF16)
    kvt = sb_ab("kvt", [128, 2048], BF16)
    kkfft = sb_ab("kkfft", [128, 2048], BF16)
    Ar = sb_ab("Ar", [128, SPEC], BF16); Ai = sb_ab("Ai", [128, SPEC], BF16)
    Brt = sb_ab("Brt", [128, SPEC], BF16); Bit = sb_ab("Bit", [128, SPEC], BF16)
    Kr = sb_ab("Kr", [128, SPEC], BF16); Ki = sb_ab("Ki", [128, SPEC], BF16)
    y2s = sb_ab("y2s", [32, 4096], BF16)

    psP = [stack.enter_context(nc.psum_tensor(f"psP{i}", [128, 1024], F32)) for i in range(2)]
    psK = stack.enter_context(nc.psum_tensor("psK", [128, 1024], F32))
    psT = [stack.enter_context(nc.psum_tensor(f"psTa{i}", [128, 512], BF16)) for i in range(2)]
    psP = psP + [psK]
    psS = psK

    sems = {s: stack.enter_context(nc.semaphore(s))
            for s in ['pe', 'dve', 'act', 'pool', 'sp16', 'gd16']}

    ps_state = {id(p): {'w': None, 'r': []} for p in psP + [psS] + psT}
    sl_state = {}
    for x in [Ar, Ai, Brt, Bit, Kr, Ki, kfft, vfft, kvt, kkfft, y2s]:
        sl_state[id(x)] = {'w': [], 'r': []}

    def mm_group(ps, mms, waits):
        st = ps_state[id(ps)]
        w = list(waits) + _maxper(st['r']) + ([st['w']] if st['w'] else [])
        tok = None
        first = True
        for i, m in enumerate(mms):
            tok = S.op('pe', m, waits=(w if first else ()), inc=(i == len(mms) - 1))
            first = False
        st['w'] = tok
        st['r'] = []
        return tok

    def ps_read(ps, eng, fn, waits=()):
        st = ps_state[id(ps)]
        w = list(waits) + ([st['w']] if st['w'] else [])
        tok = S.op(eng, fn, waits=w)
        st['r'].append(tok)
        return tok

    def pe_transpose(tp, tfn, waits=()):
        st = ps_state[id(tp)]
        w = list(waits) + _maxper(st['r']) + ([st['w']] if st['w'] else [])
        tok = S.op('pe', tfn, waits=w)
        st['w'] = tok
        st['r'] = []
        return tok

    def slot_w(x, tok):
        sl_state[id(x)]['w'].append(tok)

    def slot_begin(x):
        st = sl_state[id(x)]
        w = _maxper(st['r'] + st['w'])
        st['r'] = []
        st['w'] = []
        return w

    def slot_r(x, tok):
        sl_state[id(x)]['r'].append(tok)

    def slot_wtok(x):
        return _maxper(sl_state[id(x)]['w'])

    for k in tab_s:
        tl = S.dma('sp', tab_s[k][:], tabs[k])
    tl = S.dma('sp', i128c_own_s[:], i128c_own)
    tl = S.dma('sp', i128sn_own_s[:], i128sn_own)
    tl = S.dma('sp', bq_s[:], bq_c)
    tl = S.dma('sp', bk_s[:], bk_c)
    tl = S.dma('sp', bv_s[:], bv_c)
    tl = S.dma('sp', kd_s[:], kd_pr)
    TABTOK = S.dma('sp', dd_s[:], d_pr)

    IDB = tab_s['ident_bf']
    IDF = tab_s['ident_f32']

    ln_guard = {'ut': [None, None], 'usq': None, 'sq': None, 'xnt': None}

    def do_ln(u_src, ntiles, xn_dst):
        last = None
        for i in range(ntiles):
            buf = ut[i % 2]
            g_ = ln_guard['ut'][i % 2]
            g_ = g_ if isinstance(g_, list) else [g_]
            tl_ = S.dma('sp', buf[:], u_src[i * 128:(i + 1) * 128, :],
                        waits=[TABTOK] + g_)
            a = S.op('dve', (lambda b: lambda h: h.reduce_sum(st_s[:, 0:1], b[:], axis=mybir.AxisListType.X))(buf), waits=[tl_])
            b_ = S.op('act', (lambda b: lambda h: h.square(usq[:], b[:]))(buf), waits=[tl_, ln_guard['usq']])
            c = S.op('dve', lambda h: h.reduce_sum(st_s[:, 1:2], usq[:], axis=mybir.AxisListType.X), waits=[b_])
            d = S.op('dve', lambda h: h.tensor_scalar(st_s[:, 2:3], st_s[:, 0:1], 1.0 / DM, None, ALU.mult), waits=[a])
            e = S.op('dve', lambda h: h.tensor_mul(st_s[:, 3:4], st_s[:, 2:3], st_s[:, 2:3]), waits=[d])
            f = S.op('dve', lambda h: h.tensor_scalar(st_s[:, 4:5], st_s[:, 1:2], 1.0 / DM, None, ALU.mult), waits=[c])
            g = S.op('dve', lambda h: h.tensor_sub(st_s[:, 4:5], st_s[:, 4:5], st_s[:, 3:4]), waits=[e, f])
            g2 = S.op('dve', lambda h: h.tensor_scalar(st_s[:, 4:5], st_s[:, 4:5], 1e-5, None, ALU.add), waits=[g])
            r = S.op('dve', lambda h: h.reciprocal(st_s[:, 6:7], st_s[:, 4:5]), waits=[g2, ln_guard['sq']])
            sq = S.op('act', lambda h: h.sqrt(st_s[:, 7:8], st_s[:, 6:7]), waits=[r])
            ln_guard['sq'] = sq
            xt = S.op('dve', (lambda b: lambda h: h.tensor_scalar(xnt[:], b[:], st_s[:, 2:3], st_s[:, 7:8], ALU.subtract, ALU.mult))(buf),
                      waits=[sq, ln_guard['xnt']])
            ln_guard['ut'][i % 2] = [xt, b_]
            ln_guard['usq'] = c
            last_tt = None
            for hc in range(8):
                tp = psT[hc % 2]
                tt = pe_transpose(tp, (lambda hc, tp: lambda h: h.transpose(
                    tp[:, 0:128], xnt[:, hc * 128:(hc + 1) * 128], IDB[:]))(hc, tp), waits=[xt])
                ev = ps_read(tp, 'act', (lambda hc, i, tp: lambda h: h.copy(
                    xn_dst[hc][:, i * 128:(i + 1) * 128], tp[:, 0:128]))(hc, i, tp))
                last = ev
                last_tt = tt
            ln_guard['xnt'] = last_tt
        return last

    xn_done = do_ln(u_b, 32, xn)

    if debug:
        for hc in range(8):
            S.dma('sp', dbg_xn[hc], xn[hc][:], waits=[xn_done])

    cm_guard = {'free': None, 'wk': None, 'wv': None}
    bounce_k = [None] * 8
    bounce_v = [None] * 8
    ycm_written = [None]

    kscr_v = kscr.rearrange("(K hf c) (m n) -> K hf m c n", K=NCHUNK, hf=2, n=N1)
    vscr_v = vscr.rearrange("(K hf c) (m n) -> K hf m c n", K=NCHUNK, hf=2, n=N1)
    kker_v = kker.rearrange("(K hf c) (m n) -> K hf m c n", K=NCHUNK, hf=2, n=N1)
    sker_v = sker.rearrange("(K hf c) (m n) -> K hf m c n", K=NCHUNK, hf=2, n=N1)

    def proj_cm(WT_d, wbuf, bias_col, scratch, pchunk, wfree):
        wv = WT_d.rearrange("(hc p) d -> p hc d", hc=8)[:, :, pchunk * 128:(pchunk + 1) * 128]
        wl = S.dma('pool', wbuf[:].rearrange("p (hc d) -> p hc d", hc=8), wv, waits=[wfree])
        bw = None
        for t5 in range(8):
            ps = psP[t5 % 2]
            mms = []
            for hc in range(8):
                mms.append((lambda hc, t5, ps: lambda h: h.matmul(
                    ps[:, 0:512], wbuf[:, hc * 128:(hc + 1) * 128],
                    xn[hc][:, t5 * 512:(t5 + 1) * 512],
                    start=(hc == 0), stop=(hc == 7)))(hc, t5, ps))
            mt = mm_group(ps, mms, waits=[wl, xn_done])
            ev = ps_read(ps, 'act', (lambda t5, ps: lambda h: h.activation(
                cm_slab[:, (t5 % 2) * 512:(t5 % 2) * 512 + 512], ps[:, 0:512], AF.Identity,
                bias=bias_col, scale=1.0))(t5, ps), waits=[cm_guard['free']])
            if t5 % 2 == 1:
                bw = S.dma('sp', scratch[pchunk * 128:(pchunk + 1) * 128,
                                         (t5 - 1) * 512:(t5 + 1) * 512], cm_slab[:], waits=[ev])
                cm_guard['free'] = bw
        return bw, None

    def s1tw(rhs_t, rhs_tok, nz, Bd_r, Bd_i, mode):
        wA = slot_begin(Bd_r) + slot_begin(Bd_i)
        tw_prev = [None]
        for hf in range(2):
            lhs_c = tab_s['d128c'][64 * hf:64 * hf + nz, :]
            lhs_s = tab_s['d128s'][64 * hf:64 * hf + nz, :]
            npieces = 2 if mode == 'P' else 4
            width = 1024 if mode == 'P' else 512
            for kc in range(npieces):
                col0 = 2048 * hf + kc * width
                if mode == 'P':
                    ps_re = psP[0]; ps_im = psP[1]
                    mre = []; mim = []
                    for half in range(2):
                        sl = slice(kc * 1024 + half * 512, kc * 1024 + half * 512 + 512)
                        ob = slice(half * 512, half * 512 + 512)
                        mre.append((lambda lhs_c, sl, ob, hf, nz: lambda h: h.matmul(
                            psP[0][:, ob], lhs_c, rhs_t[64 * hf:64 * hf + nz, sl],
                            start=True, stop=True))(lhs_c, sl, ob, hf, nz))
                        mim.append((lambda lhs_s, sl, ob, hf, nz: lambda h: h.matmul(
                            psP[1][:, ob], lhs_s, rhs_t[64 * hf:64 * hf + nz, sl],
                            start=True, stop=True))(lhs_s, sl, ob, hf, nz))
                    t_re = mm_group(ps_re, mre, waits=[rhs_tok] + wA)
                    t_im = mm_group(ps_im, mim, waits=[rhs_tok])
                    re_ap = psP[0][:]; im_ap = psP[1][:]
                    nch = 16
                else:
                    ps_re = psK; ps_im = psK
                    sl = slice(kc * 512, kc * 512 + 512)
                    mm2 = [(lambda lhs_c, sl, hf, nz: lambda h: h.matmul(
                        psK[:, 0:512], lhs_c, rhs_t[64 * hf:64 * hf + nz, sl],
                        start=True, stop=True))(lhs_c, sl, hf, nz),
                        (lambda lhs_s, sl, hf, nz: lambda h: h.matmul(
                            psK[:, 512:1024], lhs_s, rhs_t[64 * hf:64 * hf + nz, sl],
                            start=True, stop=True))(lhs_s, sl, hf, nz)]
                    t_re = mm_group(psK, mm2, waits=[rhs_tok] + wA)
                    t_im = t_re
                    re_ap = psK[:, 0:512]; im_ap = psK[:, 512:1024]
                    nch = 8
                slot_r(rhs_t, t_re); slot_r(rhs_t, t_im)
                wA = []
                pv_re = re_ap.rearrange("p (c n) -> p c n", n=N1)
                pv_im = im_ap.rearrange("p (c n) -> p c n", n=N1)
                twc_b = tab_s['twc'][:].unsqueeze(1).broadcast_to([128, nch, N1])
                tws_b = tab_s['tws'][:].unsqueeze(1).broadcast_to([128, nch, N1])
                w_ = width
                a1 = ps_read(ps_re, 'dve', (lambda pv_re, twc_b, w_: lambda h: h.tensor_tensor(
                    tmp1[:, 0:w_].rearrange("p (c n) -> p c n", n=N1), pv_re, twc_b, ALU.mult))(pv_re, twc_b, w_), waits=[t_re, tw_prev[0]])
                a2 = ps_read(ps_im, 'dve', (lambda pv_im, tws_b, w_: lambda h: h.tensor_tensor(
                    tmp2[:, 0:w_].rearrange("p (c n) -> p c n", n=N1), pv_im, tws_b, ALU.mult))(pv_im, tws_b, w_), waits=[t_im])
                a3 = S.op('dve', (lambda col0, w_: lambda h: h.tensor_sub(
                    Bd_r[:, col0:col0 + w_], tmp1[:, 0:w_], tmp2[:, 0:w_]))(col0, w_), waits=[a1, a2])
                slot_w(Bd_r, a3)
                a4 = ps_read(ps_re, 'dve', (lambda pv_re, tws_b, w_: lambda h: h.tensor_tensor(
                    tmp1[:, 0:w_].rearrange("p (c n) -> p c n", n=N1), pv_re, tws_b, ALU.mult))(pv_re, tws_b, w_), waits=[a3])
                a5 = ps_read(ps_im, 'dve', (lambda pv_im, twc_b, w_: lambda h: h.tensor_tensor(
                    tmp2[:, 0:w_].rearrange("p (c n) -> p c n", n=N1), pv_im, twc_b, ALU.mult))(pv_im, twc_b, w_), waits=[a3])
                a6 = S.op('dve', (lambda col0, w_: lambda h: h.tensor_add(
                    Bd_i[:, col0:col0 + w_], tmp1[:, 0:w_], tmp2[:, 0:w_]))(col0, w_), waits=[a4, a5])
                slot_w(Bd_i, a6)
                tw_prev[0] = a6

    def trans_batch(src_r, src_i):
        wB = slot_begin(Brt) + slot_begin(Bit)
        for plane in range(2):
            src = src_r if plane == 0 else src_i
            dst = Brt if plane == 0 else Bit
            for g in range(8):
                tp = psT[plane]
                tts = None
                first = True
                for j in range(4):
                    pr = g * 4 + j
                    tfn = (lambda src, pr, tp, j: lambda h: h.transpose(
                        tp[:, j * 128:(j + 1) * 128], src[:, pr * 128:(pr + 1) * 128], IDB[:]))(src, pr, tp, j)
                    if first:
                        tts = pe_transpose(tp, tfn, waits=slot_wtok(src) + wB)
                        wB = []
                        first = False
                    else:
                        st = ps_state[id(tp)]
                        tts = S.op('pe', tfn)
                        st['w'] = tts
                slot_r(src, tts)
                ev = ps_read(tp, 'act', (lambda dst, g, tp: lambda h: h.copy(
                    dst[:, g * 512:(g + 1) * 512], tp[:, 0:512]))(dst, g, tp))
                slot_w(dst, ev)

    def s2f(Rdst, Idst, fold_tab, mode, ck, tabc='d64c', tabs_='d64s', tabsn='d64sn'):
        wR = slot_begin(Rdst) + slot_begin(Idst)
        npieces = 4 if mode == 'P' else 8
        width = 1024 if mode == 'P' else 512
        for piece in range(npieces):
            col0 = piece * width
            if mode == 'P':
                ps_re = psP[0]; ps_im = psP[1]
                re_ap = psP[0][:]; im_ap = psP[1][:]
            else:
                ps_re = psK; ps_im = psK
                re_ap = psK[:, 0:512]; im_ap = psK[:, 512:1024]
            mre = []; mim = []
            for par in range(2):
                rb = slice(64 * par, 64 * par + 64)
                tp_pos = (64 * par, 64 * par)
                nsub = 2 if mode == 'P' else 1
                for half in range(nsub):
                    s2_ = slice(col0 + half * 512, col0 + half * 512 + 512)
                    if mode == 'P':
                        ob = slice(half * 512, half * 512 + 512)
                        mre += [
                            (lambda ob, s2_, tp_pos, rb: lambda h: h.matmul(
                                psP[0][tp_pos[1]:tp_pos[1] + 64, ob], tab_s[tabc][rb, :], Brt[rb, s2_],
                                start=True, stop=False, tile_position=tp_pos))(ob, s2_, tp_pos, rb),
                            (lambda ob, s2_, tp_pos, rb: lambda h: h.matmul(
                                psP[0][tp_pos[1]:tp_pos[1] + 64, ob], tab_s[tabsn][rb, :], Bit[rb, s2_],
                                start=False, stop=True, tile_position=tp_pos))(ob, s2_, tp_pos, rb)]
                        mim += [
                            (lambda ob, s2_, tp_pos, rb: lambda h: h.matmul(
                                psP[1][tp_pos[1]:tp_pos[1] + 64, ob], tab_s[tabs_][rb, :], Brt[rb, s2_],
                                start=True, stop=False, tile_position=tp_pos))(ob, s2_, tp_pos, rb),
                            (lambda ob, s2_, tp_pos, rb: lambda h: h.matmul(
                                psP[1][tp_pos[1]:tp_pos[1] + 64, ob], tab_s[tabc][rb, :], Bit[rb, s2_],
                                start=False, stop=True, tile_position=tp_pos))(ob, s2_, tp_pos, rb)]
                    else:
                        mre += [
                            (lambda s2_, tp_pos, rb: lambda h: h.matmul(
                                psK[tp_pos[1]:tp_pos[1] + 64, 0:512], tab_s[tabc][rb.start:rb.stop, :], Brt[rb, s2_],
                                start=True, stop=False, tile_position=tp_pos))(s2_, tp_pos, rb),
                            (lambda s2_, tp_pos, rb: lambda h: h.matmul(
                                psK[tp_pos[1]:tp_pos[1] + 64, 0:512], tab_s[tabsn][rb.start:rb.stop, :], Bit[rb, s2_],
                                start=False, stop=True, tile_position=tp_pos))(s2_, tp_pos, rb)]
                        mim += [
                            (lambda s2_, tp_pos, rb: lambda h: h.matmul(
                                psK[tp_pos[1]:tp_pos[1] + 64, 512:1024], tab_s[tabs_][rb.start:rb.stop, :], Brt[rb, s2_],
                                start=True, stop=False, tile_position=tp_pos))(s2_, tp_pos, rb),
                            (lambda s2_, tp_pos, rb: lambda h: h.matmul(
                                psK[tp_pos[1]:tp_pos[1] + 64, 512:1024], tab_s[tabc][rb.start:rb.stop, :], Bit[rb, s2_],
                                start=False, stop=True, tile_position=tp_pos))(s2_, tp_pos, rb)]
            wdep = slot_wtok(Brt) + slot_wtok(Bit) + wR
            wR = []
            if mode == 'P':
                t_re = mm_group(ps_re, mre, waits=wdep)
                t_im = mm_group(ps_im, mim, waits=wdep)
            else:
                t_re = mm_group(psK, mre + mim, waits=wdep)
                t_im = t_re
            slot_r(Brt, t_im); slot_r(Bit, t_im)
            e1 = ps_read(ps_re, 'act', (lambda col0, re_ap, width: lambda h: h.copy(
                Rdst[:, col0:col0 + width], re_ap))(col0, re_ap, width))
            e2 = ps_read(ps_im, 'act', (lambda col0, im_ap, width: lambda h: h.copy(
                Idst[:, col0:col0 + width], im_ap))(col0, im_ap, width))
            slot_w(Rdst, e1); slot_w(Idst, e2)
            if fold_tab is not None:
                kdv = fold_tab[:, ck * 32:ck * 32 + 32]
                kdb = kdv.unsqueeze(2).broadcast_to([128, 32, 128])
                dst3 = Rdst[:, col0:col0 + width].rearrange("p (a k) -> p a k", k=128)
                a0 = col0 // 128
                kdb3 = kdb[:, a0:a0 + width // 128, :]
                e3 = S.op('dve', (lambda dst3, kdb3: lambda h: h.tensor_tensor(
                    dst3, dst3, kdb3, ALU.add))(dst3, kdb3), waits=[e1])
                slot_w(Rdst, e3)

    def inv_s1tw():
        wA = slot_begin(Ar) + slot_begin(Ai)
        pw = slot_wtok(Brt) + slot_wtok(Bit)
        q_prev = [None]
        for piece in range(4):
            ps_re = psP[0]; ps_im = psP[1]
            mre = []; mim = []
            for par in range(2):
                rb = slice(64 * par, 64 * par + 64)
                tp_pos = (64 * par, 64 * par)
                for half in range(2):
                    s2_ = slice(piece * 1024 + half * 512, piece * 1024 + half * 512 + 512)
                    ob = slice(half * 512, half * 512 + 512)
                    mre.append((lambda rb, s2_, ob, tp_pos: lambda h: h.matmul(
                        psP[0][tp_pos[1]:tp_pos[1] + 64, ob], tab_s['i64c'][rb, :], Brt[rb, s2_],
                        start=True, stop=False, tile_position=tp_pos))(rb, s2_, ob, tp_pos))
                    mre.append((lambda rb, s2_, ob, tp_pos: lambda h: h.matmul(
                        psP[0][tp_pos[1]:tp_pos[1] + 64, ob], tab_s['i64sn'][rb, :], Bit[rb, s2_],
                        start=False, stop=True, tile_position=tp_pos))(rb, s2_, ob, tp_pos))
                    mim.append((lambda rb, s2_, ob, tp_pos: lambda h: h.matmul(
                        psP[1][tp_pos[1]:tp_pos[1] + 64, ob], tab_s['i64s'][rb, :], Brt[rb, s2_],
                        start=True, stop=False, tile_position=tp_pos))(rb, s2_, ob, tp_pos))
                    mim.append((lambda rb, s2_, ob, tp_pos: lambda h: h.matmul(
                        psP[1][tp_pos[1]:tp_pos[1] + 64, ob], tab_s['i64c'][rb, :], Bit[rb, s2_],
                        start=False, stop=True, tile_position=tp_pos))(rb, s2_, ob, tp_pos))
            wdep = pw + wA
            wA = []; pw = []
            t_re = mm_group(ps_re, mre, waits=wdep)
            t_im = mm_group(ps_im, mim, waits=wdep)
            slot_r(Brt, t_im); slot_r(Bit, t_im)
            col0 = piece * 1024
            pv_re = psP[0][:].rearrange("p (a k) -> p a k", k=128)
            pv_im = psP[1][:].rearrange("p (a k) -> p a k", k=128)
            itc_b = tab_s['itwc'][:].unsqueeze(1).broadcast_to([128, 8, 128])
            its_b = tab_s['itws'][:].unsqueeze(1).broadcast_to([128, 8, 128])
            q1 = ps_read(ps_re, 'dve', (lambda pv_re, itc_b: lambda h: h.tensor_tensor(
                tmp1[:].rearrange("p (a k) -> p a k", k=128), pv_re, itc_b, ALU.mult))(pv_re, itc_b), waits=[t_re, q_prev[0]])
            q2 = ps_read(ps_im, 'dve', (lambda pv_im, its_b: lambda h: h.tensor_tensor(
                tmp2[:].rearrange("p (a k) -> p a k", k=128), pv_im, its_b, ALU.mult))(pv_im, its_b), waits=[t_im])
            q3 = S.op('dve', (lambda col0: lambda h: h.tensor_sub(
                Ar[:, col0:col0 + 1024], tmp1[:], tmp2[:]))(col0), waits=[q1, q2])
            slot_w(Ar, q3)
            q4 = ps_read(ps_re, 'dve', (lambda pv_re, its_b: lambda h: h.tensor_tensor(
                tmp1[:].rearrange("p (a k) -> p a k", k=128), pv_re, its_b, ALU.mult))(pv_re, its_b), waits=[q3])
            q5 = ps_read(ps_im, 'dve', (lambda pv_im, itc_b: lambda h: h.tensor_tensor(
                tmp2[:].rearrange("p (a k) -> p a k", k=128), pv_im, itc_b, ALU.mult))(pv_im, itc_b), waits=[q3])
            q6 = S.op('dve', (lambda col0: lambda h: h.tensor_add(
                Ai[:, col0:col0 + 1024], tmp1[:], tmp2[:]))(col0), waits=[q4, q5])
            slot_w(Ai, q6)
            q_prev[0] = q6

    def inv_s2(out_kind):
        if out_kind == 'kv':
            wkv = slot_begin(kvt)
            for pg in range(2):
                po = 64 * pg
                ps = psP[pg]
                for halfc in range(2):
                    lc = halfc * 1024
                    mm2 = []
                    for sub in range(2):
                        s3 = slice(pg * 2048 + halfc * 1024 + sub * 512,
                                   pg * 2048 + halfc * 1024 + sub * 512 + 512)
                        ob = slice(sub * 512, sub * 512 + 512)
                        mm2.append((lambda s3, ob, po, ps: lambda h: h.matmul(
                            ps[po:po + 64, ob], tab_s['i128c_full'][:], Brt[:, s3],
                            start=True, stop=False, tile_position=(0, po)))(s3, ob, po, ps))
                        mm2.append((lambda s3, ob, po, ps: lambda h: h.matmul(
                            ps[po:po + 64, ob], tab_s['i128sn_full'][:], Bit[:, s3],
                            start=False, stop=True, tile_position=(0, po)))(s3, ob, po, ps))
                    wdep = slot_wtok(Brt) + slot_wtok(Bit) + wkv + slot_wtok(vfft)
                    wkv = []
                    tmm = mm_group(ps, mm2, waits=wdep)
                    slot_r(Brt, tmm); slot_r(Bit, tmm)
                    ev = ps_read(ps, 'dve', (lambda ps, po, lc: lambda h: h.tensor_mul(
                        kvt[po:po + 64, lc:lc + 1024], ps[po:po + 64, 0:1024],
                        vfft[po:po + 64, lc:lc + 1024]))(ps, po, lc))
                    slot_r(vfft, ev)
                    slot_w(kvt, ev)
            return
        wy = slot_begin(y2s)
        for piece in range(4):
            sl1024 = slice(piece * 1024, piece * 1024 + 1024)
            ps = psP[piece % 2]
            mm2 = []
            for sub in range(2):
                s3 = slice(piece * 1024 + sub * 512, piece * 1024 + sub * 512 + 512)
                ob = slice(sub * 512, sub * 512 + 512)
                mm2.append((lambda s3, ob, ps: lambda h: h.matmul(
                    ps[0:32, ob], i128c_own_s[:], Brt[:, s3], start=True, stop=False))(s3, ob, ps))
                mm2.append((lambda s3, ob, ps: lambda h: h.matmul(
                    ps[0:32, ob], i128sn_own_s[:], Bit[:, s3], start=False, stop=True))(s3, ob, ps))
            wdep = slot_wtok(Brt) + slot_wtok(Bit) + wy
            wy = []
            tmm = mm_group(ps, mm2, waits=wdep)
            slot_r(Brt, tmm); slot_r(Bit, tmm)
            ev = ps_read(ps, 'dve', (lambda ps, sl1024: lambda h: h.tensor_copy(
                y2s[:, sl1024], ps[0:32, 0:1024]))(ps, sl1024))
            slot_w(y2s, ev)

    def product():
        wB = slot_begin(Brt) + slot_begin(Bit)
        deps = slot_wtok(Ar) + slot_wtok(Ai) + slot_wtok(Kr) + slot_wtok(Ki)
        pprev = None
        for piece in range(4):
            sl = slice(piece * 1024, piece * 1024 + 1024)
            p1 = S.op('dve', (lambda sl: lambda h: h.tensor_mul(tmp1[:], Ar[:, sl], Kr[:, sl]))(sl),
                      waits=deps + wB + [pprev])
            deps = []; wB = []
            p2 = S.op('dve', (lambda sl: lambda h: h.tensor_mul(tmp2[:], Ai[:, sl], Ki[:, sl]))(sl))
            p3 = S.op('dve', (lambda sl: lambda h: h.tensor_sub(Brt[:, sl], tmp1[:], tmp2[:]))(sl),
                      waits=[p1, p2])
            p4 = S.op('dve', (lambda sl: lambda h: h.tensor_mul(tmp1[:], Ar[:, sl], Ki[:, sl]))(sl),
                      waits=[p3])
            p5 = S.op('dve', (lambda sl: lambda h: h.tensor_mul(tmp2[:], Ai[:, sl], Kr[:, sl]))(sl),
                      waits=[p3])
            p6 = S.op('dve', (lambda sl: lambda h: h.tensor_add(Bit[:, sl], tmp1[:], tmp2[:]))(sl),
                      waits=[p4, p5])
            for x in (Ar, Ai, Kr, Ki):
                slot_r(x, p6)
            slot_w(Brt, p3); slot_w(Bit, p6)
            pprev = p6

    def conv_pass(ck, sig_t, sig_tok, ker_view, fold_tab, out_kind):
        wkk = slot_begin(kkfft)
        kl = None
        for hf in range(2):
            kl = S.dma('pool', kkfft[64 * hf:64 * hf + 32, :].rearrange("m (c n) -> m c n", n=N1),
                       ker_view[ck, hf], waits=wkk)
            wkk = []
        slot_w(kkfft, kl)
        s1tw(kkfft, kl, 32, Kr, Ki, 'K')
        s1tw(sig_t, sig_tok, 64, Ar, Ai, 'P')
        trans_batch(Kr, Ki)
        s2f(Kr, Ki, fold_tab, 'K', ck)
        trans_batch(Ar, Ai)
        s2f(Ar, Ai, None, 'P', ck)
        product()
        inv_s1tw()
        trans_batch(Ar, Ai)
        inv_s2(out_kind)

    for ck in range(NCHUNK):
        pchunk = ck // 2
        cchunk = ck // 2
        if ck % 2 == 0:
            bounce_k[pchunk], _ = proj_cm(WkT, wk_s, bk_s[:, pchunk:pchunk + 1], kscr, pchunk, cm_guard['wk'])
            cm_guard['wk'] = bounce_k[pchunk]
            bounce_v[pchunk], _ = proj_cm(WvT, wv_s, bv_s[:, pchunk:pchunk + 1], vscr, pchunk, cm_guard['wv'])
            cm_guard['wv'] = bounce_v[pchunk]
        wk0 = slot_begin(kfft)
        for hf in range(2):
            krt = S.dma('pool', kfft[64 * hf:64 * hf + 64, :].rearrange("m (c n) -> m c n", n=N1),
                        kscr_v[ck, hf], waits=[bounce_k[pchunk]] + wk0)
            wk0 = []
        slot_w(kfft, krt)
        wv0 = slot_begin(vfft)
        for hf in range(2):
            vrt = S.dma('pool', vfft[64 * hf:64 * hf + 64, :].rearrange("m (c n) -> m c n", n=N1),
                        vscr_v[ck, hf], waits=[bounce_v[pchunk]] + wv0)
            wv0 = []
        slot_w(vfft, vrt)
        conv_pass(ck, kfft, krt, kker_v, kd_s, 'kv')
        if debug and ck == 0:
            slot_r(kfft, S.dma('sp', dbg_kf, kfft[:], waits=slot_wtok(kfft)))
            slot_r(kvt, S.dma('sp', dbg_kv, kvt[:], waits=slot_wtok(kvt)))
            slot_r(Ar, S.dma('sp', dbg_sR, Ar[:], waits=slot_wtok(Ar)))
            slot_r(Kr, S.dma('sp', dbg_kR, Kr[:], waits=slot_wtok(Kr)))
        kv_tok = slot_wtok(kvt)
        conv_pass(ck, kvt, kv_tok[0] if kv_tok else None, sker_v, dd_s, 'y')
        ytoks = slot_wtok(y2s)
        cb = (ck * CH) % 128
        y3 = y2s[:].rearrange("p (a b n) -> p a b n", a=32, b=2)
        for n1g in range(16):
            tp = psT[n1g % 2]
            tts = None
            for j in range(4):
                n1v = n1g * 4 + j
                tfn = (lambda n1v, j, tp, cb: lambda h: h.transpose(
                    tp[cb:cb + 64, j * 32:(j + 1) * 32],
                    y3[:, :, :, n1v].rearrange("p a b -> p (a b)"),
                    IDB[0:32, 0:32], tile_position=(0, cb)))(n1v, j, tp, cb)
                if j == 0:
                    tts = pe_transpose(tp, tfn, waits=ytoks)
                else:
                    st = ps_state[id(tp)]
                    tts = S.op('pe', tfn)
                    st['w'] = tts
            slot_r(y2s, tts)
            dst = ycm[cchunk][cb:cb + 64, :].rearrange("p (m n) -> p m n", n=N1)
            def evfn(h, dst=dst, tp=tp, cb=cb, n1g=n1g):
                o = dst[:, :, n1g * 4:(n1g + 1) * 4].rearrange("p m n -> p n m")
                return h.tensor_copy(o, tp[cb:cb + 64, 0:128].rearrange("p (j m) -> p j m", j=4))
            ev = ps_read(tp, 'dve', evfn)
            ycm_written[0] = ev

    if debug:
        for hc in range(8):
            S.dma('sp', dbg_ycm[hc], ycm[hc][:], waits=[ycm_written[0]])

    S.barrier()
    stack_ab.close()
    xn_own = [sb(f"xno{i}", [128, 1024], BF16) for i in range(8)]
    yacc = [sb(f"yacc{i}", [128, DM], F32) for i in range(8)]
    w1s = [sb(f"w1s{i}", [128, 8 * 512], BF16) for i in range(2)]
    w2s = [sb(f"w2s{i}", [128, 4 * DM], BF16) for i in range(2)]
    hbuf = [sb(f"hbuf{i}", [128, 1024], BF16) for i in range(8)]
    wot_s = [sb(f"wot{i}", [128, DM], BF16) for i in range(8)]
    wq_s = sb("wq_s", [128, 8 * 128], BF16)
    wr_s = sb("wr_s", [128, 8 * 8], BF16)
    b2_s = sb("b2_s", [8, DM], BF16)
    b1c_s = sb("b1c_s", [128, 256], F32)
    bo_s = sb("bo_s", [128, DM], F32)
    br_s = sb("br_s", [8, 1], F32)
    lg_em = sb("lg_em", [8, 1024], F32)
    lg_tm = sb("lg_tm", [128, 8], F32)
    prb = sb("prb", [128, 8], F32)
    Sb = sb("Sb", [128, 8], F32)
    tmp8 = sb("tmp8", [128, 8], F32)
    gate_tm = sb("gate_tm", [128, 8 * 8], F32)
    gate_cm = sb("gate_cm", [8, 1024], BF16)
    outt = [sb(f"outt{i}", [128, DM], F32) for i in range(2)]
    uot = [sb(f"uot{i}", [128, DM], F32) for i in range(2)]

    tC = S.dma('pool', wr_s[:].rearrange("p (hc e) -> p hc e", hc=8),
               Wr.rearrange("(hc p) e -> p hc e", hc=8))
    tC = S.dma('pool', b2_s[:], b2d)
    for hc in range(8):
        tC = S.dma('pool', wot_s[hc][:], WoT.rearrange("(hc p) o -> hc p o", hc=8)[hc])
    tCp = tC
    tC2 = S.dma('sp', b1c_s[:], b1cd)
    tC2 = S.dma('sp', bo_s[:], bo_rep)
    tC2 = S.dma('sp', br_s[:], br_c)
    tCs = tC2

    STORE_TOKS = []
    mm2_prev = None

    for half in range(2):
        hb = half * 1024
        xo_done = do_ln(u_own[half * 1024:(half + 1) * 1024, :], 8, xn_own)
        lg_toks = []
        for pc in range(2):
            mmsp = []
            for hc in range(8):
                mmsp.append((lambda pc, hc: lambda h: h.matmul(
                    psS[0:8, 0:512], wr_s[:, hc * 8:(hc + 1) * 8],
                    xn_own[hc][:, pc * 512:(pc + 1) * 512],
                    start=(hc == 0), stop=(hc == 7)))(pc, hc))
            tmm = mm_group(psS, mmsp, waits=[xo_done, tCp, tCs])
            ev = ps_read(psS, 'act', (lambda pc: lambda h: h.activation(
                lg_em[:, pc * 512:(pc + 1) * 512], psS[0:8, 0:512], AF.Identity,
                bias=br_s[:], scale=1.0))(pc))
            lg_toks.append(ev)
        gate_done = []
        prev_topp = None
        for ti in range(8):
            ps = psP[2]
            tt = pe_transpose(ps, (lambda ti, ps: lambda h: h.transpose(
                ps[0:128, 0:8], lg_em[:, ti * 128:(ti + 1) * 128], IDF[0:8, 0:8]))(ti, ps),
                waits=lg_toks + [prev_topp])
            c1 = ps_read(ps, 'dve', (lambda ps: lambda h: h.tensor_copy(lg_tm[:], ps[0:128, 0:8]))(ps))
            m1 = S.op('dve', lambda h: h.reduce_max(st_s[:, 0:1], lg_tm[:], axis=mybir.AxisListType.X), waits=[c1])
            m2 = S.op('dve', lambda h: h.tensor_scalar(lg_tm[:], lg_tm[:], st_s[:, 0:1], None, ALU.subtract), waits=[m1])
            m3 = S.op('act', lambda h: h.activation(prb[:], lg_tm[:], AF.Exp), waits=[m2])
            m4 = S.op('dve', lambda h: h.reduce_sum(st_s[:, 1:2], prb[:], axis=mybir.AxisListType.X), waits=[m3])
            m5 = S.op('dve', lambda h: h.reciprocal(st_s[:, 2:3], st_s[:, 1:2]), waits=[m4])
            m6 = S.op('dve', lambda h: h.tensor_scalar(prb[:], prb[:], st_s[:, 2:3], None, ALU.mult), waits=[m5])
            last = m6
            for j in range(8):
                j1 = S.op('dve', (lambda j: lambda h: h.tensor_scalar(tmp8[:], prb[:], prb[:, j:j + 1], None, ALU.is_ge))(j), waits=[last])
                j2 = S.op('dve', lambda h: h.tensor_mul(tmp8[:], tmp8[:], prb[:]), waits=[j1])
                last = S.op('dve', (lambda j: lambda h: h.reduce_sum(Sb[:, j:j + 1], tmp8[:], axis=mybir.AxisListType.X))(j), waits=[j2])
            k1 = S.op('dve', lambda h: h.tensor_scalar(tmp8[:], Sb[:], 0.8, None, ALU.is_lt), waits=[last])
            k2 = S.op('dve', lambda h: h.tensor_tensor(Sb[:], Sb[:], prb[:], ALU.is_equal), waits=[k1])
            k3 = S.op('dve', lambda h: h.tensor_max(tmp8[:], tmp8[:], Sb[:]), waits=[k2])
            k4 = S.op('dve', (lambda ti: lambda h: h.tensor_mul(
                gate_tm[:, ti * 8:(ti + 1) * 8], tmp8[:], prb[:]))(ti), waits=[k3])
            prev_topp = k4
            tt2 = pe_transpose(ps, (lambda ti, ps: lambda h: h.transpose(
                ps[0:8, 0:128], gate_tm[:, ti * 8:(ti + 1) * 8], IDF[:]))(ti, ps), waits=[k4])
            ev2 = ps_read(ps, 'dve', (lambda ti, ps: lambda h: h.tensor_copy(
                gate_cm[:, ti * 128:(ti + 1) * 128], ps[0:8, 0:128]))(ti, ps))
            gate_done.append(ev2)
        yacc_last = [None] * 8
        yinit = []
        for ti in range(8):
            ps = psP[0]
            mmb = []
            for oh in range(2):
                mmb.append((lambda ti, oh, ps: lambda h: h.matmul(
                    ps[:, oh * 512:(oh + 1) * 512], gate_cm[:, ti * 128:(ti + 1) * 128],
                    b2_s[:, oh * 512:(oh + 1) * 512], start=True, stop=True))(ti, oh, ps))
            tmm = mm_group(ps, mmb, waits=[gate_done[ti], tCp, tCs])
            ev = ps_read(ps, 'dve', (lambda ti, ps: lambda h: h.tensor_add(
                yacc[ti][:], ps[:], bo_s[:]))(ti, ps), waits=[yacc_last[ti]])
            yacc_last[ti] = ev
            yinit.append(ev)
        g_done = []
        prev_q = None
        for dc in range(8):
            wl = S.dma('pool', wq_s[:].rearrange("p (hc d) -> p hc d", hc=8),
                       WqT.rearrange("(hc p) d -> p hc d", hc=8)[:, :, dc * 128:(dc + 1) * 128],
                       waits=[prev_q])
            for pc in range(2):
                ps = psP[pc % 2]
                mmq = []
                for hc in range(8):
                    mmq.append((lambda pc, hc, ps: lambda h: h.matmul(
                        ps[:, 0:512], wq_s[:, hc * 128:(hc + 1) * 128],
                        xn_own[hc][:, pc * 512:(pc + 1) * 512],
                        start=(hc == 0), stop=(hc == 7)))(pc, hc, ps))
                tmm = mm_group(ps, mmq, waits=[wl, xo_done, ycm_written[0]])
                ev = ps_read(ps, 'dve', (lambda dc, pc, ps, hb: lambda h: h.scalar_tensor_tensor(
                    ycm[dc][:, hb + pc * 512: hb + pc * 512 + 512], ps[:, 0:512],
                    bq_s[:, dc:dc + 1],
                    ycm[dc][:, hb + pc * 512: hb + pc * 512 + 512],
                    ALU.add, ALU.mult))(dc, pc, ps, hb))
                g_done.append(ev)
                prev_q = ev
        g_last = g_done[-1]
        w_use = {'w1': [None, None], 'w2': [None, None]}
        ld = {}

        def issue_w1(gfs, e, fs):
            pb = gfs % 2
            lt = S.dma('pool', w1s[pb][:].rearrange("p (hc f) -> p hc f", hc=8),
                       W1[e].rearrange("(hc p) f -> p hc f", hc=8)[:, :, fs * 512:(fs + 1) * 512],
                       waits=[w_use['w1'][pb]])
            ld[('w1', gfs)] = lt

        def issue_w2(gfs, e, fs):
            pb = gfs % 2
            lt = S.dma('pool', w2s[pb][:].rearrange("p (fc o) -> p fc o", fc=4),
                       W2[e].rearrange("(fc p) o -> p fc o", fc=32)[:, fs * 4:(fs + 1) * 4, :],
                       waits=[w_use['w2'][pb]])
            ld[('w2', gfs)] = lt

        issue_w1(0, 0, 0); issue_w2(0, 0, 0)
        issue_w1(1, 0, 1); issue_w2(1, 0, 1)
        h_done = [None] * 8
        for e in range(8):
            for pairi in range(4):
                for sub in range(2):
                    fs = pairi * 2 + sub
                    gfs = e * 8 + fs
                    pb = gfs % 2
                    lt1 = ld[('w1', gfs)]
                    mm1_last = None
                    for fc in range(4):
                        psm = psP[fc % 2]
                        mms = []
                        for hc in range(8):
                            for tq in range(2):
                                mms.append((lambda fc, hc, tq, psm, pb: lambda h: h.matmul(
                                    psm[:, tq * 512:(tq + 1) * 512],
                                    w1s[pb][:, hc * 512 + fc * 128: hc * 512 + fc * 128 + 128],
                                    xn_own[hc][:, tq * 512:(tq + 1) * 512],
                                    start=(hc == 0), stop=(hc == 7)))(fc, hc, tq, psm, pb))
                        tmm = mm_group(psm, mms, waits=[lt1, xo_done])
                        mm1_last = tmm
                        bcol = b1c_s[:, e * 32 + fs * 4 + fc: e * 32 + fs * 4 + fc + 1]
                        hidx = sub * 4 + fc
                        ev = ps_read(psm, 'dve', (lambda psm, bcol, hidx: lambda h: h.tensor_scalar(
                            hbuf[hidx][:], psm[:], bcol, 0.0, ALU.add, ALU.max))(psm, bcol, hidx))
                        h_done[hidx] = ev
                    w_use['w1'][pb] = mm1_last
                    if gfs + 2 < 64:
                        e2_, fs2 = divmod(gfs + 2, 8)
                        issue_w1(gfs + 2, e2_, fs2)
                mm2_tok = None
                for tt_ in range(8):
                    psm2 = psP[2]
                    mms2 = []
                    for fc8 in range(8):
                        sub2 = fc8 // 4; fc = fc8 % 4
                        pb2 = (e * 8 + pairi * 2 + sub2) % 2
                        for oh in range(2):
                            mms2.append((lambda fc8, fc, sub2, pb2, oh, tt_, psm2: lambda h: h.matmul(
                                psm2[:, oh * 512:(oh + 1) * 512],
                                hbuf[sub2 * 4 + fc][:, tt_ * 128:(tt_ + 1) * 128],
                                w2s[pb2][:, fc * 1024 + oh * 512: fc * 1024 + oh * 512 + 512],
                                start=(fc8 == 0), stop=(fc8 == 7)))(fc8, fc, sub2, pb2, oh, tt_, psm2))
                    wdep = _maxper(h_done) + [ld[('w2', e * 8 + pairi * 2)], ld[('w2', e * 8 + pairi * 2 + 1)]]
                    tmm2 = mm_group(psm2, mms2, waits=wdep)
                    mm2_tok = tmm2
                    gcol = gate_tm[:, tt_ * 8 + e: tt_ * 8 + e + 1]
                    ev2 = ps_read(psm2, 'dve', (lambda psm2, gcol, tt_: lambda h: h.scalar_tensor_tensor(
                        yacc[tt_][:], psm2[:], gcol, yacc[tt_][:], ALU.mult, ALU.add))(psm2, gcol, tt_),
                        waits=[yacc_last[tt_], gate_done[tt_]])
                    yacc_last[tt_] = ev2
                    mm2_prev = ev2
                for q_ in range(2):
                    gfs = e * 8 + pairi * 2 + q_
                    w_use['w2'][gfs % 2] = mm2_tok
                    if gfs + 2 < 64:
                        e2_, fs2 = divmod(gfs + 2, 8)
                        issue_w2(gfs + 2, e2_, fs2)
        if debug and half == 0:
            S.dma('sp', dbg_prb, prb[:], waits=[gate_done[-1]])
            S.dma('sp', dbg_Sb, Sb[:], waits=[gate_done[-1]])
            for hc in range(8):
                S.dma('sp', dbg_xno[hc], xn_own[hc][:], waits=[xo_done])
            S.dma('sp', dbg_lg, lg_em[:], waits=lg_toks)
            S.dma('sp', dbg_gate, gate_tm[:], waits=[gate_done[-1]])
            for ti in range(8):
                S.dma('sp', dbg_yacc[ti], yacc[ti][:], waits=[mm2_prev])
        for ti in range(8):
            ub = uot[ti % 2]
            ul = S.dma('sp', ub[:], u_own[half * 1024 + ti * 128: half * 1024 + (ti + 1) * 128, :],
                       waits=[STORE_TOKS[-2] if len(STORE_TOKS) >= 2 else None])
            psm = psP[ti % 2]
            mmo = []
            for chc in range(8):
                for oh in range(2):
                    mmo.append((lambda chc, oh, ti, psm, hb: lambda h: h.matmul(
                        psm[:, oh * 512:(oh + 1) * 512],
                        ycm[chc][:, hb + ti * 128: hb + ti * 128 + 128],
                        wot_s[chc][:, oh * 512:(oh + 1) * 512],
                        start=(chc == 0), stop=(chc == 7)))(chc, oh, ti, psm, hb))
            tmm = mm_group(psm, mmo, waits=[g_last, tCp, mm2_prev])
            ob = outt[ti % 2]
            e1 = ps_read(psm, 'dve', (lambda psm, ti, ob: lambda h: h.tensor_add(
                ob[:], psm[:], yacc[ti][:]))(psm, ti, ob),
                waits=[yacc_last[ti], STORE_TOKS[-2] if len(STORE_TOKS) >= 2 else None])
            e2 = S.op('dve', (lambda ob, ub: lambda h: h.tensor_add(ob[:], ob[:], ub[:]))(ob, ub),
                      waits=[e1, ul])
            stt = S.dma('sp', out_own[half * 1024 + ti * 128: half * 1024 + (ti + 1) * 128, :],
                        ob[:], waits=[e2])
            STORE_TOKS.append(stt)

    def make_runner(name):
        def run(h):
            for kind, waits, payload, tok in S.q[name]:
                for (s, v) in waits:
                    h.wait_ge(sems[s], v)
                if kind == 'op':
                    r = payload(h)
                    if tok is not None:
                        r.then_inc(sems[tok[0]], 1)
                else:
                    out_ap, in_ap = payload
                    h.dma_start(out_ap, in_ap).then_inc(sems[tok[0]], 16)
        return run

    final_sp = S.cnt['sp16']
    with nc.Block() as block:
        @block.tensor
        def _(h):
            make_runner('pe')(h)

        @block.vector
        def _(h):
            make_runner('dve')(h)

        @block.scalar
        def _(h):
            make_runner('act')(h)

        @block.gpsimd
        def _(h):
            make_runner('pool')(h)

        @block.sync
        def _(h):
            make_runner('sp')(h)
            h.wait_ge(sems['sp16'], final_sp)

    stack.close()
    return nc


def _host_arrays(ins):
    T = _tables()
    f32 = lambda x: np.ascontiguousarray(np.asarray(x), np.float32) if np.asarray(x).dtype != np.float32 else np.ascontiguousarray(x)
    shared = {}
    shared['WqT'] = np.ascontiguousarray(np.asarray(ins['L0_attn_Wq']).T)
    shared['WkT'] = np.ascontiguousarray(np.asarray(ins['L0_attn_Wk']).T)
    shared['WvT'] = np.ascontiguousarray(np.asarray(ins['L0_attn_Wv']).T)
    shared['WoT'] = np.ascontiguousarray(np.asarray(ins['L0_attn_Wo']).T)
    shared['kker'] = np.ascontiguousarray(ins['L0_attn_k_kernel'])
    shared['sker'] = np.ascontiguousarray(ins['L0_attn_ssm_kernel'])
    shared['Wr'] = np.ascontiguousarray(ins['L0_ffn_Wr'])
    shared['W1'] = np.ascontiguousarray(ins['L0_ffn_W1'])
    shared['W2'] = np.ascontiguousarray(ins['L0_ffn_W2'])
    shared['b2d'] = np.ascontiguousarray(ins['L0_ffn_b2'])
    for nm, key in [('bq_c', 'L0_attn_bq'), ('bk_c', 'L0_attn_bk'), ('bv_c', 'L0_attn_bv')]:
        shared[nm] = np.ascontiguousarray(np.asarray(ins[key]).reshape(8, 128).T)
    shared['br_c'] = np.ascontiguousarray(np.asarray(ins['L0_ffn_br']).reshape(8, 1))
    shared['bo_rep'] = np.ascontiguousarray(np.tile(np.asarray(ins['L0_attn_bo'])[None, :], (128, 1)))
    b1 = np.asarray(ins['L0_ffn_b1'], np.float32)
    b1c = np.zeros((128, 256), np.float32)
    for e in range(8):
        for fcg in range(32):
            b1c[:, e * 32 + fcg] = b1[e, fcg * 128:(fcg + 1) * 128]
    shared['b1cd'] = b1c
    kD = np.asarray(ins['L0_attn_k_D'], np.float32)
    Dv = np.asarray(ins['L0_attn_D'], np.float32)

    def mk_pairs(vec):
        a = np.zeros((128, 512), np.float32)
        for ck in range(16):
            for pr in range(32):
                a[0:64, ck * 32 + pr] = vec[ck * 64 + 2 * pr]
                a[64:128, ck * 32 + pr] = vec[ck * 64 + 2 * pr + 1]
        return a.astype(BF)
    shared['kd_pr'] = mk_pairs(kD)
    shared['d_pr'] = mk_pairs(Dv)
    for k, v in T.items():
        shared['tab_' + k] = v
    u = np.asarray(ins['L0_attn_u'], np.float32)
    per_core = []
    for c in range(8):
        b, h = c // 2, c % 2
        d = dict(shared)
        d['u_b'] = np.ascontiguousarray(u[b])
        d['u_own'] = np.ascontiguousarray(u[b, h * 2048:(h + 1) * 2048])
        d['i128c_own'] = T['i128c_h1'] if h else T['i128c_h0']
        d['i128sn_own'] = T['i128sn_h1'] if h else T['i128sn_h0']
        per_core.append(d)
    return per_core


def kernel(**ins):
    nc = build_nc()
    in_maps = _host_arrays(ins)
    res = run_bass_kernel_spmd(nc, in_maps, core_ids=list(range(8)))
    out = np.zeros((4, 4096, 1024), np.float32)
    for c in range(8):
        b, h = c // 2, c % 2
        out[b, h * 2048:(h + 1) * 2048] = res.results[c]["out_own"]
    return out




# version 04
# speedup vs baseline: 1.0133x; 1.0133x over previous
import numpy as np
import ml_dtypes
import contextlib
import concourse.bass as bass
import concourse.mybir as mybir
from concourse.bass_utils import run_bass_kernel_spmd

F32 = mybir.dt.float32
BF16 = mybir.dt.bfloat16
AF = mybir.ActivationFunctionType
ALU = mybir.AluOpType
BF = ml_dtypes.bfloat16

NFFT = 8192
N1 = 64
N2 = 128
NCHUNK = 16
CH = 64
DM = 1024
T_FULL = 4096
T_OWN = 2048
PAIRS = 32
SPEC = 4096


def _tables():
    n2 = np.arange(N2); k2 = np.arange(N2)
    n1 = np.arange(N1); k1 = np.arange(N1)
    a1 = -2 * np.pi * np.outer(n2, k2) / N2
    dft128_c = np.cos(a1); dft128_s = np.sin(a1)
    at = -2 * np.pi * np.outer(k2, n1) / NFFT
    tw_c = np.cos(at); tw_s = np.sin(at)
    a2 = -2 * np.pi * np.outer(n1, k1) / N1
    dft64_c = np.cos(a2); dft64_s = np.sin(a2)
    ai1 = 2 * np.pi * np.outer(k1, n1) / N1
    idft64_c = np.cos(ai1); idft64_s = np.sin(ai1)
    ait = 2 * np.pi * np.outer(n1, k2) / NFFT
    itw_c = np.cos(ait); itw_s = np.sin(ait)
    ai2 = 2 * np.pi * np.outer(k2, n2) / N2
    idft128_c = np.cos(ai2) / NFFT; idft128_s = np.sin(ai2) / NFFT
    dup = lambda m: np.ascontiguousarray(np.concatenate([m[:64], m[:64]], 0))
    bfc = lambda m: np.ascontiguousarray(m, np.float32).astype(BF)
    t = {}
    t['d128c'] = bfc(dup(dft128_c)); t['d128s'] = bfc(dup(dft128_s))
    t['twc'] = bfc(tw_c); t['tws'] = bfc(tw_s)
    t['d64c'] = bfc(np.concatenate([dft64_c] * 2, 0))
    t['d64s'] = bfc(np.concatenate([dft64_s] * 2, 0))
    t['d64sn'] = bfc(np.concatenate([-dft64_s] * 2, 0))
    t['i64c'] = bfc(np.concatenate([idft64_c] * 2, 0))
    t['i64s'] = bfc(np.concatenate([idft64_s] * 2, 0))
    t['i64sn'] = bfc(np.concatenate([-idft64_s] * 2, 0))
    t['itwc'] = bfc(np.concatenate([itw_c] * 2, 0))
    t['itws'] = bfc(np.concatenate([itw_s] * 2, 0))
    t['i128c_full'] = bfc(idft128_c[:, :64])
    t['i128sn_full'] = bfc(-idft128_s[:, :64])
    t['i128c_h0'] = bfc(idft128_c[:, 0:32]); t['i128sn_h0'] = bfc(-idft128_s[:, 0:32])
    t['i128c_h1'] = bfc(idft128_c[:, 32:64]); t['i128sn_h1'] = bfc(-idft128_s[:, 32:64])
    t['ident_bf'] = bfc(np.eye(128))
    t['ident_f32'] = np.ascontiguousarray(np.eye(128), np.float32)
    return t


ENG_SEM = {'pe': 'pe', 'dve': 'dve', 'act': 'act', 'pool': 'pool', 'sp': None}


class Sched:
    def __init__(self):
        self.q = {'pe': [], 'dve': [], 'act': [], 'pool': [], 'sp': []}
        self.cnt = {s: 0 for s in ['pe', 'dve', 'act', 'pool', 'sp16', 'gd16']}
        self.lastw = {e: {} for e in self.q}
        self.pending = {}

    def _waits(self, eng, waits):
        best = {}
        for t in waits:
            if t is None:
                continue
            s, v = t
            if v > best.get(s, -1):
                best[s] = v
        out = []
        for s, v in best.items():
            if self.lastw[eng].get(s, -1) >= v:
                continue
            self.lastw[eng][s] = v
            out.append((s, v))
        return out

    def barrier(self):
        toks = []
        for s, v in self.cnt.items():
            if v > 0:
                toks.append((s, v))
        self.pending = {e: list(toks) for e in self.q}

    def op(self, eng, fn, waits=(), inc=True):
        w = self._waits(eng, list(waits) + self.pending.pop(eng, []))
        sem = ENG_SEM[eng]
        tok = None
        if inc and sem:
            self.cnt[sem] += 1
            tok = (sem, self.cnt[sem])
        self.q[eng].append(('op', w, fn, tok))
        return tok

    def dma(self, eng, out_ap, in_ap, waits=()):
        w = self._waits(eng, list(waits) + self.pending.pop(eng, []))
        sem = 'sp16' if eng == 'sp' else 'gd16'
        self.cnt[sem] += 16
        tok = (sem, self.cnt[sem])
        self.q[eng].append(('dma', w, (out_ap, in_ap), tok))
        return tok


def _maxper(toks):
    best = {}
    for t in toks:
        if t is None:
            continue
        s, v = t
        if v > best.get(s, -1):
            best[s] = v
    return [(s, v) for s, v in best.items()]


def build_nc(debug=False):
    nc = bass.Bass("TRN2", target_bir_lowering=False)
    T = _tables()
    S = Sched()
    stack = contextlib.ExitStack()

    def dram_in(name, shape, dt=F32):
        return nc.dram_tensor(name, list(shape), dt, kind="ExternalInput").ap()

    u_b = dram_in("u_b", [T_FULL, DM])
    u_own = dram_in("u_own", [T_OWN, DM])
    WqT = dram_in("WqT", [DM, DM]); WkT = dram_in("WkT", [DM, DM])
    WvT = dram_in("WvT", [DM, DM]); WoT = dram_in("WoT", [DM, DM])
    kker = dram_in("kker", [DM, 2048]); sker = dram_in("sker", [DM, 2048])
    Wr = dram_in("Wr", [DM, 8])
    W1 = dram_in("W1", [8, DM, 4096]); W2 = dram_in("W2", [8, 4096, DM])
    b2d = dram_in("b2d", [8, DM])
    bq_c = dram_in("bq_c", [128, 8]); bk_c = dram_in("bk_c", [128, 8]); bv_c = dram_in("bv_c", [128, 8])
    br_c = dram_in("br_c", [8, 1]); bo_rep = dram_in("bo_rep", [128, DM])
    b1cd = dram_in("b1cd", [128, 256])
    kd_pr = dram_in("kd_pr", [128, 512], BF16)
    d_pr = dram_in("d_pr", [128, 512], BF16)
    tabs = {k: dram_in("tab_" + k, list(v.shape), BF16 if v.dtype == BF else F32)
            for k, v in T.items()}
    i128c_own = dram_in("i128c_own", [128, 32], BF16)
    i128sn_own = dram_in("i128sn_own", [128, 32], BF16)

    scrkind = "ExternalOutput" if debug else "Internal"
    kscr = nc.dram_tensor("kscr", [DM, T_FULL], F32, kind=scrkind).ap()
    vscr = nc.dram_tensor("vscr", [DM, T_FULL], F32, kind=scrkind).ap()
    if debug:
        dbg_xn = nc.dram_tensor("dbg_xn", [8, 128, T_FULL], BF16, kind="ExternalOutput").ap()
        dbg_ycm = nc.dram_tensor("dbg_ycm", [8, 128, T_OWN], BF16, kind="ExternalOutput").ap()
        dbg_lg = nc.dram_tensor("dbg_lg", [8, 1024], F32, kind="ExternalOutput").ap()
        dbg_gate = nc.dram_tensor("dbg_gate", [128, 64], F32, kind="ExternalOutput").ap()
        dbg_yacc = nc.dram_tensor("dbg_yacc", [8, 128, DM], F32, kind="ExternalOutput").ap()
        dbg_xno = nc.dram_tensor("dbg_xno", [8, 128, 1024], BF16, kind="ExternalOutput").ap()
        dbg_kv = nc.dram_tensor("dbg_kv", [128, 2048], BF16, kind="ExternalOutput").ap()
        dbg_kf = nc.dram_tensor("dbg_kf", [128, 2048], BF16, kind="ExternalOutput").ap()
        dbg_sR = nc.dram_tensor("dbg_sR", [128, SPEC], BF16, kind="ExternalOutput").ap()
        dbg_prb = nc.dram_tensor("dbg_prb", [128, 8], F32, kind="ExternalOutput").ap()
        dbg_Sb = nc.dram_tensor("dbg_Sb", [128, 8], F32, kind="ExternalOutput").ap()
        dbg_kR = nc.dram_tensor("dbg_kR", [128, SPEC], BF16, kind="ExternalOutput").ap()
    out_own = nc.dram_tensor("out_own", [T_OWN, DM], F32, kind="ExternalOutput").ap()

    stack_ab = contextlib.ExitStack()

    def sb(name, shape, dt):
        return stack.enter_context(nc.sbuf_tensor(name, list(shape), dt))

    def sb_ab(name, shape, dt):
        return stack_ab.enter_context(nc.sbuf_tensor(name, list(shape), dt))

    tab_s = {k: sb("t_" + k, v.shape, BF16 if v.dtype == BF else F32) for k, v in T.items()}
    i128c_own_s = sb("i128co", [128, 32], BF16)
    i128sn_own_s = sb("i128so", [128, 32], BF16)
    bq_s = sb("bq_s", [128, 8], F32); bk_s = sb("bk_s", [128, 8], F32); bv_s = sb("bv_s", [128, 8], F32)
    kd_s = sb("kd_s", [128, 512], BF16); dd_s = sb("dd_s", [128, 512], BF16)
    ycm = [sb(f"ycm{i}", [128, T_OWN], BF16) for i in range(8)]
    ut = [sb(f"ut{i}", [128, DM], F32) for i in range(2)]
    usq = sb("usq", [128, DM], F32)
    st_s = sb("st_s", [128, 8], F32)
    xnt = sb("xnt", [128, DM], BF16)
    xn = [sb_ab(f"xn{i}", [128, T_FULL], BF16) for i in range(8)]
    cm_slab = sb_ab("cm_slab", [128, 1024], F32)
    wk_s = sb_ab("wk_s", [128, 8 * 128], BF16)
    wv_s = sb_ab("wv_s", [128, 8 * 128], BF16)
    kfft = sb_ab("kfft", [128, 2048], BF16)
    vfft = sb_ab("vfft", [128, 2048], BF16)
    kvt = sb_ab("kvt", [128, 2048], BF16)
    kkfft = sb_ab("kkfft", [128, 2048], BF16)
    Ar = sb_ab("Ar", [128, SPEC], BF16); Ai = sb_ab("Ai", [128, SPEC], BF16)
    Brt = sb_ab("Brt", [128, SPEC], BF16); Bit = sb_ab("Bit", [128, SPEC], BF16)
    Kr = sb_ab("Kr", [128, SPEC], BF16); Ki = sb_ab("Ki", [128, SPEC], BF16)
    y2s = sb_ab("y2s", [32, 4096], BF16)
    ab_r = sb_ab("ab_r", [128, 1024], BF16); ab_i = sb_ab("ab_i", [128, 1024], BF16)
    tb1 = sb_ab("tb1", [128, 1024], BF16); tb2 = sb_ab("tb2", [128, 1024], BF16)
    tg1 = sb_ab("tg1", [128, 1024], BF16); tg2 = sb_ab("tg2", [128, 1024], BF16)

    psP = [stack.enter_context(nc.psum_tensor(f"psP{i}", [128, 1024], F32)) for i in range(2)]
    psK = stack.enter_context(nc.psum_tensor("psK", [128, 1024], F32))
    psT = [stack.enter_context(nc.psum_tensor(f"psTa{i}", [128, 512], BF16)) for i in range(2)]
    psP = psP + [psK]
    psS = psK

    sems = {s: stack.enter_context(nc.semaphore(s))
            for s in ['pe', 'dve', 'act', 'pool', 'sp16', 'gd16']}

    ps_state = {id(p): {'w': None, 'r': []} for p in psP + [psS] + psT}
    sl_state = {}
    for x in [Ar, Ai, Brt, Bit, Kr, Ki, kfft, vfft, kvt, kkfft, y2s]:
        sl_state[id(x)] = {'w': [], 'r': []}

    def mm_group(ps, mms, waits):
        st = ps_state[id(ps)]
        w = list(waits) + _maxper(st['r']) + ([st['w']] if st['w'] else [])
        tok = None
        first = True
        for i, m in enumerate(mms):
            tok = S.op('pe', m, waits=(w if first else ()), inc=(i == len(mms) - 1))
            first = False
        st['w'] = tok
        st['r'] = []
        return tok

    def ps_read(ps, eng, fn, waits=()):
        st = ps_state[id(ps)]
        w = list(waits) + ([st['w']] if st['w'] else [])
        tok = S.op(eng, fn, waits=w)
        st['r'].append(tok)
        return tok

    def pe_transpose(tp, tfn, waits=()):
        st = ps_state[id(tp)]
        w = list(waits) + _maxper(st['r']) + ([st['w']] if st['w'] else [])
        tok = S.op('pe', tfn, waits=w)
        st['w'] = tok
        st['r'] = []
        return tok

    def slot_w(x, tok):
        sl_state[id(x)]['w'].append(tok)

    def slot_begin(x):
        st = sl_state[id(x)]
        w = _maxper(st['r'] + st['w'])
        st['r'] = []
        st['w'] = []
        return w

    def slot_r(x, tok):
        sl_state[id(x)]['r'].append(tok)

    def slot_wtok(x):
        return _maxper(sl_state[id(x)]['w'])

    for k in tab_s:
        tl = S.dma('sp', tab_s[k][:], tabs[k])
    tl = S.dma('sp', i128c_own_s[:], i128c_own)
    tl = S.dma('sp', i128sn_own_s[:], i128sn_own)
    tl = S.dma('sp', bq_s[:], bq_c)
    tl = S.dma('sp', bk_s[:], bk_c)
    tl = S.dma('sp', bv_s[:], bv_c)
    tl = S.dma('sp', kd_s[:], kd_pr)
    TABTOK = S.dma('sp', dd_s[:], d_pr)

    IDB = tab_s['ident_bf']
    IDF = tab_s['ident_f32']

    ln_guard = {'ut': [None, None], 'usq': None, 'sq': None, 'xnt': None}

    def do_ln(u_src, ntiles, xn_dst):
        last = None
        for i in range(ntiles):
            buf = ut[i % 2]
            g_ = ln_guard['ut'][i % 2]
            g_ = g_ if isinstance(g_, list) else [g_]
            tl_ = S.dma('sp', buf[:], u_src[i * 128:(i + 1) * 128, :],
                        waits=[TABTOK] + g_)
            a = S.op('dve', (lambda b: lambda h: h.reduce_sum(st_s[:, 0:1], b[:], axis=mybir.AxisListType.X))(buf), waits=[tl_])
            b_ = S.op('act', (lambda b: lambda h: h.square(usq[:], b[:]))(buf), waits=[tl_, ln_guard['usq']])
            c = S.op('dve', lambda h: h.reduce_sum(st_s[:, 1:2], usq[:], axis=mybir.AxisListType.X), waits=[b_])
            d = S.op('dve', lambda h: h.tensor_scalar(st_s[:, 2:3], st_s[:, 0:1], 1.0 / DM, None, ALU.mult), waits=[a])
            e = S.op('dve', lambda h: h.tensor_mul(st_s[:, 3:4], st_s[:, 2:3], st_s[:, 2:3]), waits=[d])
            f = S.op('dve', lambda h: h.tensor_scalar(st_s[:, 4:5], st_s[:, 1:2], 1.0 / DM, None, ALU.mult), waits=[c])
            g = S.op('dve', lambda h: h.tensor_sub(st_s[:, 4:5], st_s[:, 4:5], st_s[:, 3:4]), waits=[e, f])
            g2 = S.op('dve', lambda h: h.tensor_scalar(st_s[:, 4:5], st_s[:, 4:5], 1e-5, None, ALU.add), waits=[g])
            r = S.op('dve', lambda h: h.reciprocal(st_s[:, 6:7], st_s[:, 4:5]), waits=[g2, ln_guard['sq']])
            sq = S.op('act', lambda h: h.sqrt(st_s[:, 7:8], st_s[:, 6:7]), waits=[r])
            ln_guard['sq'] = sq
            xt = S.op('dve', (lambda b: lambda h: h.tensor_scalar(xnt[:], b[:], st_s[:, 2:3], st_s[:, 7:8], ALU.subtract, ALU.mult))(buf),
                      waits=[sq, ln_guard['xnt']])
            ln_guard['ut'][i % 2] = [xt, b_]
            ln_guard['usq'] = c
            last_tt = None
            for hc in range(8):
                tp = psT[hc % 2]
                tt = pe_transpose(tp, (lambda hc, tp: lambda h: h.transpose(
                    tp[:, 0:128], xnt[:, hc * 128:(hc + 1) * 128], IDB[:]))(hc, tp), waits=[xt])
                ev = ps_read(tp, 'act', (lambda hc, i, tp: lambda h: h.copy(
                    xn_dst[hc][:, i * 128:(i + 1) * 128], tp[:, 0:128]))(hc, i, tp))
                last = ev
                last_tt = tt
            ln_guard['xnt'] = last_tt
        return last

    xn_done = do_ln(u_b, 32, xn)

    if debug:
        for hc in range(8):
            S.dma('sp', dbg_xn[hc], xn[hc][:], waits=[xn_done])

    cm_guard = {'free': None, 'wk': None, 'wv': None}
    bounce_k = [None] * 8
    bounce_v = [None] * 8
    ycm_written = [None]

    kscr_v = kscr.rearrange("(K hf c) (m n) -> K hf m c n", K=NCHUNK, hf=2, n=N1)
    vscr_v = vscr.rearrange("(K hf c) (m n) -> K hf m c n", K=NCHUNK, hf=2, n=N1)
    kker_v = kker.rearrange("(K hf c) (m n) -> K hf m c n", K=NCHUNK, hf=2, n=N1)
    sker_v = sker.rearrange("(K hf c) (m n) -> K hf m c n", K=NCHUNK, hf=2, n=N1)

    def proj_cm(WT_d, wbuf, bias_col, scratch, pchunk, wfree):
        wv = WT_d.rearrange("(hc p) d -> p hc d", hc=8)[:, :, pchunk * 128:(pchunk + 1) * 128]
        wl = S.dma('pool', wbuf[:].rearrange("p (hc d) -> p hc d", hc=8), wv, waits=[wfree])
        bw = None
        for t5 in range(8):
            ps = psP[t5 % 2]
            mms = []
            for hc in range(8):
                mms.append((lambda hc, t5, ps: lambda h: h.matmul(
                    ps[:, 0:512], wbuf[:, hc * 128:(hc + 1) * 128],
                    xn[hc][:, t5 * 512:(t5 + 1) * 512],
                    start=(hc == 0), stop=(hc == 7)))(hc, t5, ps))
            mt = mm_group(ps, mms, waits=[wl, xn_done])
            ev = ps_read(ps, 'act', (lambda t5, ps: lambda h: h.activation(
                cm_slab[:, (t5 % 2) * 512:(t5 % 2) * 512 + 512], ps[:, 0:512], AF.Identity,
                bias=bias_col, scale=1.0))(t5, ps), waits=[cm_guard['free']])
            if t5 % 2 == 1:
                bw = S.dma('sp', scratch[pchunk * 128:(pchunk + 1) * 128,
                                         (t5 - 1) * 512:(t5 + 1) * 512], cm_slab[:], waits=[ev])
                cm_guard['free'] = bw
        return bw, None

    ab_g = [None, None]

    def s1tw(rhs_t, rhs_tok, nz, Bd_r, Bd_i, mode):
        wA = slot_begin(Bd_r) + slot_begin(Bd_i)
        tw_prev = ab_g
        for hf in range(2):
            lhs_c = tab_s['d128c'][64 * hf:64 * hf + nz, :]
            lhs_s = tab_s['d128s'][64 * hf:64 * hf + nz, :]
            npieces = 2 if mode == 'P' else 4
            width = 1024 if mode == 'P' else 512
            for kc in range(npieces):
                col0 = 2048 * hf + kc * width
                if mode == 'P':
                    ps_re = psP[0]; ps_im = psP[1]
                    mre = []; mim = []
                    for half in range(2):
                        sl = slice(kc * 1024 + half * 512, kc * 1024 + half * 512 + 512)
                        ob = slice(half * 512, half * 512 + 512)
                        mre.append((lambda lhs_c, sl, ob, hf, nz: lambda h: h.matmul(
                            psP[0][:, ob], lhs_c, rhs_t[64 * hf:64 * hf + nz, sl],
                            start=True, stop=True))(lhs_c, sl, ob, hf, nz))
                        mim.append((lambda lhs_s, sl, ob, hf, nz: lambda h: h.matmul(
                            psP[1][:, ob], lhs_s, rhs_t[64 * hf:64 * hf + nz, sl],
                            start=True, stop=True))(lhs_s, sl, ob, hf, nz))
                    t_re = mm_group(ps_re, mre, waits=[rhs_tok] + wA)
                    t_im = mm_group(ps_im, mim, waits=[rhs_tok])
                    re_ap = psP[0][:]; im_ap = psP[1][:]
                    nch = 16
                else:
                    ps_re = psK; ps_im = psK
                    sl = slice(kc * 512, kc * 512 + 512)
                    mm2 = [(lambda lhs_c, sl, hf, nz: lambda h: h.matmul(
                        psK[:, 0:512], lhs_c, rhs_t[64 * hf:64 * hf + nz, sl],
                        start=True, stop=True))(lhs_c, sl, hf, nz),
                        (lambda lhs_s, sl, hf, nz: lambda h: h.matmul(
                            psK[:, 512:1024], lhs_s, rhs_t[64 * hf:64 * hf + nz, sl],
                            start=True, stop=True))(lhs_s, sl, hf, nz)]
                    t_re = mm_group(psK, mm2, waits=[rhs_tok] + wA)
                    t_im = t_re
                    re_ap = psK[:, 0:512]; im_ap = psK[:, 512:1024]
                    nch = 8
                slot_r(rhs_t, t_re); slot_r(rhs_t, t_im)
                wA = []
                w_ = width
                e_re = ps_read(ps_re, 'act', (lambda re_ap, w_: lambda h: h.copy(
                    ab_r[:, 0:w_], re_ap))(re_ap, w_), waits=[tw_prev[0], tw_prev[1]])
                e_im = ps_read(ps_im, 'act', (lambda im_ap, w_: lambda h: h.copy(
                    ab_i[:, 0:w_], im_ap))(im_ap, w_), waits=[tw_prev[0], tw_prev[1]])
                abr3 = ab_r[:, 0:w_].rearrange("p (c n) -> p c n", n=N1)
                abi3 = ab_i[:, 0:w_].rearrange("p (c n) -> p c n", n=N1)
                twc_b = tab_s['twc'][:].unsqueeze(1).broadcast_to([128, nch, N1])
                tws_b = tab_s['tws'][:].unsqueeze(1).broadcast_to([128, nch, N1])
                a1 = S.op('dve', (lambda abr3, twc_b, w_: lambda h: h.tensor_tensor(
                    tb1[:, 0:w_].rearrange("p (c n) -> p c n", n=N1), abr3, twc_b, ALU.mult))(abr3, twc_b, w_), waits=[e_re, tw_prev[0]])
                a2 = S.op('dve', (lambda abi3, tws_b, w_: lambda h: h.tensor_tensor(
                    tb2[:, 0:w_].rearrange("p (c n) -> p c n", n=N1), abi3, tws_b, ALU.mult))(abi3, tws_b, w_), waits=[e_im])
                a3 = S.op('dve', (lambda col0, w_: lambda h: h.tensor_sub(
                    Bd_r[:, col0:col0 + w_], tb1[:, 0:w_], tb2[:, 0:w_]))(col0, w_), waits=[a1, a2])
                slot_w(Bd_r, a3)
                a4 = S.op('pool', (lambda abr3, tws_b, w_: lambda h: h.tensor_tensor(
                    tg1[:, 0:w_].rearrange("p (c n) -> p c n", n=N1), abr3, tws_b, ALU.mult))(abr3, tws_b, w_), waits=[e_re, tw_prev[1]])
                a5 = S.op('pool', (lambda abi3, twc_b, w_: lambda h: h.tensor_tensor(
                    tg2[:, 0:w_].rearrange("p (c n) -> p c n", n=N1), abi3, twc_b, ALU.mult))(abi3, twc_b, w_), waits=[e_im])
                a6 = S.op('pool', (lambda col0, w_: lambda h: h.tensor_add(
                    Bd_i[:, col0:col0 + w_], tg1[:, 0:w_], tg2[:, 0:w_]))(col0, w_), waits=[a4, a5])
                slot_w(Bd_i, a6)
                tw_prev[0] = a3
                tw_prev[1] = a6

    def trans_batch(src_r, src_i):
        wB = slot_begin(Brt) + slot_begin(Bit)
        for plane in range(2):
            src = src_r if plane == 0 else src_i
            dst = Brt if plane == 0 else Bit
            for g in range(8):
                tp = psT[plane]
                tts = None
                for j in range(4):
                    pr = g * 4 + j
                    tfn = (lambda src, pr, tp, j: lambda h: h.transpose(
                        tp[:, j * 128:(j + 1) * 128], src[:, pr * 128:(pr + 1) * 128], IDB[:]))(src, pr, tp, j)
                    if j == 0:
                        tts = pe_transpose(tp, tfn, waits=slot_wtok(src) + wB)
                        wB = []
                    else:
                        stt_ = ps_state[id(tp)]
                        tts = S.op('pe', tfn)
                        stt_['w'] = tts
                slot_r(src, tts)
                ev = ps_read(tp, 'act', (lambda dst, g, tp: lambda h: h.copy(
                    dst[:, g * 512:(g + 1) * 512], tp[:, 0:512]))(dst, g, tp))
                slot_w(dst, ev)

    def s2f(Rdst, Idst, fold_tab, mode, ck, tabc='d64c', tabs_='d64s', tabsn='d64sn'):
        wR = slot_begin(Rdst) + slot_begin(Idst)
        npieces = 4 if mode == 'P' else 8
        width = 1024 if mode == 'P' else 512
        for piece in range(npieces):
            col0 = piece * width
            if mode == 'P':
                ps_re = psP[0]; ps_im = psP[1]
                re_ap = psP[0][:]; im_ap = psP[1][:]
            else:
                ps_re = psK; ps_im = psK
                re_ap = psK[:, 0:512]; im_ap = psK[:, 512:1024]
            mre = []; mim = []
            for par in range(2):
                rb = slice(64 * par, 64 * par + 64)
                tp_pos = (64 * par, 64 * par)
                nsub = 2 if mode == 'P' else 1
                for half in range(nsub):
                    s2_ = slice(col0 + half * 512, col0 + half * 512 + 512)
                    if mode == 'P':
                        ob = slice(half * 512, half * 512 + 512)
                        mre += [
                            (lambda ob, s2_, tp_pos, rb: lambda h: h.matmul(
                                psP[0][tp_pos[1]:tp_pos[1] + 64, ob], tab_s[tabc][rb, :], Brt[rb, s2_],
                                start=True, stop=False, tile_position=tp_pos))(ob, s2_, tp_pos, rb),
                            (lambda ob, s2_, tp_pos, rb: lambda h: h.matmul(
                                psP[0][tp_pos[1]:tp_pos[1] + 64, ob], tab_s[tabsn][rb, :], Bit[rb, s2_],
                                start=False, stop=True, tile_position=tp_pos))(ob, s2_, tp_pos, rb)]
                        mim += [
                            (lambda ob, s2_, tp_pos, rb: lambda h: h.matmul(
                                psP[1][tp_pos[1]:tp_pos[1] + 64, ob], tab_s[tabs_][rb, :], Brt[rb, s2_],
                                start=True, stop=False, tile_position=tp_pos))(ob, s2_, tp_pos, rb),
                            (lambda ob, s2_, tp_pos, rb: lambda h: h.matmul(
                                psP[1][tp_pos[1]:tp_pos[1] + 64, ob], tab_s[tabc][rb, :], Bit[rb, s2_],
                                start=False, stop=True, tile_position=tp_pos))(ob, s2_, tp_pos, rb)]
                    else:
                        mre += [
                            (lambda s2_, tp_pos, rb: lambda h: h.matmul(
                                psK[tp_pos[1]:tp_pos[1] + 64, 0:512], tab_s[tabc][rb.start:rb.stop, :], Brt[rb, s2_],
                                start=True, stop=False, tile_position=tp_pos))(s2_, tp_pos, rb),
                            (lambda s2_, tp_pos, rb: lambda h: h.matmul(
                                psK[tp_pos[1]:tp_pos[1] + 64, 0:512], tab_s[tabsn][rb.start:rb.stop, :], Bit[rb, s2_],
                                start=False, stop=True, tile_position=tp_pos))(s2_, tp_pos, rb)]
                        mim += [
                            (lambda s2_, tp_pos, rb: lambda h: h.matmul(
                                psK[tp_pos[1]:tp_pos[1] + 64, 512:1024], tab_s[tabs_][rb.start:rb.stop, :], Brt[rb, s2_],
                                start=True, stop=False, tile_position=tp_pos))(s2_, tp_pos, rb),
                            (lambda s2_, tp_pos, rb: lambda h: h.matmul(
                                psK[tp_pos[1]:tp_pos[1] + 64, 512:1024], tab_s[tabc][rb.start:rb.stop, :], Bit[rb, s2_],
                                start=False, stop=True, tile_position=tp_pos))(s2_, tp_pos, rb)]
            wdep = slot_wtok(Brt) + slot_wtok(Bit) + wR
            wR = []
            if mode == 'P':
                t_re = mm_group(ps_re, mre, waits=wdep)
                t_im = mm_group(ps_im, mim, waits=wdep)
            else:
                t_re = mm_group(psK, mre + mim, waits=wdep)
                t_im = t_re
            slot_r(Brt, t_im); slot_r(Bit, t_im)
            e1 = ps_read(ps_re, 'act', (lambda col0, re_ap, width: lambda h: h.copy(
                Rdst[:, col0:col0 + width], re_ap))(col0, re_ap, width))
            e2 = ps_read(ps_im, 'act', (lambda col0, im_ap, width: lambda h: h.copy(
                Idst[:, col0:col0 + width], im_ap))(col0, im_ap, width))
            slot_w(Rdst, e1); slot_w(Idst, e2)
            if fold_tab is not None:
                kdv = fold_tab[:, ck * 32:ck * 32 + 32]
                kdb = kdv.unsqueeze(2).broadcast_to([128, 32, 128])
                dst3 = Rdst[:, col0:col0 + width].rearrange("p (a k) -> p a k", k=128)
                a0 = col0 // 128
                kdb3 = kdb[:, a0:a0 + width // 128, :]
                e3 = S.op('dve', (lambda dst3, kdb3: lambda h: h.tensor_tensor(
                    dst3, dst3, kdb3, ALU.add))(dst3, kdb3), waits=[e1])
                slot_w(Rdst, e3)

    def inv_s1tw():
        wA = slot_begin(Ar) + slot_begin(Ai)
        pw = slot_wtok(Brt) + slot_wtok(Bit)
        q_prev = ab_g
        for piece in range(4):
            ps_re = psP[0]; ps_im = psP[1]
            mre = []; mim = []
            for par in range(2):
                rb = slice(64 * par, 64 * par + 64)
                tp_pos = (64 * par, 64 * par)
                for half in range(2):
                    s2_ = slice(piece * 1024 + half * 512, piece * 1024 + half * 512 + 512)
                    ob = slice(half * 512, half * 512 + 512)
                    mre.append((lambda rb, s2_, ob, tp_pos: lambda h: h.matmul(
                        psP[0][tp_pos[1]:tp_pos[1] + 64, ob], tab_s['i64c'][rb, :], Brt[rb, s2_],
                        start=True, stop=False, tile_position=tp_pos))(rb, s2_, ob, tp_pos))
                    mre.append((lambda rb, s2_, ob, tp_pos: lambda h: h.matmul(
                        psP[0][tp_pos[1]:tp_pos[1] + 64, ob], tab_s['i64sn'][rb, :], Bit[rb, s2_],
                        start=False, stop=True, tile_position=tp_pos))(rb, s2_, ob, tp_pos))
                    mim.append((lambda rb, s2_, ob, tp_pos: lambda h: h.matmul(
                        psP[1][tp_pos[1]:tp_pos[1] + 64, ob], tab_s['i64s'][rb, :], Brt[rb, s2_],
                        start=True, stop=False, tile_position=tp_pos))(rb, s2_, ob, tp_pos))
                    mim.append((lambda rb, s2_, ob, tp_pos: lambda h: h.matmul(
                        psP[1][tp_pos[1]:tp_pos[1] + 64, ob], tab_s['i64c'][rb, :], Bit[rb, s2_],
                        start=False, stop=True, tile_position=tp_pos))(rb, s2_, ob, tp_pos))
            wdep = pw + wA
            wA = []; pw = []
            t_re = mm_group(ps_re, mre, waits=wdep)
            t_im = mm_group(ps_im, mim, waits=wdep)
            slot_r(Brt, t_im); slot_r(Bit, t_im)
            col0 = piece * 1024
            e_re = ps_read(ps_re, 'act', (lambda: lambda h: h.copy(ab_r[:], psP[0][:]))(),
                           waits=[q_prev[0], q_prev[1]])
            e_im = ps_read(ps_im, 'act', (lambda: lambda h: h.copy(ab_i[:], psP[1][:]))(),
                           waits=[q_prev[0], q_prev[1]])
            abr3 = ab_r[:].rearrange("p (a k) -> p a k", k=128)
            abi3 = ab_i[:].rearrange("p (a k) -> p a k", k=128)
            itc_b = tab_s['itwc'][:].unsqueeze(1).broadcast_to([128, 8, 128])
            its_b = tab_s['itws'][:].unsqueeze(1).broadcast_to([128, 8, 128])
            q1 = S.op('dve', (lambda abr3, itc_b: lambda h: h.tensor_tensor(
                tb1[:].rearrange("p (a k) -> p a k", k=128), abr3, itc_b, ALU.mult))(abr3, itc_b), waits=[e_re, q_prev[0]])
            q2 = S.op('dve', (lambda abi3, its_b: lambda h: h.tensor_tensor(
                tb2[:].rearrange("p (a k) -> p a k", k=128), abi3, its_b, ALU.mult))(abi3, its_b), waits=[e_im])
            q3 = S.op('dve', (lambda col0: lambda h: h.tensor_sub(
                Ar[:, col0:col0 + 1024], tb1[:], tb2[:]))(col0), waits=[q1, q2])
            slot_w(Ar, q3)
            q4 = S.op('pool', (lambda abr3, its_b: lambda h: h.tensor_tensor(
                tg1[:].rearrange("p (a k) -> p a k", k=128), abr3, its_b, ALU.mult))(abr3, its_b), waits=[e_re, q_prev[1]])
            q5 = S.op('pool', (lambda abi3, itc_b: lambda h: h.tensor_tensor(
                tg2[:].rearrange("p (a k) -> p a k", k=128), abi3, itc_b, ALU.mult))(abi3, itc_b), waits=[e_im])
            q6 = S.op('pool', (lambda col0: lambda h: h.tensor_add(
                Ai[:, col0:col0 + 1024], tg1[:], tg2[:]))(col0), waits=[q4, q5])
            slot_w(Ai, q6)
            q_prev[0] = q3
            q_prev[1] = q6

    def inv_s2(out_kind):
        if out_kind == 'kv':
            wkv = slot_begin(kvt)
            for pg in range(2):
                po = 64 * pg
                ps = psP[pg]
                for halfc in range(2):
                    lc = halfc * 1024
                    mm2 = []
                    for sub in range(2):
                        s3 = slice(pg * 2048 + halfc * 1024 + sub * 512,
                                   pg * 2048 + halfc * 1024 + sub * 512 + 512)
                        ob = slice(sub * 512, sub * 512 + 512)
                        mm2.append((lambda s3, ob, po, ps: lambda h: h.matmul(
                            ps[po:po + 64, ob], tab_s['i128c_full'][:], Brt[:, s3],
                            start=True, stop=False, tile_position=(0, po)))(s3, ob, po, ps))
                        mm2.append((lambda s3, ob, po, ps: lambda h: h.matmul(
                            ps[po:po + 64, ob], tab_s['i128sn_full'][:], Bit[:, s3],
                            start=False, stop=True, tile_position=(0, po)))(s3, ob, po, ps))
                    wdep = slot_wtok(Brt) + slot_wtok(Bit) + wkv + slot_wtok(vfft)
                    wkv = []
                    tmm = mm_group(ps, mm2, waits=wdep)
                    slot_r(Brt, tmm); slot_r(Bit, tmm)
                    ev = ps_read(ps, 'dve', (lambda ps, po, lc: lambda h: h.tensor_mul(
                        kvt[po:po + 64, lc:lc + 1024], ps[po:po + 64, 0:1024],
                        vfft[po:po + 64, lc:lc + 1024]))(ps, po, lc))
                    slot_r(vfft, ev)
                    slot_w(kvt, ev)
            return
        wy = slot_begin(y2s)
        for piece in range(4):
            sl1024 = slice(piece * 1024, piece * 1024 + 1024)
            ps = psP[piece % 2]
            mm2 = []
            for sub in range(2):
                s3 = slice(piece * 1024 + sub * 512, piece * 1024 + sub * 512 + 512)
                ob = slice(sub * 512, sub * 512 + 512)
                mm2.append((lambda s3, ob, ps: lambda h: h.matmul(
                    ps[0:32, ob], i128c_own_s[:], Brt[:, s3], start=True, stop=False))(s3, ob, ps))
                mm2.append((lambda s3, ob, ps: lambda h: h.matmul(
                    ps[0:32, ob], i128sn_own_s[:], Bit[:, s3], start=False, stop=True))(s3, ob, ps))
            wdep = slot_wtok(Brt) + slot_wtok(Bit) + wy
            wy = []
            tmm = mm_group(ps, mm2, waits=wdep)
            slot_r(Brt, tmm); slot_r(Bit, tmm)
            ev = ps_read(ps, 'dve', (lambda ps, sl1024: lambda h: h.tensor_copy(
                y2s[:, sl1024], ps[0:32, 0:1024]))(ps, sl1024))
            slot_w(y2s, ev)

    def product():
        wB = slot_begin(Brt) + slot_begin(Bit)
        deps = slot_wtok(Ar) + slot_wtok(Ai) + slot_wtok(Kr) + slot_wtok(Ki)
        for piece in range(4):
            sl = slice(piece * 1024, piece * 1024 + 1024)
            p1 = S.op('dve', (lambda sl: lambda h: h.tensor_mul(tb1[:], Ar[:, sl], Kr[:, sl]))(sl),
                      waits=deps + wB + [ab_g[0]])
            p2 = S.op('dve', (lambda sl: lambda h: h.tensor_mul(tb2[:], Ai[:, sl], Ki[:, sl]))(sl))
            p3 = S.op('dve', (lambda sl: lambda h: h.tensor_sub(Brt[:, sl], tb1[:], tb2[:]))(sl),
                      waits=[p1, p2])
            p4 = S.op('pool', (lambda sl: lambda h: h.tensor_mul(tg1[:], Ar[:, sl], Ki[:, sl]))(sl),
                      waits=deps + wB + [ab_g[1]])
            p5 = S.op('pool', (lambda sl: lambda h: h.tensor_mul(tg2[:], Ai[:, sl], Kr[:, sl]))(sl))
            p6 = S.op('pool', (lambda sl: lambda h: h.tensor_add(Bit[:, sl], tg1[:], tg2[:]))(sl),
                      waits=[p4, p5])
            deps = []; wB = []
            for x in (Ar, Ai, Kr, Ki):
                slot_r(x, p3); slot_r(x, p6)
            slot_w(Brt, p3); slot_w(Bit, p6)
            ab_g[0] = p3
            ab_g[1] = p6

    def conv_pass(ck, sig_t, sig_tok, ker_view, fold_tab, out_kind):
        wkk = slot_begin(kkfft)
        kl = None
        for hf in range(2):
            kl = S.dma('pool', kkfft[64 * hf:64 * hf + 32, :].rearrange("m (c n) -> m c n", n=N1),
                       ker_view[ck, hf], waits=wkk)
            wkk = []
        slot_w(kkfft, kl)
        s1tw(kkfft, kl, 32, Kr, Ki, 'K')
        s1tw(sig_t, sig_tok, 64, Ar, Ai, 'P')
        trans_batch(Kr, Ki)
        s2f(Kr, Ki, fold_tab, 'K', ck)
        trans_batch(Ar, Ai)
        s2f(Ar, Ai, None, 'P', ck)
        product()
        inv_s1tw()
        trans_batch(Ar, Ai)
        inv_s2(out_kind)

    for ck in range(NCHUNK):
        pchunk = ck // 2
        cchunk = ck // 2
        if ck % 2 == 0:
            bounce_k[pchunk], _ = proj_cm(WkT, wk_s, bk_s[:, pchunk:pchunk + 1], kscr, pchunk, cm_guard['wk'])
            cm_guard['wk'] = bounce_k[pchunk]
            bounce_v[pchunk], _ = proj_cm(WvT, wv_s, bv_s[:, pchunk:pchunk + 1], vscr, pchunk, cm_guard['wv'])
            cm_guard['wv'] = bounce_v[pchunk]
        wk0 = slot_begin(kfft)
        for hf in range(2):
            krt = S.dma('pool', kfft[64 * hf:64 * hf + 64, :].rearrange("m (c n) -> m c n", n=N1),
                        kscr_v[ck, hf], waits=[bounce_k[pchunk]] + wk0)
            wk0 = []
        slot_w(kfft, krt)
        wv0 = slot_begin(vfft)
        for hf in range(2):
            vrt = S.dma('pool', vfft[64 * hf:64 * hf + 64, :].rearrange("m (c n) -> m c n", n=N1),
                        vscr_v[ck, hf], waits=[bounce_v[pchunk]] + wv0)
            wv0 = []
        slot_w(vfft, vrt)
        conv_pass(ck, kfft, krt, kker_v, kd_s, 'kv')
        if debug and ck == 0:
            slot_r(kfft, S.dma('sp', dbg_kf, kfft[:], waits=slot_wtok(kfft)))
            slot_r(kvt, S.dma('sp', dbg_kv, kvt[:], waits=slot_wtok(kvt)))
            slot_r(Ar, S.dma('sp', dbg_sR, Ar[:], waits=slot_wtok(Ar)))
            slot_r(Kr, S.dma('sp', dbg_kR, Kr[:], waits=slot_wtok(Kr)))
        kv_tok = slot_wtok(kvt)
        conv_pass(ck, kvt, kv_tok[0] if kv_tok else None, sker_v, dd_s, 'y')
        ytoks = slot_wtok(y2s)
        cb = (ck * CH) % 128
        y3 = y2s[:].rearrange("p (a b n) -> p a b n", a=32, b=2)
        for n1g in range(16):
            tp = psT[n1g % 2]
            tts = None
            for j in range(4):
                n1v = n1g * 4 + j
                tfn = (lambda n1v, j, tp, cb: lambda h: h.transpose(
                    tp[cb:cb + 64, j * 32:(j + 1) * 32],
                    y3[:, :, :, n1v].rearrange("p a b -> p (a b)"),
                    IDB[0:32, 0:32], tile_position=(0, cb)))(n1v, j, tp, cb)
                if j == 0:
                    tts = pe_transpose(tp, tfn, waits=ytoks)
                else:
                    st = ps_state[id(tp)]
                    tts = S.op('pe', tfn)
                    st['w'] = tts
            slot_r(y2s, tts)
            dst = ycm[cchunk][cb:cb + 64, :].rearrange("p (m n) -> p m n", n=N1)
            def evfn(h, dst=dst, tp=tp, cb=cb, n1g=n1g):
                o = dst[:, :, n1g * 4:(n1g + 1) * 4].rearrange("p m n -> p n m")
                return h.tensor_copy(o, tp[cb:cb + 64, 0:128].rearrange("p (j m) -> p j m", j=4))
            ev = ps_read(tp, 'dve', evfn)
            ycm_written[0] = ev

    if debug:
        for hc in range(8):
            S.dma('sp', dbg_ycm[hc], ycm[hc][:], waits=[ycm_written[0]])

    S.barrier()
    stack_ab.close()
    xn_own = [sb(f"xno{i}", [128, 1024], BF16) for i in range(8)]
    yacc = [sb(f"yacc{i}", [128, DM], F32) for i in range(8)]
    w1s = [sb(f"w1s{i}", [128, 8 * 512], BF16) for i in range(2)]
    w2s = [sb(f"w2s{i}", [128, 4 * DM], BF16) for i in range(2)]
    hbuf = [sb(f"hbuf{i}", [128, 1024], BF16) for i in range(8)]
    wot_s = [sb(f"wot{i}", [128, DM], BF16) for i in range(8)]
    wq_s = sb("wq_s", [128, 8 * 128], BF16)
    wr_s = sb("wr_s", [128, 8 * 8], BF16)
    b2_s = sb("b2_s", [8, DM], BF16)
    b1c_s = sb("b1c_s", [128, 256], F32)
    bo_s = sb("bo_s", [128, DM], F32)
    br_s = sb("br_s", [8, 1], F32)
    lg_em = sb("lg_em", [8, 1024], F32)
    lg_tm = sb("lg_tm", [128, 8], F32)
    prb = sb("prb", [128, 8], F32)
    Sb = sb("Sb", [128, 8], F32)
    tmp8 = sb("tmp8", [128, 8], F32)
    gate_tm = sb("gate_tm", [128, 8 * 8], F32)
    gate_cm = sb("gate_cm", [8, 1024], BF16)
    outt = [sb(f"outt{i}", [128, DM], F32) for i in range(2)]
    uot = [sb(f"uot{i}", [128, DM], F32) for i in range(2)]

    tC = S.dma('pool', wr_s[:].rearrange("p (hc e) -> p hc e", hc=8),
               Wr.rearrange("(hc p) e -> p hc e", hc=8))
    tC = S.dma('pool', b2_s[:], b2d)
    for hc in range(8):
        tC = S.dma('pool', wot_s[hc][:], WoT.rearrange("(hc p) o -> hc p o", hc=8)[hc])
    tCp = tC
    tC2 = S.dma('sp', b1c_s[:], b1cd)
    tC2 = S.dma('sp', bo_s[:], bo_rep)
    tC2 = S.dma('sp', br_s[:], br_c)
    tCs = tC2

    STORE_TOKS = []
    mm2_prev = None

    for half in range(2):
        hb = half * 1024
        xo_done = do_ln(u_own[half * 1024:(half + 1) * 1024, :], 8, xn_own)
        lg_toks = []
        for pc in range(2):
            mmsp = []
            for hc in range(8):
                mmsp.append((lambda pc, hc: lambda h: h.matmul(
                    psS[0:8, 0:512], wr_s[:, hc * 8:(hc + 1) * 8],
                    xn_own[hc][:, pc * 512:(pc + 1) * 512],
                    start=(hc == 0), stop=(hc == 7)))(pc, hc))
            tmm = mm_group(psS, mmsp, waits=[xo_done, tCp, tCs])
            ev = ps_read(psS, 'act', (lambda pc: lambda h: h.activation(
                lg_em[:, pc * 512:(pc + 1) * 512], psS[0:8, 0:512], AF.Identity,
                bias=br_s[:], scale=1.0))(pc))
            lg_toks.append(ev)
        gate_done = []
        prev_topp = None
        for ti in range(8):
            ps = psP[2]
            tt = pe_transpose(ps, (lambda ti, ps: lambda h: h.transpose(
                ps[0:128, 0:8], lg_em[:, ti * 128:(ti + 1) * 128], IDF[0:8, 0:8]))(ti, ps),
                waits=lg_toks + [prev_topp])
            c1 = ps_read(ps, 'dve', (lambda ps: lambda h: h.tensor_copy(lg_tm[:], ps[0:128, 0:8]))(ps))
            m1 = S.op('dve', lambda h: h.reduce_max(st_s[:, 0:1], lg_tm[:], axis=mybir.AxisListType.X), waits=[c1])
            m2 = S.op('dve', lambda h: h.tensor_scalar(lg_tm[:], lg_tm[:], st_s[:, 0:1], None, ALU.subtract), waits=[m1])
            m3 = S.op('act', lambda h: h.activation(prb[:], lg_tm[:], AF.Exp), waits=[m2])
            m4 = S.op('dve', lambda h: h.reduce_sum(st_s[:, 1:2], prb[:], axis=mybir.AxisListType.X), waits=[m3])
            m5 = S.op('dve', lambda h: h.reciprocal(st_s[:, 2:3], st_s[:, 1:2]), waits=[m4])
            m6 = S.op('dve', lambda h: h.tensor_scalar(prb[:], prb[:], st_s[:, 2:3], None, ALU.mult), waits=[m5])
            last = m6
            for j in range(8):
                j1 = S.op('dve', (lambda j: lambda h: h.tensor_scalar(tmp8[:], prb[:], prb[:, j:j + 1], None, ALU.is_ge))(j), waits=[last])
                j2 = S.op('dve', lambda h: h.tensor_mul(tmp8[:], tmp8[:], prb[:]), waits=[j1])
                last = S.op('dve', (lambda j: lambda h: h.reduce_sum(Sb[:, j:j + 1], tmp8[:], axis=mybir.AxisListType.X))(j), waits=[j2])
            k1 = S.op('dve', lambda h: h.tensor_scalar(tmp8[:], Sb[:], 0.8, None, ALU.is_lt), waits=[last])
            k2 = S.op('dve', lambda h: h.tensor_tensor(Sb[:], Sb[:], prb[:], ALU.is_equal), waits=[k1])
            k3 = S.op('dve', lambda h: h.tensor_max(tmp8[:], tmp8[:], Sb[:]), waits=[k2])
            k4 = S.op('dve', (lambda ti: lambda h: h.tensor_mul(
                gate_tm[:, ti * 8:(ti + 1) * 8], tmp8[:], prb[:]))(ti), waits=[k3])
            prev_topp = k4
            tt2 = pe_transpose(ps, (lambda ti, ps: lambda h: h.transpose(
                ps[0:8, 0:128], gate_tm[:, ti * 8:(ti + 1) * 8], IDF[:]))(ti, ps), waits=[k4])
            ev2 = ps_read(ps, 'dve', (lambda ti, ps: lambda h: h.tensor_copy(
                gate_cm[:, ti * 128:(ti + 1) * 128], ps[0:8, 0:128]))(ti, ps))
            gate_done.append(ev2)
        yacc_last = [None] * 8
        yinit = []
        for ti in range(8):
            ps = psP[0]
            mmb = []
            for oh in range(2):
                mmb.append((lambda ti, oh, ps: lambda h: h.matmul(
                    ps[:, oh * 512:(oh + 1) * 512], gate_cm[:, ti * 128:(ti + 1) * 128],
                    b2_s[:, oh * 512:(oh + 1) * 512], start=True, stop=True))(ti, oh, ps))
            tmm = mm_group(ps, mmb, waits=[gate_done[ti], tCp, tCs])
            ev = ps_read(ps, 'dve', (lambda ti, ps: lambda h: h.tensor_add(
                yacc[ti][:], ps[:], bo_s[:]))(ti, ps), waits=[yacc_last[ti]])
            yacc_last[ti] = ev
            yinit.append(ev)
        g_done = []
        prev_q = None
        for dc in range(8):
            wl = S.dma('pool', wq_s[:].rearrange("p (hc d) -> p hc d", hc=8),
                       WqT.rearrange("(hc p) d -> p hc d", hc=8)[:, :, dc * 128:(dc + 1) * 128],
                       waits=[prev_q])
            for pc in range(2):
                ps = psP[pc % 2]
                mmq = []
                for hc in range(8):
                    mmq.append((lambda pc, hc, ps: lambda h: h.matmul(
                        ps[:, 0:512], wq_s[:, hc * 128:(hc + 1) * 128],
                        xn_own[hc][:, pc * 512:(pc + 1) * 512],
                        start=(hc == 0), stop=(hc == 7)))(pc, hc, ps))
                tmm = mm_group(ps, mmq, waits=[wl, xo_done, ycm_written[0]])
                ev = ps_read(ps, 'dve', (lambda dc, pc, ps, hb: lambda h: h.scalar_tensor_tensor(
                    ycm[dc][:, hb + pc * 512: hb + pc * 512 + 512], ps[:, 0:512],
                    bq_s[:, dc:dc + 1],
                    ycm[dc][:, hb + pc * 512: hb + pc * 512 + 512],
                    ALU.add, ALU.mult))(dc, pc, ps, hb))
                g_done.append(ev)
                prev_q = ev
        g_last = g_done[-1]
        w_use = {'w1': [None, None], 'w2': [None, None]}
        ld = {}

        def issue_w1(gfs, e, fs):
            pb = gfs % 2
            lt = S.dma('pool', w1s[pb][:].rearrange("p (hc f) -> p hc f", hc=8),
                       W1[e].rearrange("(hc p) f -> p hc f", hc=8)[:, :, fs * 512:(fs + 1) * 512],
                       waits=[w_use['w1'][pb]])
            ld[('w1', gfs)] = lt

        def issue_w2(gfs, e, fs):
            pb = gfs % 2
            lt = S.dma('pool', w2s[pb][:].rearrange("p (fc o) -> p fc o", fc=4),
                       W2[e].rearrange("(fc p) o -> p fc o", fc=32)[:, fs * 4:(fs + 1) * 4, :],
                       waits=[w_use['w2'][pb]])
            ld[('w2', gfs)] = lt

        issue_w1(0, 0, 0); issue_w2(0, 0, 0)
        issue_w1(1, 0, 1); issue_w2(1, 0, 1)
        h_done = [None] * 8
        for e in range(8):
            for pairi in range(4):
                for sub in range(2):
                    fs = pairi * 2 + sub
                    gfs = e * 8 + fs
                    pb = gfs % 2
                    lt1 = ld[('w1', gfs)]
                    mm1_last = None
                    for fc in range(4):
                        psm = psP[fc % 2]
                        mms = []
                        for hc in range(8):
                            for tq in range(2):
                                mms.append((lambda fc, hc, tq, psm, pb: lambda h: h.matmul(
                                    psm[:, tq * 512:(tq + 1) * 512],
                                    w1s[pb][:, hc * 512 + fc * 128: hc * 512 + fc * 128 + 128],
                                    xn_own[hc][:, tq * 512:(tq + 1) * 512],
                                    start=(hc == 0), stop=(hc == 7)))(fc, hc, tq, psm, pb))
                        tmm = mm_group(psm, mms, waits=[lt1, xo_done])
                        mm1_last = tmm
                        bcol = b1c_s[:, e * 32 + fs * 4 + fc: e * 32 + fs * 4 + fc + 1]
                        hidx = sub * 4 + fc
                        ev = ps_read(psm, 'dve', (lambda psm, bcol, hidx: lambda h: h.tensor_scalar(
                            hbuf[hidx][:], psm[:], bcol, 0.0, ALU.add, ALU.max))(psm, bcol, hidx))
                        h_done[hidx] = ev
                    w_use['w1'][pb] = mm1_last
                    if gfs + 2 < 64:
                        e2_, fs2 = divmod(gfs + 2, 8)
                        issue_w1(gfs + 2, e2_, fs2)
                mm2_tok = None
                for tt_ in range(8):
                    psm2 = psP[2]
                    mms2 = []
                    for fc8 in range(8):
                        sub2 = fc8 // 4; fc = fc8 % 4
                        pb2 = (e * 8 + pairi * 2 + sub2) % 2
                        for oh in range(2):
                            mms2.append((lambda fc8, fc, sub2, pb2, oh, tt_, psm2: lambda h: h.matmul(
                                psm2[:, oh * 512:(oh + 1) * 512],
                                hbuf[sub2 * 4 + fc][:, tt_ * 128:(tt_ + 1) * 128],
                                w2s[pb2][:, fc * 1024 + oh * 512: fc * 1024 + oh * 512 + 512],
                                start=(fc8 == 0), stop=(fc8 == 7)))(fc8, fc, sub2, pb2, oh, tt_, psm2))
                    wdep = _maxper(h_done) + [ld[('w2', e * 8 + pairi * 2)], ld[('w2', e * 8 + pairi * 2 + 1)]]
                    tmm2 = mm_group(psm2, mms2, waits=wdep)
                    mm2_tok = tmm2
                    gcol = gate_tm[:, tt_ * 8 + e: tt_ * 8 + e + 1]
                    ev2 = ps_read(psm2, 'dve', (lambda psm2, gcol, tt_: lambda h: h.scalar_tensor_tensor(
                        yacc[tt_][:], psm2[:], gcol, yacc[tt_][:], ALU.mult, ALU.add))(psm2, gcol, tt_),
                        waits=[yacc_last[tt_], gate_done[tt_]])
                    yacc_last[tt_] = ev2
                    mm2_prev = ev2
                for q_ in range(2):
                    gfs = e * 8 + pairi * 2 + q_
                    w_use['w2'][gfs % 2] = mm2_tok
                    if gfs + 2 < 64:
                        e2_, fs2 = divmod(gfs + 2, 8)
                        issue_w2(gfs + 2, e2_, fs2)
        if debug and half == 0:
            S.dma('sp', dbg_prb, prb[:], waits=[gate_done[-1]])
            S.dma('sp', dbg_Sb, Sb[:], waits=[gate_done[-1]])
            for hc in range(8):
                S.dma('sp', dbg_xno[hc], xn_own[hc][:], waits=[xo_done])
            S.dma('sp', dbg_lg, lg_em[:], waits=lg_toks)
            S.dma('sp', dbg_gate, gate_tm[:], waits=[gate_done[-1]])
            for ti in range(8):
                S.dma('sp', dbg_yacc[ti], yacc[ti][:], waits=[mm2_prev])
        for ti in range(8):
            ub = uot[ti % 2]
            ul = S.dma('sp', ub[:], u_own[half * 1024 + ti * 128: half * 1024 + (ti + 1) * 128, :],
                       waits=[STORE_TOKS[-2] if len(STORE_TOKS) >= 2 else None])
            psm = psP[ti % 2]
            mmo = []
            for chc in range(8):
                for oh in range(2):
                    mmo.append((lambda chc, oh, ti, psm, hb: lambda h: h.matmul(
                        psm[:, oh * 512:(oh + 1) * 512],
                        ycm[chc][:, hb + ti * 128: hb + ti * 128 + 128],
                        wot_s[chc][:, oh * 512:(oh + 1) * 512],
                        start=(chc == 0), stop=(chc == 7)))(chc, oh, ti, psm, hb))
            tmm = mm_group(psm, mmo, waits=[g_last, tCp, mm2_prev])
            ob = outt[ti % 2]
            e1 = ps_read(psm, 'dve', (lambda psm, ti, ob: lambda h: h.tensor_add(
                ob[:], psm[:], yacc[ti][:]))(psm, ti, ob),
                waits=[yacc_last[ti], STORE_TOKS[-2] if len(STORE_TOKS) >= 2 else None])
            e2 = S.op('dve', (lambda ob, ub: lambda h: h.tensor_add(ob[:], ob[:], ub[:]))(ob, ub),
                      waits=[e1, ul])
            stt = S.dma('sp', out_own[half * 1024 + ti * 128: half * 1024 + (ti + 1) * 128, :],
                        ob[:], waits=[e2])
            STORE_TOKS.append(stt)

    def make_runner(name):
        def run(h):
            for kind, waits, payload, tok in S.q[name]:
                for (s, v) in waits:
                    h.wait_ge(sems[s], v)
                if kind == 'op':
                    r = payload(h)
                    if tok is not None:
                        r.then_inc(sems[tok[0]], 1)
                else:
                    out_ap, in_ap = payload
                    h.dma_start(out_ap, in_ap).then_inc(sems[tok[0]], 16)
        return run

    final_sp = S.cnt['sp16']
    with nc.Block() as block:
        @block.tensor
        def _(h):
            make_runner('pe')(h)

        @block.vector
        def _(h):
            make_runner('dve')(h)

        @block.scalar
        def _(h):
            make_runner('act')(h)

        @block.gpsimd
        def _(h):
            make_runner('pool')(h)

        @block.sync
        def _(h):
            make_runner('sp')(h)
            h.wait_ge(sems['sp16'], final_sp)

    stack.close()
    return nc


def _host_arrays(ins):
    T = _tables()
    f32 = lambda x: np.ascontiguousarray(np.asarray(x), np.float32) if np.asarray(x).dtype != np.float32 else np.ascontiguousarray(x)
    shared = {}
    shared['WqT'] = np.ascontiguousarray(np.asarray(ins['L0_attn_Wq']).T)
    shared['WkT'] = np.ascontiguousarray(np.asarray(ins['L0_attn_Wk']).T)
    shared['WvT'] = np.ascontiguousarray(np.asarray(ins['L0_attn_Wv']).T)
    shared['WoT'] = np.ascontiguousarray(np.asarray(ins['L0_attn_Wo']).T)
    shared['kker'] = np.ascontiguousarray(ins['L0_attn_k_kernel'])
    shared['sker'] = np.ascontiguousarray(ins['L0_attn_ssm_kernel'])
    shared['Wr'] = np.ascontiguousarray(ins['L0_ffn_Wr'])
    shared['W1'] = np.ascontiguousarray(ins['L0_ffn_W1'])
    shared['W2'] = np.ascontiguousarray(ins['L0_ffn_W2'])
    shared['b2d'] = np.ascontiguousarray(ins['L0_ffn_b2'])
    for nm, key in [('bq_c', 'L0_attn_bq'), ('bk_c', 'L0_attn_bk'), ('bv_c', 'L0_attn_bv')]:
        shared[nm] = np.ascontiguousarray(np.asarray(ins[key]).reshape(8, 128).T)
    shared['br_c'] = np.ascontiguousarray(np.asarray(ins['L0_ffn_br']).reshape(8, 1))
    shared['bo_rep'] = np.ascontiguousarray(np.tile(np.asarray(ins['L0_attn_bo'])[None, :], (128, 1)))
    b1 = np.asarray(ins['L0_ffn_b1'], np.float32)
    b1c = np.zeros((128, 256), np.float32)
    for e in range(8):
        for fcg in range(32):
            b1c[:, e * 32 + fcg] = b1[e, fcg * 128:(fcg + 1) * 128]
    shared['b1cd'] = b1c
    kD = np.asarray(ins['L0_attn_k_D'], np.float32)
    Dv = np.asarray(ins['L0_attn_D'], np.float32)

    def mk_pairs(vec):
        a = np.zeros((128, 512), np.float32)
        for ck in range(16):
            for pr in range(32):
                a[0:64, ck * 32 + pr] = vec[ck * 64 + 2 * pr]
                a[64:128, ck * 32 + pr] = vec[ck * 64 + 2 * pr + 1]
        return a.astype(BF)
    shared['kd_pr'] = mk_pairs(kD)
    shared['d_pr'] = mk_pairs(Dv)
    for k, v in T.items():
        shared['tab_' + k] = v
    u = np.asarray(ins['L0_attn_u'], np.float32)
    per_core = []
    for c in range(8):
        b, h = c // 2, c % 2
        d = dict(shared)
        d['u_b'] = np.ascontiguousarray(u[b])
        d['u_own'] = np.ascontiguousarray(u[b, h * 2048:(h + 1) * 2048])
        d['i128c_own'] = T['i128c_h1'] if h else T['i128c_h0']
        d['i128sn_own'] = T['i128sn_h1'] if h else T['i128sn_h0']
        per_core.append(d)
    return per_core


def kernel(**ins):
    nc = build_nc()
    in_maps = _host_arrays(ins)
    res = run_bass_kernel_spmd(nc, in_maps, core_ids=list(range(8)))
    out = np.zeros((4, 4096, 1024), np.float32)
    for c in range(8):
        b, h = c // 2, c % 2
        out[b, h * 2048:(h + 1) * 2048] = res.results[c]["out_own"]
    return out




# version 08 (final)
# speedup vs baseline: 1.0522x; 1.0384x over previous
import numpy as np
import ml_dtypes
import contextlib
import concourse.bass as bass
import concourse.mybir as mybir
from concourse.bass_utils import run_bass_kernel_spmd

F32 = mybir.dt.float32
BF16 = mybir.dt.bfloat16
AF = mybir.ActivationFunctionType
ALU = mybir.AluOpType
BF = ml_dtypes.bfloat16

NFFT = 8192
N1 = 64
N2 = 128
NCHUNK = 16
CH = 64
DM = 1024
T_FULL = 4096
T_OWN = 2048
PAIRS = 32
SPEC = 4096


def _tables():
    n2 = np.arange(N2); k2 = np.arange(N2)
    n1 = np.arange(N1); k1 = np.arange(N1)
    a1 = -2 * np.pi * np.outer(n2, k2) / N2
    dft128_c = np.cos(a1); dft128_s = np.sin(a1)
    at = -2 * np.pi * np.outer(k2, n1) / NFFT
    tw_c = np.cos(at); tw_s = np.sin(at)
    a2 = -2 * np.pi * np.outer(n1, k1) / N1
    dft64_c = np.cos(a2); dft64_s = np.sin(a2)
    ai1 = 2 * np.pi * np.outer(k1, n1) / N1
    idft64_c = np.cos(ai1); idft64_s = np.sin(ai1)
    ait = 2 * np.pi * np.outer(n1, k2) / NFFT
    itw_c = np.cos(ait); itw_s = np.sin(ait)
    ai2 = 2 * np.pi * np.outer(k2, n2) / N2
    idft128_c = np.cos(ai2) / NFFT; idft128_s = np.sin(ai2) / NFFT
    dup = lambda m: np.ascontiguousarray(np.concatenate([m[:64], m[:64]], 0))
    bfc = lambda m: np.ascontiguousarray(m, np.float32).astype(BF)
    t = {}
    t['d128c'] = bfc(dup(dft128_c)); t['d128s'] = bfc(dup(dft128_s))
    t['twc'] = bfc(tw_c); t['tws'] = bfc(tw_s)
    t['d64c'] = bfc(np.concatenate([dft64_c] * 2, 0))
    t['d64s'] = bfc(np.concatenate([dft64_s] * 2, 0))
    t['d64sn'] = bfc(np.concatenate([-dft64_s] * 2, 0))
    t['i64c'] = bfc(np.concatenate([idft64_c] * 2, 0))
    t['i64s'] = bfc(np.concatenate([idft64_s] * 2, 0))
    t['i64sn'] = bfc(np.concatenate([-idft64_s] * 2, 0))
    t['itwc'] = bfc(np.concatenate([itw_c] * 2, 0))
    t['itws'] = bfc(np.concatenate([itw_s] * 2, 0))
    t['i128c_full'] = bfc(idft128_c[:, :64])
    t['i128sn_full'] = bfc(-idft128_s[:, :64])
    t['i128c_h0'] = bfc(idft128_c[:, 0:32]); t['i128sn_h0'] = bfc(-idft128_s[:, 0:32])
    t['i128c_h1'] = bfc(idft128_c[:, 32:64]); t['i128sn_h1'] = bfc(-idft128_s[:, 32:64])
    t['ident_bf'] = bfc(np.eye(128))
    t['ident_f32'] = np.ascontiguousarray(np.eye(128), np.float32)
    return t


ENG_SEM = {'pe': 'pe', 'dve': 'dve', 'act': 'act', 'pool': 'pool', 'sp': None}


class Sched:
    def __init__(self):
        self.q = {'pe': [], 'dve': [], 'act': [], 'pool': [], 'sp': []}
        self.cnt = {s: 0 for s in ['pe', 'dve', 'act', 'pool', 'sp16', 'gd16']}
        self.lastw = {e: {} for e in self.q}
        self.pending = {}

    def _waits(self, eng, waits):
        best = {}
        for t in waits:
            if t is None:
                continue
            s, v = t
            if v > best.get(s, -1):
                best[s] = v
        out = []
        for s, v in best.items():
            if self.lastw[eng].get(s, -1) >= v:
                continue
            self.lastw[eng][s] = v
            out.append((s, v))
        return out

    def barrier(self):
        toks = []
        for s, v in self.cnt.items():
            if v > 0:
                toks.append((s, v))
        self.pending = {e: list(toks) for e in self.q}

    def op(self, eng, fn, waits=(), inc=True):
        w = self._waits(eng, list(waits) + self.pending.pop(eng, []))
        sem = ENG_SEM[eng]
        tok = None
        if inc and sem:
            self.cnt[sem] += 1
            tok = (sem, self.cnt[sem])
        self.q[eng].append(('op', w, fn, tok))
        return tok

    def dma(self, eng, out_ap, in_ap, waits=()):
        sem = 'sp16' if eng == 'sp' else 'gd16'
        prior = (sem, self.cnt[sem]) if self.cnt[sem] else None
        w = self._waits(eng, list(waits) + [prior] + self.pending.pop(eng, []))
        self.cnt[sem] += 16
        tok = (sem, self.cnt[sem])
        self.q[eng].append(('dma', w, (out_ap, in_ap), tok))
        return tok


def _maxper(toks):
    best = {}
    for t in toks:
        if t is None:
            continue
        s, v = t
        if v > best.get(s, -1):
            best[s] = v
    return [(s, v) for s, v in best.items()]


def build_nc(debug=False):
    nc = bass.Bass("TRN2", target_bir_lowering=False)
    T = _tables()
    S = Sched()
    stack = contextlib.ExitStack()

    def dram_in(name, shape, dt=F32):
        return nc.dram_tensor(name, list(shape), dt, kind="ExternalInput").ap()

    u_b = dram_in("u_b", [T_FULL, DM])
    u_own = dram_in("u_own", [T_OWN, DM])
    WqT = dram_in("WqT", [DM, DM]); WkT = dram_in("WkT", [DM, DM])
    WvT = dram_in("WvT", [DM, DM]); WoT = dram_in("WoT", [DM, DM])
    kker = dram_in("kker", [DM, 2048]); sker = dram_in("sker", [DM, 2048])
    Wr = dram_in("Wr", [DM, 8])
    W1 = dram_in("W1", [8, DM, 4096]); W2 = dram_in("W2", [8, 4096, DM])
    b2d = dram_in("b2d", [8, DM])
    bq_c = dram_in("bq_c", [128, 8]); bk_c = dram_in("bk_c", [128, 8]); bv_c = dram_in("bv_c", [128, 8])
    br_c = dram_in("br_c", [8, 1]); bo_rep = dram_in("bo_rep", [128, DM])
    b1cd = dram_in("b1cd", [128, 256])
    kd_pr = dram_in("kd_pr", [128, 512], BF16)
    d_pr = dram_in("d_pr", [128, 512], BF16)
    tabs = {k: dram_in("tab_" + k, list(v.shape), BF16 if v.dtype == BF else F32)
            for k, v in T.items()}
    i128c_own = dram_in("i128c_own", [128, 32], BF16)
    i128sn_own = dram_in("i128sn_own", [128, 32], BF16)

    scrkind = "ExternalOutput" if debug else "Internal"
    kscr = nc.dram_tensor("kscr", [DM, T_FULL], F32, kind=scrkind).ap()
    vscr = nc.dram_tensor("vscr", [DM, T_FULL], F32, kind=scrkind).ap()
    if debug:
        dbg_xn = nc.dram_tensor("dbg_xn", [8, 128, T_FULL], BF16, kind="ExternalOutput").ap()
        dbg_ycm = nc.dram_tensor("dbg_ycm", [8, 128, T_OWN], BF16, kind="ExternalOutput").ap()
        dbg_lg = nc.dram_tensor("dbg_lg", [8, 1024], F32, kind="ExternalOutput").ap()
        dbg_gate = nc.dram_tensor("dbg_gate", [128, 64], F32, kind="ExternalOutput").ap()
        dbg_yacc = nc.dram_tensor("dbg_yacc", [8, 128, DM], F32, kind="ExternalOutput").ap()
        dbg_xno = nc.dram_tensor("dbg_xno", [8, 128, 1024], BF16, kind="ExternalOutput").ap()
        dbg_kv = nc.dram_tensor("dbg_kv", [128, 2048], BF16, kind="ExternalOutput").ap()
        dbg_kf = nc.dram_tensor("dbg_kf", [128, 2048], BF16, kind="ExternalOutput").ap()
        dbg_sR = nc.dram_tensor("dbg_sR", [128, SPEC], BF16, kind="ExternalOutput").ap()
        dbg_prb = nc.dram_tensor("dbg_prb", [128, 8], F32, kind="ExternalOutput").ap()
        dbg_Sb = nc.dram_tensor("dbg_Sb", [128, 8], F32, kind="ExternalOutput").ap()
        dbg_kR = nc.dram_tensor("dbg_kR", [128, SPEC], BF16, kind="ExternalOutput").ap()
    out_own = nc.dram_tensor("out_own", [T_OWN, DM], F32, kind="ExternalOutput").ap()

    stack_ab = contextlib.ExitStack()

    def sb(name, shape, dt):
        return stack.enter_context(nc.sbuf_tensor(name, list(shape), dt))

    def sb_ab(name, shape, dt):
        return stack_ab.enter_context(nc.sbuf_tensor(name, list(shape), dt))

    tab_s = {k: sb("t_" + k, v.shape, BF16 if v.dtype == BF else F32) for k, v in T.items()}
    i128c_own_s = sb("i128co", [128, 32], BF16)
    i128sn_own_s = sb("i128so", [128, 32], BF16)
    bq_s = sb("bq_s", [128, 8], F32); bk_s = sb("bk_s", [128, 8], F32); bv_s = sb("bv_s", [128, 8], F32)
    kd_s = sb("kd_s", [128, 512], BF16); dd_s = sb("dd_s", [128, 512], BF16)
    ycm = [sb(f"ycm{i}", [128, T_OWN], BF16) for i in range(8)]
    tmp1 = sb("tmp1", [128, 1024], F32); tmp2 = sb("tmp2", [128, 1024], F32)
    ut = [sb(f"ut{i}", [128, DM], F32) for i in range(2)]
    usq = sb("usq", [128, DM], F32)
    st_s = sb("st_s", [128, 8], F32)
    xnt = sb("xnt", [128, DM], BF16)
    xn = [sb_ab(f"xn{i}", [128, T_FULL], BF16) for i in range(8)]
    cm_slab = sb_ab("cm_slab", [128, 1024], F32)
    wk_s = sb_ab("wk_s", [128, 8 * 128], BF16)
    wv_s = sb_ab("wv_s", [128, 8 * 128], BF16)
    kfft = sb_ab("kfft", [128, 2048], BF16)
    vfft = sb_ab("vfft", [128, 2048], BF16)
    kvt = sb_ab("kvt", [128, 2048], BF16)
    kkfft = sb_ab("kkfft", [128, 2048], BF16)
    Ar = sb_ab("Ar", [128, SPEC], BF16); Ai = sb_ab("Ai", [128, SPEC], BF16)
    Brt = sb_ab("Brt", [128, SPEC], BF16); Bit = sb_ab("Bit", [128, SPEC], BF16)
    Kr = sb_ab("Kr", [128, SPEC], BF16); Ki = sb_ab("Ki", [128, SPEC], BF16)
    y2s = sb_ab("y2s", [32, 4096], BF16)

    psP = [stack.enter_context(nc.psum_tensor(f"psP{i}", [128, 1024], F32)) for i in range(2)]
    psK = stack.enter_context(nc.psum_tensor("psK", [128, 1024], F32))
    psT = [stack.enter_context(nc.psum_tensor(f"psTa{i}", [128, 512], BF16)) for i in range(2)]
    psP = psP + [psK]
    psS = psK

    sems = {s: stack.enter_context(nc.semaphore(s))
            for s in ['pe', 'dve', 'act', 'pool', 'sp16', 'gd16']}

    ps_state = {id(p): {'w': None, 'r': []} for p in psP + [psS] + psT}
    sl_state = {}
    for x in [Ar, Ai, Brt, Bit, Kr, Ki, kfft, vfft, kvt, kkfft, y2s]:
        sl_state[id(x)] = {'w': [], 'r': []}

    def mm_group(ps, mms, waits):
        st = ps_state[id(ps)]
        w = list(waits) + _maxper(st['r']) + ([st['w']] if st['w'] else [])
        tok = None
        first = True
        for i, m in enumerate(mms):
            tok = S.op('pe', m, waits=(w if first else ()), inc=(i == len(mms) - 1))
            first = False
        st['w'] = tok
        st['r'] = []
        return tok

    def ps_read(ps, eng, fn, waits=()):
        st = ps_state[id(ps)]
        w = list(waits) + ([st['w']] if st['w'] else [])
        tok = S.op(eng, fn, waits=w)
        st['r'].append(tok)
        return tok

    def pe_transpose(tp, tfn, waits=()):
        st = ps_state[id(tp)]
        w = list(waits) + _maxper(st['r']) + ([st['w']] if st['w'] else [])
        tok = S.op('pe', tfn, waits=w)
        st['w'] = tok
        st['r'] = []
        return tok

    def slot_w(x, tok):
        sl_state[id(x)]['w'].append(tok)

    def slot_begin(x):
        st = sl_state[id(x)]
        w = _maxper(st['r'] + st['w'])
        st['r'] = []
        st['w'] = []
        return w

    def slot_r(x, tok):
        sl_state[id(x)]['r'].append(tok)

    def slot_wtok(x):
        return _maxper(sl_state[id(x)]['w'])

    for k in tab_s:
        tl = S.dma('sp', tab_s[k][:], tabs[k])
    tl = S.dma('sp', i128c_own_s[:], i128c_own)
    tl = S.dma('sp', i128sn_own_s[:], i128sn_own)
    tl = S.dma('sp', bq_s[:], bq_c)
    tl = S.dma('sp', bk_s[:], bk_c)
    tl = S.dma('sp', bv_s[:], bv_c)
    tl = S.dma('sp', kd_s[:], kd_pr)
    TABTOK = S.dma('sp', dd_s[:], d_pr)

    IDB = tab_s['ident_bf']
    IDF = tab_s['ident_f32']

    ln_guard = {'ut': [None, None], 'usq': None, 'sq': None, 'xnt': None}

    def do_ln(u_src, ntiles, xn_dst):
        last = None
        for i in range(ntiles):
            buf = ut[i % 2]
            g_ = ln_guard['ut'][i % 2]
            g_ = g_ if isinstance(g_, list) else [g_]
            tl_ = S.dma('sp', buf[:], u_src[i * 128:(i + 1) * 128, :],
                        waits=[TABTOK] + g_)
            a = S.op('dve', (lambda b: lambda h: h.reduce_sum(st_s[:, 0:1], b[:], axis=mybir.AxisListType.X))(buf), waits=[tl_])
            b_ = S.op('act', (lambda b: lambda h: h.square(usq[:], b[:]))(buf), waits=[tl_, ln_guard['usq']])
            c = S.op('dve', lambda h: h.reduce_sum(st_s[:, 1:2], usq[:], axis=mybir.AxisListType.X), waits=[b_])
            d = S.op('dve', lambda h: h.tensor_scalar(st_s[:, 2:3], st_s[:, 0:1], 1.0 / DM, None, ALU.mult), waits=[a])
            e = S.op('dve', lambda h: h.tensor_mul(st_s[:, 3:4], st_s[:, 2:3], st_s[:, 2:3]), waits=[d])
            f = S.op('dve', lambda h: h.tensor_scalar(st_s[:, 4:5], st_s[:, 1:2], 1.0 / DM, None, ALU.mult), waits=[c])
            g = S.op('dve', lambda h: h.tensor_sub(st_s[:, 4:5], st_s[:, 4:5], st_s[:, 3:4]), waits=[e, f])
            g2 = S.op('dve', lambda h: h.tensor_scalar(st_s[:, 4:5], st_s[:, 4:5], 1e-5, None, ALU.add), waits=[g])
            r = S.op('dve', lambda h: h.reciprocal(st_s[:, 6:7], st_s[:, 4:5]), waits=[g2, ln_guard['sq']])
            sq = S.op('act', lambda h: h.sqrt(st_s[:, 7:8], st_s[:, 6:7]), waits=[r])
            ln_guard['sq'] = sq
            xt = S.op('dve', (lambda b: lambda h: h.tensor_scalar(xnt[:], b[:], st_s[:, 2:3], st_s[:, 7:8], ALU.subtract, ALU.mult))(buf),
                      waits=[sq, ln_guard['xnt']])
            ln_guard['ut'][i % 2] = [xt, b_]
            ln_guard['usq'] = c
            last_tt = None
            for hc in range(8):
                tp = psT[hc % 2]
                tt = pe_transpose(tp, (lambda hc, tp: lambda h: h.transpose(
                    tp[:, 0:128], xnt[:, hc * 128:(hc + 1) * 128], IDB[:]))(hc, tp), waits=[xt])
                ev = ps_read(tp, 'act', (lambda hc, i, tp: lambda h: h.copy(
                    xn_dst[hc][:, i * 128:(i + 1) * 128], tp[:, 0:128]))(hc, i, tp))
                last = ev
                last_tt = tt
            ln_guard['xnt'] = last_tt
        return last

    xn_done = do_ln(u_b, 32, xn)

    if debug:
        for hc in range(8):
            S.dma('sp', dbg_xn[hc], xn[hc][:], waits=[xn_done])

    cm_guard = {'free': None, 'wk': None, 'wv': None}
    bounce_k = [None] * 8
    bounce_v = [None] * 8
    ycm_written = [None]

    kscr_v = kscr.rearrange("(K hf c) (m n) -> K hf m c n", K=NCHUNK, hf=2, n=N1)
    vscr_v = vscr.rearrange("(K hf c) (m n) -> K hf m c n", K=NCHUNK, hf=2, n=N1)
    kker_v = kker.rearrange("(K hf c) (m n) -> K hf m c n", K=NCHUNK, hf=2, n=N1)
    sker_v = sker.rearrange("(K hf c) (m n) -> K hf m c n", K=NCHUNK, hf=2, n=N1)

    def proj_cm(WT_d, wbuf, bias_col, scratch, pchunk, wfree):
        wv = WT_d.rearrange("(hc p) d -> p hc d", hc=8)[:, :, pchunk * 128:(pchunk + 1) * 128]
        wl = S.dma('pool', wbuf[:].rearrange("p (hc d) -> p hc d", hc=8), wv, waits=[wfree])
        bw = None
        for t5 in range(8):
            ps = psP[t5 % 2]
            mms = []
            for hc in range(8):
                mms.append((lambda hc, t5, ps: lambda h: h.matmul(
                    ps[:, 0:512], wbuf[:, hc * 128:(hc + 1) * 128],
                    xn[hc][:, t5 * 512:(t5 + 1) * 512],
                    start=(hc == 0), stop=(hc == 7)))(hc, t5, ps))
            mt = mm_group(ps, mms, waits=[wl, xn_done])
            ev = ps_read(ps, 'act', (lambda t5, ps: lambda h: h.activation(
                cm_slab[:, (t5 % 2) * 512:(t5 % 2) * 512 + 512], ps[:, 0:512], AF.Identity,
                bias=bias_col, scale=1.0))(t5, ps), waits=[cm_guard['free']])
            if t5 % 2 == 1:
                bw = S.dma('sp', scratch[pchunk * 128:(pchunk + 1) * 128,
                                         (t5 - 1) * 512:(t5 + 1) * 512], cm_slab[:], waits=[ev])
                cm_guard['free'] = bw
        return bw, None

    def s1tw(rhs_t, rhs_tok, nz, Bd_r, Bd_i, mode):
        wA = slot_begin(Bd_r) + slot_begin(Bd_i)
        tw_prev = [None]
        for hf in range(2):
            lhs_c = tab_s['d128c'][64 * hf:64 * hf + nz, :]
            lhs_s = tab_s['d128s'][64 * hf:64 * hf + nz, :]
            npieces = 2 if mode == 'P' else 4
            width = 1024 if mode == 'P' else 512
            for kc in range(npieces):
                col0 = 2048 * hf + kc * width
                if mode == 'P':
                    ps_re = psP[0]; ps_im = psP[1]
                    mre = []; mim = []
                    for half in range(2):
                        sl = slice(kc * 1024 + half * 512, kc * 1024 + half * 512 + 512)
                        ob = slice(half * 512, half * 512 + 512)
                        mre.append((lambda lhs_c, sl, ob, hf, nz: lambda h: h.matmul(
                            psP[0][:, ob], lhs_c, rhs_t[64 * hf:64 * hf + nz, sl],
                            start=True, stop=True))(lhs_c, sl, ob, hf, nz))
                        mim.append((lambda lhs_s, sl, ob, hf, nz: lambda h: h.matmul(
                            psP[1][:, ob], lhs_s, rhs_t[64 * hf:64 * hf + nz, sl],
                            start=True, stop=True))(lhs_s, sl, ob, hf, nz))
                    t_re = mm_group(ps_re, mre, waits=[rhs_tok] + wA)
                    t_im = mm_group(ps_im, mim, waits=[rhs_tok])
                    re_ap = psP[0][:]; im_ap = psP[1][:]
                    nch = 16
                else:
                    ps_re = psK; ps_im = psK
                    sl = slice(kc * 512, kc * 512 + 512)
                    mm2 = [(lambda lhs_c, sl, hf, nz: lambda h: h.matmul(
                        psK[:, 0:512], lhs_c, rhs_t[64 * hf:64 * hf + nz, sl],
                        start=True, stop=True))(lhs_c, sl, hf, nz),
                        (lambda lhs_s, sl, hf, nz: lambda h: h.matmul(
                            psK[:, 512:1024], lhs_s, rhs_t[64 * hf:64 * hf + nz, sl],
                            start=True, stop=True))(lhs_s, sl, hf, nz)]
                    t_re = mm_group(psK, mm2, waits=[rhs_tok] + wA)
                    t_im = t_re
                    re_ap = psK[:, 0:512]; im_ap = psK[:, 512:1024]
                    nch = 8
                slot_r(rhs_t, t_re); slot_r(rhs_t, t_im)
                wA = []
                pv_re = re_ap.rearrange("p (c n) -> p c n", n=N1)
                pv_im = im_ap.rearrange("p (c n) -> p c n", n=N1)
                twc_b = tab_s['twc'][:].unsqueeze(1).broadcast_to([128, nch, N1])
                tws_b = tab_s['tws'][:].unsqueeze(1).broadcast_to([128, nch, N1])
                w_ = width
                a1 = ps_read(ps_re, 'dve', (lambda pv_re, twc_b, w_: lambda h: h.tensor_tensor(
                    tmp1[:, 0:w_].rearrange("p (c n) -> p c n", n=N1), pv_re, twc_b, ALU.mult))(pv_re, twc_b, w_), waits=[t_re, tw_prev[0]])
                a2 = ps_read(ps_im, 'dve', (lambda pv_im, tws_b, w_: lambda h: h.tensor_tensor(
                    tmp2[:, 0:w_].rearrange("p (c n) -> p c n", n=N1), pv_im, tws_b, ALU.mult))(pv_im, tws_b, w_), waits=[t_im])
                a3 = S.op('dve', (lambda col0, w_: lambda h: h.tensor_sub(
                    Bd_r[:, col0:col0 + w_], tmp1[:, 0:w_], tmp2[:, 0:w_]))(col0, w_), waits=[a1, a2])
                slot_w(Bd_r, a3)
                a4 = ps_read(ps_re, 'dve', (lambda pv_re, tws_b, w_: lambda h: h.tensor_tensor(
                    tmp1[:, 0:w_].rearrange("p (c n) -> p c n", n=N1), pv_re, tws_b, ALU.mult))(pv_re, tws_b, w_), waits=[a3])
                a5 = ps_read(ps_im, 'dve', (lambda pv_im, twc_b, w_: lambda h: h.tensor_tensor(
                    tmp2[:, 0:w_].rearrange("p (c n) -> p c n", n=N1), pv_im, twc_b, ALU.mult))(pv_im, twc_b, w_), waits=[a3])
                a6 = S.op('dve', (lambda col0, w_: lambda h: h.tensor_add(
                    Bd_i[:, col0:col0 + w_], tmp1[:, 0:w_], tmp2[:, 0:w_]))(col0, w_), waits=[a4, a5])
                slot_w(Bd_i, a6)
                tw_prev[0] = a6

    def trans_batch(src_r, src_i):
        wB = slot_begin(Brt) + slot_begin(Bit)
        for plane in range(2):
            src = src_r if plane == 0 else src_i
            dst = Brt if plane == 0 else Bit
            for g in range(8):
                tp = psT[plane]
                tts = None
                first = True
                for j in range(4):
                    pr = g * 4 + j
                    tfn = (lambda src, pr, tp, j: lambda h: h.transpose(
                        tp[:, j * 128:(j + 1) * 128], src[:, pr * 128:(pr + 1) * 128], IDB[:]))(src, pr, tp, j)
                    if first:
                        tts = pe_transpose(tp, tfn, waits=slot_wtok(src) + wB)
                        wB = []
                        first = False
                    else:
                        st = ps_state[id(tp)]
                        tts = S.op('pe', tfn)
                        st['w'] = tts
                slot_r(src, tts)
                ev = ps_read(tp, 'act', (lambda dst, g, tp: lambda h: h.copy(
                    dst[:, g * 512:(g + 1) * 512], tp[:, 0:512]))(dst, g, tp))
                slot_w(dst, ev)

    def s2f(Rdst, Idst, fold_tab, mode, ck, tabc='d64c', tabs_='d64s', tabsn='d64sn'):
        wR = slot_begin(Rdst) + slot_begin(Idst)
        npieces = 4 if mode == 'P' else 8
        width = 1024 if mode == 'P' else 512
        for piece in range(npieces):
            col0 = piece * width
            if mode == 'P':
                ps_re = psP[0]; ps_im = psP[1]
                re_ap = psP[0][:]; im_ap = psP[1][:]
            else:
                ps_re = psK; ps_im = psK
                re_ap = psK[:, 0:512]; im_ap = psK[:, 512:1024]
            mre = []; mim = []
            for par in range(2):
                rb = slice(64 * par, 64 * par + 64)
                tp_pos = (64 * par, 64 * par)
                nsub = 2 if mode == 'P' else 1
                for half in range(nsub):
                    s2_ = slice(col0 + half * 512, col0 + half * 512 + 512)
                    if mode == 'P':
                        ob = slice(half * 512, half * 512 + 512)
                        mre += [
                            (lambda ob, s2_, tp_pos, rb: lambda h: h.matmul(
                                psP[0][tp_pos[1]:tp_pos[1] + 64, ob], tab_s[tabc][rb, :], Brt[rb, s2_],
                                start=True, stop=False, tile_position=tp_pos))(ob, s2_, tp_pos, rb),
                            (lambda ob, s2_, tp_pos, rb: lambda h: h.matmul(
                                psP[0][tp_pos[1]:tp_pos[1] + 64, ob], tab_s[tabsn][rb, :], Bit[rb, s2_],
                                start=False, stop=True, tile_position=tp_pos))(ob, s2_, tp_pos, rb)]
                        mim += [
                            (lambda ob, s2_, tp_pos, rb: lambda h: h.matmul(
                                psP[1][tp_pos[1]:tp_pos[1] + 64, ob], tab_s[tabs_][rb, :], Brt[rb, s2_],
                                start=True, stop=False, tile_position=tp_pos))(ob, s2_, tp_pos, rb),
                            (lambda ob, s2_, tp_pos, rb: lambda h: h.matmul(
                                psP[1][tp_pos[1]:tp_pos[1] + 64, ob], tab_s[tabc][rb, :], Bit[rb, s2_],
                                start=False, stop=True, tile_position=tp_pos))(ob, s2_, tp_pos, rb)]
                    else:
                        mre += [
                            (lambda s2_, tp_pos, rb: lambda h: h.matmul(
                                psK[tp_pos[1]:tp_pos[1] + 64, 0:512], tab_s[tabc][rb.start:rb.stop, :], Brt[rb, s2_],
                                start=True, stop=False, tile_position=tp_pos))(s2_, tp_pos, rb),
                            (lambda s2_, tp_pos, rb: lambda h: h.matmul(
                                psK[tp_pos[1]:tp_pos[1] + 64, 0:512], tab_s[tabsn][rb.start:rb.stop, :], Bit[rb, s2_],
                                start=False, stop=True, tile_position=tp_pos))(s2_, tp_pos, rb)]
                        mim += [
                            (lambda s2_, tp_pos, rb: lambda h: h.matmul(
                                psK[tp_pos[1]:tp_pos[1] + 64, 512:1024], tab_s[tabs_][rb.start:rb.stop, :], Brt[rb, s2_],
                                start=True, stop=False, tile_position=tp_pos))(s2_, tp_pos, rb),
                            (lambda s2_, tp_pos, rb: lambda h: h.matmul(
                                psK[tp_pos[1]:tp_pos[1] + 64, 512:1024], tab_s[tabc][rb.start:rb.stop, :], Bit[rb, s2_],
                                start=False, stop=True, tile_position=tp_pos))(s2_, tp_pos, rb)]
            wdep = slot_wtok(Brt) + slot_wtok(Bit) + wR
            wR = []
            if mode == 'P':
                t_re = mm_group(ps_re, mre, waits=wdep)
                t_im = mm_group(ps_im, mim, waits=wdep)
            else:
                t_re = mm_group(psK, mre + mim, waits=wdep)
                t_im = t_re
            slot_r(Brt, t_im); slot_r(Bit, t_im)
            e1 = ps_read(ps_re, 'act', (lambda col0, re_ap, width: lambda h: h.copy(
                Rdst[:, col0:col0 + width], re_ap))(col0, re_ap, width))
            e2 = ps_read(ps_im, 'act', (lambda col0, im_ap, width: lambda h: h.copy(
                Idst[:, col0:col0 + width], im_ap))(col0, im_ap, width))
            slot_w(Rdst, e1); slot_w(Idst, e2)
            if fold_tab is not None:
                kdv = fold_tab[:, ck * 32:ck * 32 + 32]
                kdb = kdv.unsqueeze(2).broadcast_to([128, 32, 128])
                dst3 = Rdst[:, col0:col0 + width].rearrange("p (a k) -> p a k", k=128)
                a0 = col0 // 128
                kdb3 = kdb[:, a0:a0 + width // 128, :]
                e3 = S.op('dve', (lambda dst3, kdb3: lambda h: h.tensor_tensor(
                    dst3, dst3, kdb3, ALU.add))(dst3, kdb3), waits=[e1])
                slot_w(Rdst, e3)

    def inv_s1tw():
        wA = slot_begin(Ar) + slot_begin(Ai)
        pw = slot_wtok(Brt) + slot_wtok(Bit)
        q_prev = [None]
        for piece in range(4):
            ps_re = psP[0]; ps_im = psP[1]
            mre = []; mim = []
            for par in range(2):
                rb = slice(64 * par, 64 * par + 64)
                tp_pos = (64 * par, 64 * par)
                for half in range(2):
                    s2_ = slice(piece * 1024 + half * 512, piece * 1024 + half * 512 + 512)
                    ob = slice(half * 512, half * 512 + 512)
                    mre.append((lambda rb, s2_, ob, tp_pos: lambda h: h.matmul(
                        psP[0][tp_pos[1]:tp_pos[1] + 64, ob], tab_s['i64c'][rb, :], Brt[rb, s2_],
                        start=True, stop=False, tile_position=tp_pos))(rb, s2_, ob, tp_pos))
                    mre.append((lambda rb, s2_, ob, tp_pos: lambda h: h.matmul(
                        psP[0][tp_pos[1]:tp_pos[1] + 64, ob], tab_s['i64sn'][rb, :], Bit[rb, s2_],
                        start=False, stop=True, tile_position=tp_pos))(rb, s2_, ob, tp_pos))
                    mim.append((lambda rb, s2_, ob, tp_pos: lambda h: h.matmul(
                        psP[1][tp_pos[1]:tp_pos[1] + 64, ob], tab_s['i64s'][rb, :], Brt[rb, s2_],
                        start=True, stop=False, tile_position=tp_pos))(rb, s2_, ob, tp_pos))
                    mim.append((lambda rb, s2_, ob, tp_pos: lambda h: h.matmul(
                        psP[1][tp_pos[1]:tp_pos[1] + 64, ob], tab_s['i64c'][rb, :], Bit[rb, s2_],
                        start=False, stop=True, tile_position=tp_pos))(rb, s2_, ob, tp_pos))
            wdep = pw + wA
            wA = []; pw = []
            t_re = mm_group(ps_re, mre, waits=wdep)
            t_im = mm_group(ps_im, mim, waits=wdep)
            slot_r(Brt, t_im); slot_r(Bit, t_im)
            col0 = piece * 1024
            pv_re = psP[0][:].rearrange("p (a k) -> p a k", k=128)
            pv_im = psP[1][:].rearrange("p (a k) -> p a k", k=128)
            itc_b = tab_s['itwc'][:].unsqueeze(1).broadcast_to([128, 8, 128])
            its_b = tab_s['itws'][:].unsqueeze(1).broadcast_to([128, 8, 128])
            q1 = ps_read(ps_re, 'dve', (lambda pv_re, itc_b: lambda h: h.tensor_tensor(
                tmp1[:].rearrange("p (a k) -> p a k", k=128), pv_re, itc_b, ALU.mult))(pv_re, itc_b), waits=[t_re, q_prev[0]])
            q2 = ps_read(ps_im, 'dve', (lambda pv_im, its_b: lambda h: h.tensor_tensor(
                tmp2[:].rearrange("p (a k) -> p a k", k=128), pv_im, its_b, ALU.mult))(pv_im, its_b), waits=[t_im])
            q3 = S.op('dve', (lambda col0: lambda h: h.tensor_sub(
                Ar[:, col0:col0 + 1024], tmp1[:], tmp2[:]))(col0), waits=[q1, q2])
            slot_w(Ar, q3)
            q4 = ps_read(ps_re, 'dve', (lambda pv_re, its_b: lambda h: h.tensor_tensor(
                tmp1[:].rearrange("p (a k) -> p a k", k=128), pv_re, its_b, ALU.mult))(pv_re, its_b), waits=[q3])
            q5 = ps_read(ps_im, 'dve', (lambda pv_im, itc_b: lambda h: h.tensor_tensor(
                tmp2[:].rearrange("p (a k) -> p a k", k=128), pv_im, itc_b, ALU.mult))(pv_im, itc_b), waits=[q3])
            q6 = S.op('dve', (lambda col0: lambda h: h.tensor_add(
                Ai[:, col0:col0 + 1024], tmp1[:], tmp2[:]))(col0), waits=[q4, q5])
            slot_w(Ai, q6)
            q_prev[0] = q6

    def inv_s2(out_kind):
        if out_kind == 'kv':
            wkv = slot_begin(kvt)
            for pg in range(2):
                po = 64 * pg
                ps = psP[pg]
                for halfc in range(2):
                    lc = halfc * 1024
                    mm2 = []
                    for sub in range(2):
                        s3 = slice(pg * 2048 + halfc * 1024 + sub * 512,
                                   pg * 2048 + halfc * 1024 + sub * 512 + 512)
                        ob = slice(sub * 512, sub * 512 + 512)
                        mm2.append((lambda s3, ob, po, ps: lambda h: h.matmul(
                            ps[po:po + 64, ob], tab_s['i128c_full'][:], Brt[:, s3],
                            start=True, stop=False, tile_position=(0, po)))(s3, ob, po, ps))
                        mm2.append((lambda s3, ob, po, ps: lambda h: h.matmul(
                            ps[po:po + 64, ob], tab_s['i128sn_full'][:], Bit[:, s3],
                            start=False, stop=True, tile_position=(0, po)))(s3, ob, po, ps))
                    wdep = slot_wtok(Brt) + slot_wtok(Bit) + wkv + slot_wtok(vfft)
                    wkv = []
                    tmm = mm_group(ps, mm2, waits=wdep)
                    slot_r(Brt, tmm); slot_r(Bit, tmm)
                    ev = ps_read(ps, 'dve', (lambda ps, po, lc: lambda h: h.tensor_mul(
                        kvt[po:po + 64, lc:lc + 1024], ps[po:po + 64, 0:1024],
                        vfft[po:po + 64, lc:lc + 1024]))(ps, po, lc))
                    slot_r(vfft, ev)
                    slot_w(kvt, ev)
            return
        wy = slot_begin(y2s)
        for piece in range(4):
            sl1024 = slice(piece * 1024, piece * 1024 + 1024)
            ps = psP[piece % 2]
            mm2 = []
            for sub in range(2):
                s3 = slice(piece * 1024 + sub * 512, piece * 1024 + sub * 512 + 512)
                ob = slice(sub * 512, sub * 512 + 512)
                mm2.append((lambda s3, ob, ps: lambda h: h.matmul(
                    ps[0:32, ob], i128c_own_s[:], Brt[:, s3], start=True, stop=False))(s3, ob, ps))
                mm2.append((lambda s3, ob, ps: lambda h: h.matmul(
                    ps[0:32, ob], i128sn_own_s[:], Bit[:, s3], start=False, stop=True))(s3, ob, ps))
            wdep = slot_wtok(Brt) + slot_wtok(Bit) + wy
            wy = []
            tmm = mm_group(ps, mm2, waits=wdep)
            slot_r(Brt, tmm); slot_r(Bit, tmm)
            ev = ps_read(ps, 'dve', (lambda ps, sl1024: lambda h: h.tensor_copy(
                y2s[:, sl1024], ps[0:32, 0:1024]))(ps, sl1024))
            slot_w(y2s, ev)

    def product():
        wB = slot_begin(Brt) + slot_begin(Bit)
        deps = slot_wtok(Ar) + slot_wtok(Ai) + slot_wtok(Kr) + slot_wtok(Ki)
        pprev = None
        for piece in range(4):
            sl = slice(piece * 1024, piece * 1024 + 1024)
            p1 = S.op('dve', (lambda sl: lambda h: h.tensor_mul(tmp1[:], Ar[:, sl], Kr[:, sl]))(sl),
                      waits=deps + wB + [pprev])
            deps = []; wB = []
            p2 = S.op('dve', (lambda sl: lambda h: h.tensor_mul(tmp2[:], Ai[:, sl], Ki[:, sl]))(sl))
            p3 = S.op('dve', (lambda sl: lambda h: h.tensor_sub(Brt[:, sl], tmp1[:], tmp2[:]))(sl),
                      waits=[p1, p2])
            p4 = S.op('dve', (lambda sl: lambda h: h.tensor_mul(tmp1[:], Ar[:, sl], Ki[:, sl]))(sl),
                      waits=[p3])
            p5 = S.op('dve', (lambda sl: lambda h: h.tensor_mul(tmp2[:], Ai[:, sl], Kr[:, sl]))(sl),
                      waits=[p3])
            p6 = S.op('dve', (lambda sl: lambda h: h.tensor_add(Bit[:, sl], tmp1[:], tmp2[:]))(sl),
                      waits=[p4, p5])
            for x in (Ar, Ai, Kr, Ki):
                slot_r(x, p6)
            slot_w(Brt, p3); slot_w(Bit, p6)
            pprev = p6

    def conv_pass(ck, sig_t, sig_tok, ker_view, fold_tab, out_kind):
        wkk = slot_begin(kkfft)
        kl = None
        for hf in range(2):
            kl = S.dma('pool', kkfft[64 * hf:64 * hf + 32, :].rearrange("m (c n) -> m c n", n=N1),
                       ker_view[ck, hf], waits=wkk)
            wkk = []
        slot_w(kkfft, kl)
        s1tw(kkfft, kl, 32, Kr, Ki, 'K')
        s1tw(sig_t, sig_tok, 64, Ar, Ai, 'P')
        trans_batch(Kr, Ki)
        s2f(Kr, Ki, fold_tab, 'K', ck)
        trans_batch(Ar, Ai)
        s2f(Ar, Ai, None, 'P', ck)
        product()
        inv_s1tw()
        trans_batch(Ar, Ai)
        inv_s2(out_kind)

    def do_proj(pc):
        bounce_k[pc], _ = proj_cm(WkT, wk_s, bk_s[:, pc:pc + 1], kscr, pc, cm_guard['wk'])
        cm_guard['wk'] = bounce_k[pc]
        bounce_v[pc], _ = proj_cm(WvT, wv_s, bv_s[:, pc:pc + 1], vscr, pc, cm_guard['wv'])
        cm_guard['wv'] = bounce_v[pc]

    do_proj(0)
    for ck in range(NCHUNK):
        pchunk = ck // 2
        cchunk = ck // 2
        if ck % 2 == 0 and pchunk + 1 < 8:
            do_proj(pchunk + 1)
        wk0 = slot_begin(kfft)
        for hf in range(2):
            krt = S.dma('pool', kfft[64 * hf:64 * hf + 64, :].rearrange("m (c n) -> m c n", n=N1),
                        kscr_v[ck, hf], waits=[bounce_k[pchunk]] + wk0)
            wk0 = []
        slot_w(kfft, krt)
        wv0 = slot_begin(vfft)
        for hf in range(2):
            vrt = S.dma('pool', vfft[64 * hf:64 * hf + 64, :].rearrange("m (c n) -> m c n", n=N1),
                        vscr_v[ck, hf], waits=[bounce_v[pchunk]] + wv0)
            wv0 = []
        slot_w(vfft, vrt)
        conv_pass(ck, kfft, krt, kker_v, kd_s, 'kv')
        if debug and ck == 0:
            slot_r(kfft, S.dma('sp', dbg_kf, kfft[:], waits=slot_wtok(kfft)))
            slot_r(kvt, S.dma('sp', dbg_kv, kvt[:], waits=slot_wtok(kvt)))
            slot_r(Ar, S.dma('sp', dbg_sR, Ar[:], waits=slot_wtok(Ar)))
            slot_r(Kr, S.dma('sp', dbg_kR, Kr[:], waits=slot_wtok(Kr)))
        kv_tok = slot_wtok(kvt)
        conv_pass(ck, kvt, kv_tok[0] if kv_tok else None, sker_v, dd_s, 'y')
        ytoks = slot_wtok(y2s)
        cb = (ck * CH) % 128
        y3 = y2s[:].rearrange("p (a b n) -> p a b n", a=32, b=2)
        for n1g in range(16):
            tp = psT[n1g % 2]
            tts = None
            for j in range(4):
                n1v = n1g * 4 + j
                tfn = (lambda n1v, j, tp, cb: lambda h: h.transpose(
                    tp[cb:cb + 64, j * 32:(j + 1) * 32],
                    y3[:, :, :, n1v].rearrange("p a b -> p (a b)"),
                    IDB[0:32, 0:32], tile_position=(0, cb)))(n1v, j, tp, cb)
                if j == 0:
                    tts = pe_transpose(tp, tfn, waits=ytoks)
                else:
                    st = ps_state[id(tp)]
                    tts = S.op('pe', tfn)
                    st['w'] = tts
            slot_r(y2s, tts)
            dst = ycm[cchunk][cb:cb + 64, :].rearrange("p (m n) -> p m n", n=N1)
            def evfn(h, dst=dst, tp=tp, cb=cb, n1g=n1g):
                o = dst[:, :, n1g * 4:(n1g + 1) * 4].rearrange("p m n -> p n m")
                return h.tensor_copy(o, tp[cb:cb + 64, 0:128].rearrange("p (j m) -> p j m", j=4))
            ev = ps_read(tp, 'dve', evfn)
            ycm_written[0] = ev

    if debug:
        for hc in range(8):
            S.dma('sp', dbg_ycm[hc], ycm[hc][:], waits=[ycm_written[0]])

    S.barrier()
    stack_ab.close()
    xn_own = [sb(f"xno{i}", [128, 1024], BF16) for i in range(8)]
    yacc = [sb(f"yacc{i}", [128, DM], F32) for i in range(8)]
    w1s = [sb(f"w1s{i}", [128, 8 * 512], BF16) for i in range(2)]
    w2s = [sb(f"w2s{i}", [128, 4 * DM], BF16) for i in range(2)]
    hbuf = [sb(f"hbuf{i}", [128, 1024], BF16) for i in range(8)]
    wot_s = [sb(f"wot{i}", [128, DM], BF16) for i in range(8)]
    wq_s = sb("wq_s", [128, 8 * 128], BF16)
    wr_s = sb("wr_s", [128, 8 * 8], BF16)
    b2_s = sb("b2_s", [8, DM], BF16)
    b1c_s = sb("b1c_s", [128, 256], F32)
    bo_s = sb("bo_s", [128, DM], F32)
    br_s = sb("br_s", [8, 1], F32)
    lg_em = sb("lg_em", [8, 1024], F32)
    lg_tm = sb("lg_tm", [128, 8], F32)
    prb = sb("prb", [128, 8], F32)
    Sb = sb("Sb", [128, 8], F32)
    tmp8 = sb("tmp8", [128, 8], F32)
    gate_tm = sb("gate_tm", [128, 8 * 8], F32)
    gate_cm = sb("gate_cm", [8, 1024], BF16)
    outt = [sb(f"outt{i}", [128, DM], F32) for i in range(2)]
    uot = [sb(f"uot{i}", [128, DM], F32) for i in range(2)]

    tC = S.dma('pool', wr_s[:].rearrange("p (hc e) -> p hc e", hc=8),
               Wr.rearrange("(hc p) e -> p hc e", hc=8))
    tC = S.dma('pool', b2_s[:], b2d)
    for hc in range(8):
        tC = S.dma('pool', wot_s[hc][:], WoT.rearrange("(hc p) o -> hc p o", hc=8)[hc])
    tCp = tC
    tC2 = S.dma('sp', b1c_s[:], b1cd)
    tC2 = S.dma('sp', bo_s[:], bo_rep)
    tC2 = S.dma('sp', br_s[:], br_c)
    tCs = tC2

    STORE_TOKS = []
    mm2_prev = None

    for half in range(2):
        hb = half * 1024
        xo_done = do_ln(u_own[half * 1024:(half + 1) * 1024, :], 8, xn_own)
        lg_toks = []
        for pc in range(2):
            mmsp = []
            for hc in range(8):
                mmsp.append((lambda pc, hc: lambda h: h.matmul(
                    psS[0:8, 0:512], wr_s[:, hc * 8:(hc + 1) * 8],
                    xn_own[hc][:, pc * 512:(pc + 1) * 512],
                    start=(hc == 0), stop=(hc == 7)))(pc, hc))
            tmm = mm_group(psS, mmsp, waits=[xo_done, tCp, tCs])
            ev = ps_read(psS, 'act', (lambda pc: lambda h: h.activation(
                lg_em[:, pc * 512:(pc + 1) * 512], psS[0:8, 0:512], AF.Identity,
                bias=br_s[:], scale=1.0))(pc))
            lg_toks.append(ev)
        gate_done = []
        prev_topp = None
        for ti in range(8):
            ps = psP[2]
            tt = pe_transpose(ps, (lambda ti, ps: lambda h: h.transpose(
                ps[0:128, 0:8], lg_em[:, ti * 128:(ti + 1) * 128], IDF[0:8, 0:8]))(ti, ps),
                waits=lg_toks + [prev_topp])
            c1 = ps_read(ps, 'dve', (lambda ps: lambda h: h.tensor_copy(lg_tm[:], ps[0:128, 0:8]))(ps))
            m1 = S.op('dve', lambda h: h.reduce_max(st_s[:, 0:1], lg_tm[:], axis=mybir.AxisListType.X), waits=[c1])
            m2 = S.op('dve', lambda h: h.tensor_scalar(lg_tm[:], lg_tm[:], st_s[:, 0:1], None, ALU.subtract), waits=[m1])
            m3 = S.op('act', lambda h: h.activation(prb[:], lg_tm[:], AF.Exp), waits=[m2])
            m4 = S.op('dve', lambda h: h.reduce_sum(st_s[:, 1:2], prb[:], axis=mybir.AxisListType.X), waits=[m3])
            m5 = S.op('dve', lambda h: h.reciprocal(st_s[:, 2:3], st_s[:, 1:2]), waits=[m4])
            m6 = S.op('dve', lambda h: h.tensor_scalar(prb[:], prb[:], st_s[:, 2:3], None, ALU.mult), waits=[m5])
            last = m6
            for j in range(8):
                j1 = S.op('dve', (lambda j: lambda h: h.tensor_scalar(tmp8[:], prb[:], prb[:, j:j + 1], None, ALU.is_ge))(j), waits=[last])
                j2 = S.op('dve', lambda h: h.tensor_mul(tmp8[:], tmp8[:], prb[:]), waits=[j1])
                last = S.op('dve', (lambda j: lambda h: h.reduce_sum(Sb[:, j:j + 1], tmp8[:], axis=mybir.AxisListType.X))(j), waits=[j2])
            k1 = S.op('dve', lambda h: h.tensor_scalar(tmp8[:], Sb[:], 0.8, None, ALU.is_lt), waits=[last])
            k2 = S.op('dve', lambda h: h.tensor_tensor(Sb[:], Sb[:], prb[:], ALU.is_equal), waits=[k1])
            k3 = S.op('dve', lambda h: h.tensor_max(tmp8[:], tmp8[:], Sb[:]), waits=[k2])
            k4 = S.op('dve', (lambda ti: lambda h: h.tensor_mul(
                gate_tm[:, ti * 8:(ti + 1) * 8], tmp8[:], prb[:]))(ti), waits=[k3])
            prev_topp = k4
            tt2 = pe_transpose(ps, (lambda ti, ps: lambda h: h.transpose(
                ps[0:8, 0:128], gate_tm[:, ti * 8:(ti + 1) * 8], IDF[:]))(ti, ps), waits=[k4])
            ev2 = ps_read(ps, 'dve', (lambda ti, ps: lambda h: h.tensor_copy(
                gate_cm[:, ti * 128:(ti + 1) * 128], ps[0:8, 0:128]))(ti, ps))
            gate_done.append(ev2)
        yacc_last = [None] * 8
        yinit = []
        for ti in range(8):
            ps = psP[0]
            mmb = []
            for oh in range(2):
                mmb.append((lambda ti, oh, ps: lambda h: h.matmul(
                    ps[:, oh * 512:(oh + 1) * 512], gate_cm[:, ti * 128:(ti + 1) * 128],
                    b2_s[:, oh * 512:(oh + 1) * 512], start=True, stop=True))(ti, oh, ps))
            tmm = mm_group(ps, mmb, waits=[gate_done[ti], tCp, tCs])
            ev = ps_read(ps, 'dve', (lambda ti, ps: lambda h: h.tensor_add(
                yacc[ti][:], ps[:], bo_s[:]))(ti, ps), waits=[yacc_last[ti]])
            yacc_last[ti] = ev
            yinit.append(ev)
        g_done = []
        prev_q = None
        for dc in range(8):
            wl = S.dma('pool', wq_s[:].rearrange("p (hc d) -> p hc d", hc=8),
                       WqT.rearrange("(hc p) d -> p hc d", hc=8)[:, :, dc * 128:(dc + 1) * 128],
                       waits=[prev_q])
            for pc in range(2):
                ps = psP[pc % 2]
                mmq = []
                for hc in range(8):
                    mmq.append((lambda pc, hc, ps: lambda h: h.matmul(
                        ps[:, 0:512], wq_s[:, hc * 128:(hc + 1) * 128],
                        xn_own[hc][:, pc * 512:(pc + 1) * 512],
                        start=(hc == 0), stop=(hc == 7)))(pc, hc, ps))
                tmm = mm_group(ps, mmq, waits=[wl, xo_done, ycm_written[0]])
                ev = ps_read(ps, 'dve', (lambda dc, pc, ps, hb: lambda h: h.scalar_tensor_tensor(
                    ycm[dc][:, hb + pc * 512: hb + pc * 512 + 512], ps[:, 0:512],
                    bq_s[:, dc:dc + 1],
                    ycm[dc][:, hb + pc * 512: hb + pc * 512 + 512],
                    ALU.add, ALU.mult))(dc, pc, ps, hb))
                g_done.append(ev)
                prev_q = ev
        g_last = g_done[-1]
        w_use = {'w1': [None, None], 'w2': [None, None]}
        ld = {}

        def issue_w1(gfs, e, fs):
            pb = gfs % 2
            lt = S.dma('pool', w1s[pb][:].rearrange("p (hc f) -> p hc f", hc=8),
                       W1[e].rearrange("(hc p) f -> p hc f", hc=8)[:, :, fs * 512:(fs + 1) * 512],
                       waits=[w_use['w1'][pb]])
            ld[('w1', gfs)] = lt

        def issue_w2(gfs, e, fs):
            pb = gfs % 2
            lt = S.dma('pool', w2s[pb][:].rearrange("p (fc o) -> p fc o", fc=4),
                       W2[e].rearrange("(fc p) o -> p fc o", fc=32)[:, fs * 4:(fs + 1) * 4, :],
                       waits=[w_use['w2'][pb]])
            ld[('w2', gfs)] = lt

        issue_w1(0, 0, 0); issue_w2(0, 0, 0)
        issue_w1(1, 0, 1); issue_w2(1, 0, 1)
        h_done = [None] * 8
        for e in range(8):
            for pairi in range(4):
                for sub in range(2):
                    fs = pairi * 2 + sub
                    gfs = e * 8 + fs
                    pb = gfs % 2
                    lt1 = ld[('w1', gfs)]
                    mm1_last = None
                    for fc in range(4):
                        psm = psP[fc % 2]
                        mms = []
                        for hc in range(8):
                            for tq in range(2):
                                mms.append((lambda fc, hc, tq, psm, pb: lambda h: h.matmul(
                                    psm[:, tq * 512:(tq + 1) * 512],
                                    w1s[pb][:, hc * 512 + fc * 128: hc * 512 + fc * 128 + 128],
                                    xn_own[hc][:, tq * 512:(tq + 1) * 512],
                                    start=(hc == 0), stop=(hc == 7)))(fc, hc, tq, psm, pb))
                        tmm = mm_group(psm, mms, waits=[lt1, xo_done])
                        mm1_last = tmm
                        bcol = b1c_s[:, e * 32 + fs * 4 + fc: e * 32 + fs * 4 + fc + 1]
                        hidx = sub * 4 + fc
                        ev = ps_read(psm, 'dve', (lambda psm, bcol, hidx: lambda h: h.tensor_scalar(
                            hbuf[hidx][:], psm[:], bcol, 0.0, ALU.add, ALU.max))(psm, bcol, hidx))
                        h_done[hidx] = ev
                    w_use['w1'][pb] = mm1_last
                    if gfs + 2 < 64:
                        e2_, fs2 = divmod(gfs + 2, 8)
                        issue_w1(gfs + 2, e2_, fs2)
                mm2_tok = None
                for tt_ in range(8):
                    psm2 = psP[2]
                    mms2 = []
                    for fc8 in range(8):
                        sub2 = fc8 // 4; fc = fc8 % 4
                        pb2 = (e * 8 + pairi * 2 + sub2) % 2
                        for oh in range(2):
                            mms2.append((lambda fc8, fc, sub2, pb2, oh, tt_, psm2: lambda h: h.matmul(
                                psm2[:, oh * 512:(oh + 1) * 512],
                                hbuf[sub2 * 4 + fc][:, tt_ * 128:(tt_ + 1) * 128],
                                w2s[pb2][:, fc * 1024 + oh * 512: fc * 1024 + oh * 512 + 512],
                                start=(fc8 == 0), stop=(fc8 == 7)))(fc8, fc, sub2, pb2, oh, tt_, psm2))
                    wdep = _maxper(h_done) + [ld[('w2', e * 8 + pairi * 2)], ld[('w2', e * 8 + pairi * 2 + 1)]]
                    tmm2 = mm_group(psm2, mms2, waits=wdep)
                    mm2_tok = tmm2
                    gcol = gate_tm[:, tt_ * 8 + e: tt_ * 8 + e + 1]
                    ev2 = ps_read(psm2, 'dve', (lambda psm2, gcol, tt_: lambda h: h.scalar_tensor_tensor(
                        yacc[tt_][:], psm2[:], gcol, yacc[tt_][:], ALU.mult, ALU.add))(psm2, gcol, tt_),
                        waits=[yacc_last[tt_], gate_done[tt_]])
                    yacc_last[tt_] = ev2
                    mm2_prev = ev2
                for q_ in range(2):
                    gfs = e * 8 + pairi * 2 + q_
                    w_use['w2'][gfs % 2] = mm2_tok
                    if gfs + 2 < 64:
                        e2_, fs2 = divmod(gfs + 2, 8)
                        issue_w2(gfs + 2, e2_, fs2)
        if debug and half == 0:
            S.dma('sp', dbg_prb, prb[:], waits=[gate_done[-1]])
            S.dma('sp', dbg_Sb, Sb[:], waits=[gate_done[-1]])
            for hc in range(8):
                S.dma('sp', dbg_xno[hc], xn_own[hc][:], waits=[xo_done])
            S.dma('sp', dbg_lg, lg_em[:], waits=lg_toks)
            S.dma('sp', dbg_gate, gate_tm[:], waits=[gate_done[-1]])
            for ti in range(8):
                S.dma('sp', dbg_yacc[ti], yacc[ti][:], waits=[mm2_prev])
        for ti in range(8):
            ub = uot[ti % 2]
            ul = S.dma('sp', ub[:], u_own[half * 1024 + ti * 128: half * 1024 + (ti + 1) * 128, :],
                       waits=[STORE_TOKS[-2] if len(STORE_TOKS) >= 2 else None])
            psm = psP[ti % 2]
            mmo = []
            for chc in range(8):
                for oh in range(2):
                    mmo.append((lambda chc, oh, ti, psm, hb: lambda h: h.matmul(
                        psm[:, oh * 512:(oh + 1) * 512],
                        ycm[chc][:, hb + ti * 128: hb + ti * 128 + 128],
                        wot_s[chc][:, oh * 512:(oh + 1) * 512],
                        start=(chc == 0), stop=(chc == 7)))(chc, oh, ti, psm, hb))
            tmm = mm_group(psm, mmo, waits=[g_last, tCp, mm2_prev])
            ob = outt[ti % 2]
            e1 = ps_read(psm, 'dve', (lambda psm, ti, ob: lambda h: h.tensor_add(
                ob[:], psm[:], yacc[ti][:]))(psm, ti, ob),
                waits=[yacc_last[ti], STORE_TOKS[-2] if len(STORE_TOKS) >= 2 else None])
            e2 = S.op('dve', (lambda ob, ub: lambda h: h.tensor_add(ob[:], ob[:], ub[:]))(ob, ub),
                      waits=[e1, ul])
            stt = S.dma('sp', out_own[half * 1024 + ti * 128: half * 1024 + (ti + 1) * 128, :],
                        ob[:], waits=[e2])
            STORE_TOKS.append(stt)

    def make_runner(name):
        def run(h):
            for kind, waits, payload, tok in S.q[name]:
                for (s, v) in waits:
                    h.wait_ge(sems[s], v)
                if kind == 'op':
                    r = payload(h)
                    if tok is not None:
                        r.then_inc(sems[tok[0]], 1)
                else:
                    out_ap, in_ap = payload
                    h.dma_start(out_ap, in_ap).then_inc(sems[tok[0]], 16)
        return run

    final_sp = S.cnt['sp16']
    with nc.Block() as block:
        @block.tensor
        def _(h):
            make_runner('pe')(h)

        @block.vector
        def _(h):
            make_runner('dve')(h)

        @block.scalar
        def _(h):
            make_runner('act')(h)

        @block.gpsimd
        def _(h):
            make_runner('pool')(h)

        @block.sync
        def _(h):
            make_runner('sp')(h)
            h.wait_ge(sems['sp16'], final_sp)

    stack.close()
    return nc


def _host_arrays(ins):
    T = _tables()
    f32 = lambda x: np.ascontiguousarray(np.asarray(x), np.float32) if np.asarray(x).dtype != np.float32 else np.ascontiguousarray(x)
    shared = {}
    shared['WqT'] = np.ascontiguousarray(np.asarray(ins['L0_attn_Wq']).T)
    shared['WkT'] = np.ascontiguousarray(np.asarray(ins['L0_attn_Wk']).T)
    shared['WvT'] = np.ascontiguousarray(np.asarray(ins['L0_attn_Wv']).T)
    shared['WoT'] = np.ascontiguousarray(np.asarray(ins['L0_attn_Wo']).T)
    shared['kker'] = np.ascontiguousarray(ins['L0_attn_k_kernel'])
    shared['sker'] = np.ascontiguousarray(ins['L0_attn_ssm_kernel'])
    shared['Wr'] = np.ascontiguousarray(ins['L0_ffn_Wr'])
    shared['W1'] = np.ascontiguousarray(ins['L0_ffn_W1'])
    shared['W2'] = np.ascontiguousarray(ins['L0_ffn_W2'])
    shared['b2d'] = np.ascontiguousarray(ins['L0_ffn_b2'])
    for nm, key in [('bq_c', 'L0_attn_bq'), ('bk_c', 'L0_attn_bk'), ('bv_c', 'L0_attn_bv')]:
        shared[nm] = np.ascontiguousarray(np.asarray(ins[key]).reshape(8, 128).T)
    shared['br_c'] = np.ascontiguousarray(np.asarray(ins['L0_ffn_br']).reshape(8, 1))
    shared['bo_rep'] = np.ascontiguousarray(np.tile(np.asarray(ins['L0_attn_bo'])[None, :], (128, 1)))
    b1 = np.asarray(ins['L0_ffn_b1'], np.float32)
    b1c = np.zeros((128, 256), np.float32)
    for e in range(8):
        for fcg in range(32):
            b1c[:, e * 32 + fcg] = b1[e, fcg * 128:(fcg + 1) * 128]
    shared['b1cd'] = b1c
    kD = np.asarray(ins['L0_attn_k_D'], np.float32)
    Dv = np.asarray(ins['L0_attn_D'], np.float32)

    def mk_pairs(vec):
        a = np.zeros((128, 512), np.float32)
        for ck in range(16):
            for pr in range(32):
                a[0:64, ck * 32 + pr] = vec[ck * 64 + 2 * pr]
                a[64:128, ck * 32 + pr] = vec[ck * 64 + 2 * pr + 1]
        return a.astype(BF)
    shared['kd_pr'] = mk_pairs(kD)
    shared['d_pr'] = mk_pairs(Dv)
    for k, v in T.items():
        shared['tab_' + k] = v
    u = np.asarray(ins['L0_attn_u'], np.float32)
    per_core = []
    for c in range(8):
        b, h = c // 2, c % 2
        d = dict(shared)
        d['u_b'] = np.ascontiguousarray(u[b])
        d['u_own'] = np.ascontiguousarray(u[b, h * 2048:(h + 1) * 2048])
        d['i128c_own'] = T['i128c_h1'] if h else T['i128c_h0']
        d['i128sn_own'] = T['i128sn_h1'] if h else T['i128sn_h0']
        per_core.append(d)
    return per_core


def kernel(**ins):
    nc = build_nc()
    in_maps = _host_arrays(ins)
    res = run_bass_kernel_spmd(nc, in_maps, core_ids=list(range(8)))
    out = np.zeros((4, 4096, 1024), np.float32)
    for c in range(8):
        b, h = c // 2, c % 2
        out[b, h * 2048:(h + 1) * 2048] = res.results[c]["out_own"]
    return out




# version 09
# speedup vs baseline: 1.0536x; 1.0014x over previous
import numpy as np
import ml_dtypes
import contextlib
import concourse.bass as bass
import concourse.mybir as mybir
from concourse.bass_utils import run_bass_kernel_spmd

F32 = mybir.dt.float32
BF16 = mybir.dt.bfloat16
AF = mybir.ActivationFunctionType
ALU = mybir.AluOpType
BF = ml_dtypes.bfloat16

NFFT = 8192
N1 = 64
N2 = 128
NCHUNK = 16
CH = 64
DM = 1024
T_FULL = 4096
T_OWN = 2048
PAIRS = 32
SPEC = 4096


def _tables():
    n2 = np.arange(N2); k2 = np.arange(N2)
    n1 = np.arange(N1); k1 = np.arange(N1)
    a1 = -2 * np.pi * np.outer(n2, k2) / N2
    dft128_c = np.cos(a1); dft128_s = np.sin(a1)
    at = -2 * np.pi * np.outer(k2, n1) / NFFT
    tw_c = np.cos(at); tw_s = np.sin(at)
    a2 = -2 * np.pi * np.outer(n1, k1) / N1
    dft64_c = np.cos(a2); dft64_s = np.sin(a2)
    ai1 = 2 * np.pi * np.outer(k1, n1) / N1
    idft64_c = np.cos(ai1); idft64_s = np.sin(ai1)
    ait = 2 * np.pi * np.outer(n1, k2) / NFFT
    itw_c = np.cos(ait); itw_s = np.sin(ait)
    ai2 = 2 * np.pi * np.outer(k2, n2) / N2
    idft128_c = np.cos(ai2) / NFFT; idft128_s = np.sin(ai2) / NFFT
    dup = lambda m: np.ascontiguousarray(np.concatenate([m[:64], m[:64]], 0))
    bfc = lambda m: np.ascontiguousarray(m, np.float32).astype(BF)
    t = {}
    t['d128c'] = bfc(dup(dft128_c)); t['d128s'] = bfc(dup(dft128_s))
    t['twc'] = bfc(tw_c); t['tws'] = bfc(tw_s)
    t['d64c'] = bfc(np.concatenate([dft64_c] * 2, 0))
    t['d64s'] = bfc(np.concatenate([dft64_s] * 2, 0))
    t['d64sn'] = bfc(np.concatenate([-dft64_s] * 2, 0))
    t['i64c'] = bfc(np.concatenate([idft64_c] * 2, 0))
    t['i64s'] = bfc(np.concatenate([idft64_s] * 2, 0))
    t['i64sn'] = bfc(np.concatenate([-idft64_s] * 2, 0))
    t['itwc'] = bfc(np.concatenate([itw_c] * 2, 0))
    t['itws'] = bfc(np.concatenate([itw_s] * 2, 0))
    t['i128c_full'] = bfc(idft128_c[:, :64])
    t['i128sn_full'] = bfc(-idft128_s[:, :64])
    t['i128c_h0'] = bfc(idft128_c[:, 0:32]); t['i128sn_h0'] = bfc(-idft128_s[:, 0:32])
    t['i128c_h1'] = bfc(idft128_c[:, 32:64]); t['i128sn_h1'] = bfc(-idft128_s[:, 32:64])
    t['ident_bf'] = bfc(np.eye(128))
    t['ident_f32'] = np.ascontiguousarray(np.eye(128), np.float32)
    return t


ENG_SEM = {'pe': 'pe', 'dve': 'dve', 'act': 'act', 'pool': 'pool', 'sp': None}


class Sched:
    def __init__(self):
        self.q = {'pe': [], 'dve': [], 'act': [], 'pool': [], 'sp': []}
        self.cnt = {s: 0 for s in ['pe', 'dve', 'act', 'pool', 'sp16', 'gd16']}
        self.lastw = {e: {} for e in self.q}
        self.pending = {}

    def _waits(self, eng, waits):
        best = {}
        for t in waits:
            if t is None:
                continue
            s, v = t
            if v > best.get(s, -1):
                best[s] = v
        out = []
        for s, v in best.items():
            if self.lastw[eng].get(s, -1) >= v:
                continue
            self.lastw[eng][s] = v
            out.append((s, v))
        return out

    def barrier(self):
        toks = []
        for s, v in self.cnt.items():
            if v > 0:
                toks.append((s, v))
        self.pending = {e: list(toks) for e in self.q}

    def op(self, eng, fn, waits=(), inc=True):
        w = self._waits(eng, list(waits) + self.pending.pop(eng, []))
        sem = ENG_SEM[eng]
        tok = None
        if inc and sem:
            self.cnt[sem] += 1
            tok = (sem, self.cnt[sem])
        self.q[eng].append(('op', w, fn, tok))
        return tok

    def dma(self, eng, out_ap, in_ap, waits=()):
        sem = 'sp16' if eng == 'sp' else 'gd16'
        prior = (sem, self.cnt[sem]) if self.cnt[sem] else None
        w = self._waits(eng, list(waits) + [prior] + self.pending.pop(eng, []))
        self.cnt[sem] += 16
        tok = (sem, self.cnt[sem])
        self.q[eng].append(('dma', w, (out_ap, in_ap), tok))
        return tok


def _maxper(toks):
    best = {}
    for t in toks:
        if t is None:
            continue
        s, v = t
        if v > best.get(s, -1):
            best[s] = v
    return [(s, v) for s, v in best.items()]


def build_nc(debug=False):
    nc = bass.Bass("TRN2", target_bir_lowering=False)
    T = _tables()
    S = Sched()
    stack = contextlib.ExitStack()

    def dram_in(name, shape, dt=F32):
        return nc.dram_tensor(name, list(shape), dt, kind="ExternalInput").ap()

    u_b = dram_in("u_b", [T_FULL, DM])
    u_own = dram_in("u_own", [T_OWN, DM])
    WqT = dram_in("WqT", [DM, DM]); WkT = dram_in("WkT", [DM, DM])
    WvT = dram_in("WvT", [DM, DM]); WoT = dram_in("WoT", [DM, DM])
    kker = dram_in("kker", [DM, 2048]); sker = dram_in("sker", [DM, 2048])
    Wr = dram_in("Wr", [DM, 8])
    W1 = dram_in("W1", [8, DM, 4096]); W2 = dram_in("W2", [8, 4096, DM])
    b2d = dram_in("b2d", [8, DM])
    bq_c = dram_in("bq_c", [128, 8]); bk_c = dram_in("bk_c", [128, 8]); bv_c = dram_in("bv_c", [128, 8])
    br_c = dram_in("br_c", [8, 1]); bo_rep = dram_in("bo_rep", [128, DM])
    b1cd = dram_in("b1cd", [128, 256])
    kd_pr = dram_in("kd_pr", [128, 512], BF16)
    d_pr = dram_in("d_pr", [128, 512], BF16)
    tabs = {k: dram_in("tab_" + k, list(v.shape), BF16 if v.dtype == BF else F32)
            for k, v in T.items()}
    i128c_own = dram_in("i128c_own", [128, 32], BF16)
    i128sn_own = dram_in("i128sn_own", [128, 32], BF16)

    scrkind = "ExternalOutput" if debug else "Internal"
    kscr = nc.dram_tensor("kscr", [DM, T_FULL], F32, kind=scrkind).ap()
    vscr = nc.dram_tensor("vscr", [DM, T_FULL], F32, kind=scrkind).ap()
    if debug:
        dbg_xn = nc.dram_tensor("dbg_xn", [8, 128, T_FULL], BF16, kind="ExternalOutput").ap()
        dbg_ycm = nc.dram_tensor("dbg_ycm", [8, 128, T_OWN], BF16, kind="ExternalOutput").ap()
        dbg_lg = nc.dram_tensor("dbg_lg", [8, 1024], F32, kind="ExternalOutput").ap()
        dbg_gate = nc.dram_tensor("dbg_gate", [128, 64], F32, kind="ExternalOutput").ap()
        dbg_yacc = nc.dram_tensor("dbg_yacc", [8, 128, DM], F32, kind="ExternalOutput").ap()
        dbg_xno = nc.dram_tensor("dbg_xno", [8, 128, 1024], BF16, kind="ExternalOutput").ap()
        dbg_kv = nc.dram_tensor("dbg_kv", [128, 2048], BF16, kind="ExternalOutput").ap()
        dbg_kf = nc.dram_tensor("dbg_kf", [128, 2048], BF16, kind="ExternalOutput").ap()
        dbg_sR = nc.dram_tensor("dbg_sR", [128, SPEC], BF16, kind="ExternalOutput").ap()
        dbg_prb = nc.dram_tensor("dbg_prb", [128, 8], F32, kind="ExternalOutput").ap()
        dbg_Sb = nc.dram_tensor("dbg_Sb", [128, 8], F32, kind="ExternalOutput").ap()
        dbg_kR = nc.dram_tensor("dbg_kR", [128, SPEC], BF16, kind="ExternalOutput").ap()
    out_own = nc.dram_tensor("out_own", [T_OWN, DM], F32, kind="ExternalOutput").ap()

    stack_ab = contextlib.ExitStack()

    def sb(name, shape, dt):
        return stack.enter_context(nc.sbuf_tensor(name, list(shape), dt))

    def sb_ab(name, shape, dt):
        return stack_ab.enter_context(nc.sbuf_tensor(name, list(shape), dt))

    tab_s = {k: sb("t_" + k, v.shape, BF16 if v.dtype == BF else F32) for k, v in T.items()}
    i128c_own_s = sb("i128co", [128, 32], BF16)
    i128sn_own_s = sb("i128so", [128, 32], BF16)
    bq_s = sb("bq_s", [128, 8], F32); bk_s = sb("bk_s", [128, 8], F32); bv_s = sb("bv_s", [128, 8], F32)
    kd_s = sb("kd_s", [128, 512], BF16); dd_s = sb("dd_s", [128, 512], BF16)
    ycm = [sb(f"ycm{i}", [128, T_OWN], BF16) for i in range(8)]
    tmp1 = sb("tmp1", [128, 1024], F32); tmp2 = sb("tmp2", [128, 1024], F32)
    ut = [sb(f"ut{i}", [128, DM], F32) for i in range(2)]
    usq = sb("usq", [128, DM], F32)
    st_s = sb("st_s", [128, 8], F32)
    xnt = sb("xnt", [128, DM], BF16)
    xn = [sb_ab(f"xn{i}", [128, T_FULL], BF16) for i in range(8)]
    cm_slab = sb_ab("cm_slab", [128, 1024], F32)
    wk_s = sb_ab("wk_s", [128, 8 * 128], BF16)
    wv_s = sb_ab("wv_s", [128, 8 * 128], BF16)
    kfft = sb_ab("kfft", [128, 2048], BF16)
    vfft = sb_ab("vfft", [128, 2048], BF16)
    kvt = sb_ab("kvt", [128, 2048], BF16)
    kkfft = sb_ab("kkfft", [128, 2048], BF16)
    Ar = sb_ab("Ar", [128, SPEC], BF16); Ai = sb_ab("Ai", [128, SPEC], BF16)
    Brt = sb_ab("Brt", [128, SPEC], BF16); Bit = sb_ab("Bit", [128, SPEC], BF16)
    Kr = sb_ab("Kr", [128, SPEC], BF16); Ki = sb_ab("Ki", [128, SPEC], BF16)
    y2s = sb_ab("y2s", [32, 4096], BF16)

    psP = [stack.enter_context(nc.psum_tensor(f"psP{i}", [128, 1024], F32)) for i in range(2)]
    psK = stack.enter_context(nc.psum_tensor("psK", [128, 1024], F32))
    psT = [stack.enter_context(nc.psum_tensor(f"psTa{i}", [128, 512], BF16)) for i in range(2)]
    psP = psP + [psK]
    psS = psK

    sems = {s: stack.enter_context(nc.semaphore(s))
            for s in ['pe', 'dve', 'act', 'pool', 'sp16', 'gd16']}

    ps_state = {id(p): {'w': None, 'r': []} for p in psP + [psS] + psT}
    sl_state = {}
    for x in [Ar, Ai, Brt, Bit, Kr, Ki, kfft, vfft, kvt, kkfft, y2s]:
        sl_state[id(x)] = {'w': [], 'r': []}

    def mm_group(ps, mms, waits):
        st = ps_state[id(ps)]
        w = list(waits) + _maxper(st['r']) + ([st['w']] if st['w'] else [])
        tok = None
        first = True
        for i, m in enumerate(mms):
            tok = S.op('pe', m, waits=(w if first else ()), inc=(i == len(mms) - 1))
            first = False
        st['w'] = tok
        st['r'] = []
        return tok

    def ps_read(ps, eng, fn, waits=()):
        st = ps_state[id(ps)]
        w = list(waits) + ([st['w']] if st['w'] else [])
        tok = S.op(eng, fn, waits=w)
        st['r'].append(tok)
        return tok

    def pe_transpose(tp, tfn, waits=()):
        st = ps_state[id(tp)]
        w = list(waits) + _maxper(st['r']) + ([st['w']] if st['w'] else [])
        tok = S.op('pe', tfn, waits=w)
        st['w'] = tok
        st['r'] = []
        return tok

    def slot_w(x, tok):
        sl_state[id(x)]['w'].append(tok)

    def slot_begin(x):
        st = sl_state[id(x)]
        w = _maxper(st['r'] + st['w'])
        st['r'] = []
        st['w'] = []
        return w

    def slot_r(x, tok):
        sl_state[id(x)]['r'].append(tok)

    def slot_wtok(x):
        return _maxper(sl_state[id(x)]['w'])

    for k in tab_s:
        tl = S.dma('sp', tab_s[k][:], tabs[k])
    tl = S.dma('sp', i128c_own_s[:], i128c_own)
    tl = S.dma('sp', i128sn_own_s[:], i128sn_own)
    tl = S.dma('sp', bq_s[:], bq_c)
    tl = S.dma('sp', bk_s[:], bk_c)
    tl = S.dma('sp', bv_s[:], bv_c)
    tl = S.dma('sp', kd_s[:], kd_pr)
    TABTOK = S.dma('sp', dd_s[:], d_pr)

    IDB = tab_s['ident_bf']
    IDF = tab_s['ident_f32']

    ln_guard = {'ut': [None, None], 'usq': None, 'sq': None, 'xnt': None}

    def do_ln(u_src, ntiles, xn_dst):
        last = None
        for i in range(ntiles):
            buf = ut[i % 2]
            g_ = ln_guard['ut'][i % 2]
            g_ = g_ if isinstance(g_, list) else [g_]
            tl_ = S.dma('sp', buf[:], u_src[i * 128:(i + 1) * 128, :],
                        waits=[TABTOK] + g_)
            a = S.op('dve', (lambda b: lambda h: h.reduce_sum(st_s[:, 0:1], b[:], axis=mybir.AxisListType.X))(buf), waits=[tl_])
            b_ = S.op('act', (lambda b: lambda h: h.square(usq[:], b[:]))(buf), waits=[tl_, ln_guard['usq']])
            c = S.op('dve', lambda h: h.reduce_sum(st_s[:, 1:2], usq[:], axis=mybir.AxisListType.X), waits=[b_])
            d = S.op('dve', lambda h: h.tensor_scalar(st_s[:, 2:3], st_s[:, 0:1], 1.0 / DM, None, ALU.mult), waits=[a])
            e = S.op('dve', lambda h: h.tensor_mul(st_s[:, 3:4], st_s[:, 2:3], st_s[:, 2:3]), waits=[d])
            f = S.op('dve', lambda h: h.tensor_scalar(st_s[:, 4:5], st_s[:, 1:2], 1.0 / DM, None, ALU.mult), waits=[c])
            g = S.op('dve', lambda h: h.tensor_sub(st_s[:, 4:5], st_s[:, 4:5], st_s[:, 3:4]), waits=[e, f])
            g2 = S.op('dve', lambda h: h.tensor_scalar(st_s[:, 4:5], st_s[:, 4:5], 1e-5, None, ALU.add), waits=[g])
            r = S.op('dve', lambda h: h.reciprocal(st_s[:, 6:7], st_s[:, 4:5]), waits=[g2, ln_guard['sq']])
            sq = S.op('act', lambda h: h.sqrt(st_s[:, 7:8], st_s[:, 6:7]), waits=[r])
            ln_guard['sq'] = sq
            xt = S.op('dve', (lambda b: lambda h: h.tensor_scalar(xnt[:], b[:], st_s[:, 2:3], st_s[:, 7:8], ALU.subtract, ALU.mult))(buf),
                      waits=[sq, ln_guard['xnt']])
            ln_guard['ut'][i % 2] = [xt, b_]
            ln_guard['usq'] = c
            last_tt = None
            for hc in range(8):
                tp = psT[hc % 2]
                tt = pe_transpose(tp, (lambda hc, tp: lambda h: h.transpose(
                    tp[:, 0:128], xnt[:, hc * 128:(hc + 1) * 128], IDB[:]))(hc, tp), waits=[xt])
                ev = ps_read(tp, 'act', (lambda hc, i, tp: lambda h: h.copy(
                    xn_dst[hc][:, i * 128:(i + 1) * 128], tp[:, 0:128]))(hc, i, tp))
                last = ev
                last_tt = tt
            ln_guard['xnt'] = last_tt
        return last

    xn_done = do_ln(u_b, 32, xn)

    if debug:
        for hc in range(8):
            S.dma('sp', dbg_xn[hc], xn[hc][:], waits=[xn_done])

    cm_guard = {'free': None, 'wk': None, 'wv': None}
    bounce_k = [None] * 8
    bounce_v = [None] * 8
    ycm_written = [None]

    kscr_v = kscr.rearrange("(K hf c) (m n) -> K hf m c n", K=NCHUNK, hf=2, n=N1)
    vscr_v = vscr.rearrange("(K hf c) (m n) -> K hf m c n", K=NCHUNK, hf=2, n=N1)
    kker_v = kker.rearrange("(K hf c) (m n) -> K hf m c n", K=NCHUNK, hf=2, n=N1)
    sker_v = sker.rearrange("(K hf c) (m n) -> K hf m c n", K=NCHUNK, hf=2, n=N1)

    def proj_cm(WT_d, wbuf, bias_col, scratch, pchunk, wfree):
        wv = WT_d.rearrange("(hc p) d -> p hc d", hc=8)[:, :, pchunk * 128:(pchunk + 1) * 128]
        wl = S.dma('pool', wbuf[:].rearrange("p (hc d) -> p hc d", hc=8), wv, waits=[wfree])
        bw = None
        for t5 in range(8):
            ps = psP[t5 % 2]
            mms = []
            for hc in range(8):
                mms.append((lambda hc, t5, ps: lambda h: h.matmul(
                    ps[:, 0:512], wbuf[:, hc * 128:(hc + 1) * 128],
                    xn[hc][:, t5 * 512:(t5 + 1) * 512],
                    start=(hc == 0), stop=(hc == 7)))(hc, t5, ps))
            mt = mm_group(ps, mms, waits=[wl, xn_done])
            ev = ps_read(ps, 'act', (lambda t5, ps: lambda h: h.activation(
                cm_slab[:, (t5 % 2) * 512:(t5 % 2) * 512 + 512], ps[:, 0:512], AF.Identity,
                bias=bias_col, scale=1.0))(t5, ps), waits=[cm_guard['free']])
            if t5 % 2 == 1:
                bw = S.dma('sp', scratch[pchunk * 128:(pchunk + 1) * 128,
                                         (t5 - 1) * 512:(t5 + 1) * 512], cm_slab[:], waits=[ev])
                cm_guard['free'] = bw
        return bw, None

    def s1tw(rhs_t, rhs_tok, nz, Bd_r, Bd_i, mode):
        wA = slot_begin(Bd_r) + slot_begin(Bd_i)
        tw_prev = [None]
        for hf in range(2):
            lhs_c = tab_s['d128c'][64 * hf:64 * hf + nz, :]
            lhs_s = tab_s['d128s'][64 * hf:64 * hf + nz, :]
            npieces = 2 if mode == 'P' else 4
            width = 1024 if mode == 'P' else 512
            for kc in range(npieces):
                col0 = 2048 * hf + kc * width
                if mode == 'P':
                    ps_re = psP[0]; ps_im = psP[1]
                    mre = []; mim = []
                    for half in range(2):
                        sl = slice(kc * 1024 + half * 512, kc * 1024 + half * 512 + 512)
                        ob = slice(half * 512, half * 512 + 512)
                        mre.append((lambda lhs_c, sl, ob, hf, nz: lambda h: h.matmul(
                            psP[0][:, ob], lhs_c, rhs_t[64 * hf:64 * hf + nz, sl],
                            start=True, stop=True))(lhs_c, sl, ob, hf, nz))
                        mim.append((lambda lhs_s, sl, ob, hf, nz: lambda h: h.matmul(
                            psP[1][:, ob], lhs_s, rhs_t[64 * hf:64 * hf + nz, sl],
                            start=True, stop=True))(lhs_s, sl, ob, hf, nz))
                    t_re = mm_group(ps_re, mre, waits=[rhs_tok] + wA)
                    t_im = mm_group(ps_im, mim, waits=[rhs_tok])
                    re_ap = psP[0][:]; im_ap = psP[1][:]
                    nch = 16
                else:
                    ps_re = psK; ps_im = psK
                    sl = slice(kc * 512, kc * 512 + 512)
                    mm2 = [(lambda lhs_c, sl, hf, nz: lambda h: h.matmul(
                        psK[:, 0:512], lhs_c, rhs_t[64 * hf:64 * hf + nz, sl],
                        start=True, stop=True))(lhs_c, sl, hf, nz),
                        (lambda lhs_s, sl, hf, nz: lambda h: h.matmul(
                            psK[:, 512:1024], lhs_s, rhs_t[64 * hf:64 * hf + nz, sl],
                            start=True, stop=True))(lhs_s, sl, hf, nz)]
                    t_re = mm_group(psK, mm2, waits=[rhs_tok] + wA)
                    t_im = t_re
                    re_ap = psK[:, 0:512]; im_ap = psK[:, 512:1024]
                    nch = 8
                slot_r(rhs_t, t_re); slot_r(rhs_t, t_im)
                wA = []
                pv_re = re_ap.rearrange("p (c n) -> p c n", n=N1)
                pv_im = im_ap.rearrange("p (c n) -> p c n", n=N1)
                twc_b = tab_s['twc'][:].unsqueeze(1).broadcast_to([128, nch, N1])
                tws_b = tab_s['tws'][:].unsqueeze(1).broadcast_to([128, nch, N1])
                w_ = width
                a1 = ps_read(ps_re, 'dve', (lambda pv_re, twc_b, w_: lambda h: h.tensor_tensor(
                    tmp1[:, 0:w_].rearrange("p (c n) -> p c n", n=N1), pv_re, twc_b, ALU.mult))(pv_re, twc_b, w_), waits=[t_re, tw_prev[0]])
                a2 = ps_read(ps_im, 'dve', (lambda pv_im, tws_b, w_: lambda h: h.tensor_tensor(
                    tmp2[:, 0:w_].rearrange("p (c n) -> p c n", n=N1), pv_im, tws_b, ALU.mult))(pv_im, tws_b, w_), waits=[t_im])
                a3 = S.op('dve', (lambda col0, w_: lambda h: h.tensor_sub(
                    Bd_r[:, col0:col0 + w_], tmp1[:, 0:w_], tmp2[:, 0:w_]))(col0, w_), waits=[a1, a2])
                slot_w(Bd_r, a3)
                a4 = ps_read(ps_re, 'dve', (lambda pv_re, tws_b, w_: lambda h: h.tensor_tensor(
                    tmp1[:, 0:w_].rearrange("p (c n) -> p c n", n=N1), pv_re, tws_b, ALU.mult))(pv_re, tws_b, w_), waits=[a3])
                a5 = ps_read(ps_im, 'dve', (lambda pv_im, twc_b, w_: lambda h: h.tensor_tensor(
                    tmp2[:, 0:w_].rearrange("p (c n) -> p c n", n=N1), pv_im, twc_b, ALU.mult))(pv_im, twc_b, w_), waits=[a3])
                a6 = S.op('dve', (lambda col0, w_: lambda h: h.tensor_add(
                    Bd_i[:, col0:col0 + w_], tmp1[:, 0:w_], tmp2[:, 0:w_]))(col0, w_), waits=[a4, a5])
                slot_w(Bd_i, a6)
                tw_prev[0] = a6

    def trans_batch(src_r, src_i):
        wB = slot_begin(Brt) + slot_begin(Bit)
        for plane in range(2):
            src = src_r if plane == 0 else src_i
            dst = Brt if plane == 0 else Bit
            for g in range(8):
                tp = psT[plane]
                tts = None
                first = True
                for j in range(4):
                    pr = g * 4 + j
                    tfn = (lambda src, pr, tp, j: lambda h: h.transpose(
                        tp[:, j * 128:(j + 1) * 128], src[:, pr * 128:(pr + 1) * 128], IDB[:]))(src, pr, tp, j)
                    if first:
                        tts = pe_transpose(tp, tfn, waits=slot_wtok(src) + wB)
                        wB = []
                        first = False
                    else:
                        st = ps_state[id(tp)]
                        tts = S.op('pe', tfn)
                        st['w'] = tts
                slot_r(src, tts)
                ev = ps_read(tp, 'act', (lambda dst, g, tp: lambda h: h.copy(
                    dst[:, g * 512:(g + 1) * 512], tp[:, 0:512]))(dst, g, tp))
                slot_w(dst, ev)

    def s2f(Rdst, Idst, fold_tab, mode, ck, tabc='d64c', tabs_='d64s', tabsn='d64sn'):
        wR = slot_begin(Rdst) + slot_begin(Idst)
        npieces = 4 if mode == 'P' else 8
        width = 1024 if mode == 'P' else 512
        for piece in range(npieces):
            col0 = piece * width
            if mode == 'P':
                ps_re = psP[0]; ps_im = psP[1]
                re_ap = psP[0][:]; im_ap = psP[1][:]
            else:
                ps_re = psK; ps_im = psK
                re_ap = psK[:, 0:512]; im_ap = psK[:, 512:1024]
            mre = []; mim = []
            for par in range(2):
                rb = slice(64 * par, 64 * par + 64)
                tp_pos = (64 * par, 64 * par)
                nsub = 2 if mode == 'P' else 1
                for half in range(nsub):
                    s2_ = slice(col0 + half * 512, col0 + half * 512 + 512)
                    if mode == 'P':
                        ob = slice(half * 512, half * 512 + 512)
                        mre += [
                            (lambda ob, s2_, tp_pos, rb: lambda h: h.matmul(
                                psP[0][tp_pos[1]:tp_pos[1] + 64, ob], tab_s[tabc][rb, :], Brt[rb, s2_],
                                start=True, stop=False, tile_position=tp_pos))(ob, s2_, tp_pos, rb),
                            (lambda ob, s2_, tp_pos, rb: lambda h: h.matmul(
                                psP[0][tp_pos[1]:tp_pos[1] + 64, ob], tab_s[tabsn][rb, :], Bit[rb, s2_],
                                start=False, stop=True, tile_position=tp_pos))(ob, s2_, tp_pos, rb)]
                        mim += [
                            (lambda ob, s2_, tp_pos, rb: lambda h: h.matmul(
                                psP[1][tp_pos[1]:tp_pos[1] + 64, ob], tab_s[tabs_][rb, :], Brt[rb, s2_],
                                start=True, stop=False, tile_position=tp_pos))(ob, s2_, tp_pos, rb),
                            (lambda ob, s2_, tp_pos, rb: lambda h: h.matmul(
                                psP[1][tp_pos[1]:tp_pos[1] + 64, ob], tab_s[tabc][rb, :], Bit[rb, s2_],
                                start=False, stop=True, tile_position=tp_pos))(ob, s2_, tp_pos, rb)]
                    else:
                        mre += [
                            (lambda s2_, tp_pos, rb: lambda h: h.matmul(
                                psK[tp_pos[1]:tp_pos[1] + 64, 0:512], tab_s[tabc][rb.start:rb.stop, :], Brt[rb, s2_],
                                start=True, stop=False, tile_position=tp_pos))(s2_, tp_pos, rb),
                            (lambda s2_, tp_pos, rb: lambda h: h.matmul(
                                psK[tp_pos[1]:tp_pos[1] + 64, 0:512], tab_s[tabsn][rb.start:rb.stop, :], Bit[rb, s2_],
                                start=False, stop=True, tile_position=tp_pos))(s2_, tp_pos, rb)]
                        mim += [
                            (lambda s2_, tp_pos, rb: lambda h: h.matmul(
                                psK[tp_pos[1]:tp_pos[1] + 64, 512:1024], tab_s[tabs_][rb.start:rb.stop, :], Brt[rb, s2_],
                                start=True, stop=False, tile_position=tp_pos))(s2_, tp_pos, rb),
                            (lambda s2_, tp_pos, rb: lambda h: h.matmul(
                                psK[tp_pos[1]:tp_pos[1] + 64, 512:1024], tab_s[tabc][rb.start:rb.stop, :], Bit[rb, s2_],
                                start=False, stop=True, tile_position=tp_pos))(s2_, tp_pos, rb)]
            wdep = slot_wtok(Brt) + slot_wtok(Bit) + wR
            wR = []
            if mode == 'P':
                t_re = mm_group(ps_re, mre, waits=wdep)
                t_im = mm_group(ps_im, mim, waits=wdep)
            else:
                t_re = mm_group(psK, mre + mim, waits=wdep)
                t_im = t_re
            slot_r(Brt, t_im); slot_r(Bit, t_im)
            e1 = ps_read(ps_re, 'act', (lambda col0, re_ap, width: lambda h: h.copy(
                Rdst[:, col0:col0 + width], re_ap))(col0, re_ap, width))
            e2 = ps_read(ps_im, 'act', (lambda col0, im_ap, width: lambda h: h.copy(
                Idst[:, col0:col0 + width], im_ap))(col0, im_ap, width))
            slot_w(Rdst, e1); slot_w(Idst, e2)
            if fold_tab is not None:
                kdv = fold_tab[:, ck * 32:ck * 32 + 32]
                kdb = kdv.unsqueeze(2).broadcast_to([128, 32, 128])
                dst3 = Rdst[:, col0:col0 + width].rearrange("p (a k) -> p a k", k=128)
                a0 = col0 // 128
                kdb3 = kdb[:, a0:a0 + width // 128, :]
                e3 = S.op('dve', (lambda dst3, kdb3: lambda h: h.tensor_tensor(
                    dst3, dst3, kdb3, ALU.add))(dst3, kdb3), waits=[e1])
                slot_w(Rdst, e3)

    def inv_s1tw():
        wA = slot_begin(Ar) + slot_begin(Ai)
        pw = slot_wtok(Brt) + slot_wtok(Bit)
        q_prev = [None]
        for piece in range(4):
            ps_re = psP[0]; ps_im = psP[1]
            mre = []; mim = []
            for par in range(2):
                rb = slice(64 * par, 64 * par + 64)
                tp_pos = (64 * par, 64 * par)
                for half in range(2):
                    s2_ = slice(piece * 1024 + half * 512, piece * 1024 + half * 512 + 512)
                    ob = slice(half * 512, half * 512 + 512)
                    mre.append((lambda rb, s2_, ob, tp_pos: lambda h: h.matmul(
                        psP[0][tp_pos[1]:tp_pos[1] + 64, ob], tab_s['i64c'][rb, :], Brt[rb, s2_],
                        start=True, stop=False, tile_position=tp_pos))(rb, s2_, ob, tp_pos))
                    mre.append((lambda rb, s2_, ob, tp_pos: lambda h: h.matmul(
                        psP[0][tp_pos[1]:tp_pos[1] + 64, ob], tab_s['i64sn'][rb, :], Bit[rb, s2_],
                        start=False, stop=True, tile_position=tp_pos))(rb, s2_, ob, tp_pos))
                    mim.append((lambda rb, s2_, ob, tp_pos: lambda h: h.matmul(
                        psP[1][tp_pos[1]:tp_pos[1] + 64, ob], tab_s['i64s'][rb, :], Brt[rb, s2_],
                        start=True, stop=False, tile_position=tp_pos))(rb, s2_, ob, tp_pos))
                    mim.append((lambda rb, s2_, ob, tp_pos: lambda h: h.matmul(
                        psP[1][tp_pos[1]:tp_pos[1] + 64, ob], tab_s['i64c'][rb, :], Bit[rb, s2_],
                        start=False, stop=True, tile_position=tp_pos))(rb, s2_, ob, tp_pos))
            wdep = pw + wA
            wA = []; pw = []
            t_re = mm_group(ps_re, mre, waits=wdep)
            t_im = mm_group(ps_im, mim, waits=wdep)
            slot_r(Brt, t_im); slot_r(Bit, t_im)
            col0 = piece * 1024
            pv_re = psP[0][:].rearrange("p (a k) -> p a k", k=128)
            pv_im = psP[1][:].rearrange("p (a k) -> p a k", k=128)
            itc_b = tab_s['itwc'][:].unsqueeze(1).broadcast_to([128, 8, 128])
            its_b = tab_s['itws'][:].unsqueeze(1).broadcast_to([128, 8, 128])
            q1 = ps_read(ps_re, 'dve', (lambda pv_re, itc_b: lambda h: h.tensor_tensor(
                tmp1[:].rearrange("p (a k) -> p a k", k=128), pv_re, itc_b, ALU.mult))(pv_re, itc_b), waits=[t_re, q_prev[0]])
            q2 = ps_read(ps_im, 'dve', (lambda pv_im, its_b: lambda h: h.tensor_tensor(
                tmp2[:].rearrange("p (a k) -> p a k", k=128), pv_im, its_b, ALU.mult))(pv_im, its_b), waits=[t_im])
            q3 = S.op('dve', (lambda col0: lambda h: h.tensor_sub(
                Ar[:, col0:col0 + 1024], tmp1[:], tmp2[:]))(col0), waits=[q1, q2])
            slot_w(Ar, q3)
            q4 = ps_read(ps_re, 'dve', (lambda pv_re, its_b: lambda h: h.tensor_tensor(
                tmp1[:].rearrange("p (a k) -> p a k", k=128), pv_re, its_b, ALU.mult))(pv_re, its_b), waits=[q3])
            q5 = ps_read(ps_im, 'dve', (lambda pv_im, itc_b: lambda h: h.tensor_tensor(
                tmp2[:].rearrange("p (a k) -> p a k", k=128), pv_im, itc_b, ALU.mult))(pv_im, itc_b), waits=[q3])
            q6 = S.op('dve', (lambda col0: lambda h: h.tensor_add(
                Ai[:, col0:col0 + 1024], tmp1[:], tmp2[:]))(col0), waits=[q4, q5])
            slot_w(Ai, q6)
            q_prev[0] = q6

    def inv_s2(out_kind):
        if out_kind == 'kv':
            wkv = slot_begin(kvt)
            for pg in range(2):
                po = 64 * pg
                ps = psP[pg]
                for halfc in range(2):
                    lc = halfc * 1024
                    mm2 = []
                    for sub in range(2):
                        s3 = slice(pg * 2048 + halfc * 1024 + sub * 512,
                                   pg * 2048 + halfc * 1024 + sub * 512 + 512)
                        ob = slice(sub * 512, sub * 512 + 512)
                        mm2.append((lambda s3, ob, po, ps: lambda h: h.matmul(
                            ps[po:po + 64, ob], tab_s['i128c_full'][:], Brt[:, s3],
                            start=True, stop=False, tile_position=(0, po)))(s3, ob, po, ps))
                        mm2.append((lambda s3, ob, po, ps: lambda h: h.matmul(
                            ps[po:po + 64, ob], tab_s['i128sn_full'][:], Bit[:, s3],
                            start=False, stop=True, tile_position=(0, po)))(s3, ob, po, ps))
                    wdep = slot_wtok(Brt) + slot_wtok(Bit) + wkv + slot_wtok(vfft)
                    wkv = []
                    tmm = mm_group(ps, mm2, waits=wdep)
                    slot_r(Brt, tmm); slot_r(Bit, tmm)
                    ev = ps_read(ps, 'dve', (lambda ps, po, lc: lambda h: h.tensor_mul(
                        kvt[po:po + 64, lc:lc + 1024], ps[po:po + 64, 0:1024],
                        vfft[po:po + 64, lc:lc + 1024]))(ps, po, lc))
                    slot_r(vfft, ev)
                    slot_w(kvt, ev)
            return
        wy = slot_begin(y2s)
        for piece in range(4):
            sl1024 = slice(piece * 1024, piece * 1024 + 1024)
            ps = psP[piece % 2]
            mm2 = []
            for sub in range(2):
                s3 = slice(piece * 1024 + sub * 512, piece * 1024 + sub * 512 + 512)
                ob = slice(sub * 512, sub * 512 + 512)
                mm2.append((lambda s3, ob, ps: lambda h: h.matmul(
                    ps[0:32, ob], i128c_own_s[:], Brt[:, s3], start=True, stop=False))(s3, ob, ps))
                mm2.append((lambda s3, ob, ps: lambda h: h.matmul(
                    ps[0:32, ob], i128sn_own_s[:], Bit[:, s3], start=False, stop=True))(s3, ob, ps))
            wdep = slot_wtok(Brt) + slot_wtok(Bit) + wy
            wy = []
            tmm = mm_group(ps, mm2, waits=wdep)
            slot_r(Brt, tmm); slot_r(Bit, tmm)
            ev = ps_read(ps, 'dve', (lambda ps, sl1024: lambda h: h.tensor_copy(
                y2s[:, sl1024], ps[0:32, 0:1024]))(ps, sl1024))
            slot_w(y2s, ev)

    def product():
        wB = slot_begin(Brt) + slot_begin(Bit)
        deps = slot_wtok(Ar) + slot_wtok(Ai) + slot_wtok(Kr) + slot_wtok(Ki)
        pprev = None
        for piece in range(4):
            sl = slice(piece * 1024, piece * 1024 + 1024)
            p1 = S.op('dve', (lambda sl: lambda h: h.tensor_mul(tmp1[:], Ar[:, sl], Kr[:, sl]))(sl),
                      waits=deps + wB + [pprev])
            deps = []; wB = []
            p2 = S.op('dve', (lambda sl: lambda h: h.tensor_mul(tmp2[:], Ai[:, sl], Ki[:, sl]))(sl))
            p3 = S.op('dve', (lambda sl: lambda h: h.tensor_sub(Brt[:, sl], tmp1[:], tmp2[:]))(sl),
                      waits=[p1, p2])
            p4 = S.op('dve', (lambda sl: lambda h: h.tensor_mul(tmp1[:], Ar[:, sl], Ki[:, sl]))(sl),
                      waits=[p3])
            p5 = S.op('dve', (lambda sl: lambda h: h.tensor_mul(tmp2[:], Ai[:, sl], Kr[:, sl]))(sl),
                      waits=[p3])
            p6 = S.op('dve', (lambda sl: lambda h: h.tensor_add(Bit[:, sl], tmp1[:], tmp2[:]))(sl),
                      waits=[p4, p5])
            for x in (Ar, Ai, Kr, Ki):
                slot_r(x, p6)
            slot_w(Brt, p3); slot_w(Bit, p6)
            pprev = p6

    def conv_pass(ck, sig_t, sig_tok, ker_view, fold_tab, out_kind):
        wkk = slot_begin(kkfft)
        kl = None
        for hf in range(2):
            kl = S.dma('pool', kkfft[64 * hf:64 * hf + 32, :].rearrange("m (c n) -> m c n", n=N1),
                       ker_view[ck, hf], waits=wkk)
            wkk = []
        slot_w(kkfft, kl)
        s1tw(kkfft, kl, 32, Kr, Ki, 'K')
        s1tw(sig_t, sig_tok, 64, Ar, Ai, 'P')
        trans_batch(Kr, Ki)
        s2f(Kr, Ki, fold_tab, 'K', ck)
        trans_batch(Ar, Ai)
        s2f(Ar, Ai, None, 'P', ck)
        product()
        inv_s1tw()
        trans_batch(Ar, Ai)
        inv_s2(out_kind)

    def do_proj(pc):
        bounce_k[pc], _ = proj_cm(WkT, wk_s, bk_s[:, pc:pc + 1], kscr, pc, cm_guard['wk'])
        cm_guard['wk'] = bounce_k[pc]
        bounce_v[pc], _ = proj_cm(WvT, wv_s, bv_s[:, pc:pc + 1], vscr, pc, cm_guard['wv'])
        cm_guard['wv'] = bounce_v[pc]

    do_proj(0)
    for ck in range(NCHUNK):
        pchunk = ck // 2
        cchunk = ck // 2
        if ck % 2 == 0 and pchunk + 1 < 8:
            do_proj(pchunk + 1)
        wk0 = slot_begin(kfft)
        for hf in range(2):
            krt = S.dma('pool', kfft[64 * hf:64 * hf + 64, :].rearrange("m (c n) -> m c n", n=N1),
                        kscr_v[ck, hf], waits=[bounce_k[pchunk]] + wk0)
            wk0 = []
        slot_w(kfft, krt)
        wv0 = slot_begin(vfft)
        for hf in range(2):
            vrt = S.dma('pool', vfft[64 * hf:64 * hf + 64, :].rearrange("m (c n) -> m c n", n=N1),
                        vscr_v[ck, hf], waits=[bounce_v[pchunk]] + wv0)
            wv0 = []
        slot_w(vfft, vrt)
        conv_pass(ck, kfft, krt, kker_v, kd_s, 'kv')
        if debug and ck == 0:
            slot_r(kfft, S.dma('sp', dbg_kf, kfft[:], waits=slot_wtok(kfft)))
            slot_r(kvt, S.dma('sp', dbg_kv, kvt[:], waits=slot_wtok(kvt)))
            slot_r(Ar, S.dma('sp', dbg_sR, Ar[:], waits=slot_wtok(Ar)))
            slot_r(Kr, S.dma('sp', dbg_kR, Kr[:], waits=slot_wtok(Kr)))
        kv_tok = slot_wtok(kvt)
        conv_pass(ck, kvt, kv_tok[0] if kv_tok else None, sker_v, dd_s, 'y')
        ytoks = slot_wtok(y2s)
        cb = (ck * CH) % 128
        y3 = y2s[:].rearrange("p (a b n) -> p a b n", a=32, b=2)
        for n1g in range(16):
            tp = psT[n1g % 2]
            tts = None
            for j in range(4):
                n1v = n1g * 4 + j
                tfn = (lambda n1v, j, tp, cb: lambda h: h.transpose(
                    tp[cb:cb + 64, j * 32:(j + 1) * 32],
                    y3[:, :, :, n1v].rearrange("p a b -> p (a b)"),
                    IDB[0:32, 0:32], tile_position=(0, cb)))(n1v, j, tp, cb)
                if j == 0:
                    tts = pe_transpose(tp, tfn, waits=ytoks)
                else:
                    st = ps_state[id(tp)]
                    tts = S.op('pe', tfn)
                    st['w'] = tts
            slot_r(y2s, tts)
            dst = ycm[cchunk][cb:cb + 64, :].rearrange("p (m n) -> p m n", n=N1)
            def evfn(h, dst=dst, tp=tp, cb=cb, n1g=n1g):
                o = dst[:, :, n1g * 4:(n1g + 1) * 4].rearrange("p m n -> p n m")
                return h.tensor_copy(o, tp[cb:cb + 64, 0:128].rearrange("p (j m) -> p j m", j=4))
            ev = ps_read(tp, 'dve', evfn)
            ycm_written[0] = ev

    if debug:
        for hc in range(8):
            S.dma('sp', dbg_ycm[hc], ycm[hc][:], waits=[ycm_written[0]])

    S.barrier()
    stack_ab.close()
    xn_own = [sb(f"xno{i}", [128, 1024], BF16) for i in range(8)]
    yacc = [sb(f"yacc{i}", [128, DM], F32) for i in range(8)]
    w1s = [sb(f"w1s{i}", [128, 8 * 512], BF16) for i in range(2)]
    w2s = [sb(f"w2s{i}", [128, 4 * DM], BF16) for i in range(2)]
    hbuf = [sb(f"hbuf{i}", [128, 1024], BF16) for i in range(8)]
    wot_s = [sb(f"wot{i}", [128, DM], BF16) for i in range(8)]
    wq_s = sb("wq_s", [128, 8 * 128], BF16)
    wr_s = sb("wr_s", [128, 8 * 8], BF16)
    b2_s = sb("b2_s", [8, DM], BF16)
    b1c_s = sb("b1c_s", [128, 256], F32)
    bo_s = sb("bo_s", [128, DM], F32)
    br_s = sb("br_s", [8, 1], F32)
    lg_em = sb("lg_em", [8, 1024], F32)
    lg_tm = sb("lg_tm", [128, 8], F32)
    prb = sb("prb", [128, 8], F32)
    Sb = sb("Sb", [128, 8], F32)
    tmp8 = sb("tmp8", [128, 8], F32)
    gate_tm = sb("gate_tm", [128, 8 * 8], F32)
    gate_cm = sb("gate_cm", [8, 1024], BF16)
    outt = [sb(f"outt{i}", [128, DM], F32) for i in range(2)]
    uot = [sb(f"uot{i}", [128, DM], F32) for i in range(2)]

    tC = S.dma('pool', wr_s[:].rearrange("p (hc e) -> p hc e", hc=8),
               Wr.rearrange("(hc p) e -> p hc e", hc=8))
    tC = S.dma('pool', b2_s[:], b2d)
    for hc in range(8):
        tC = S.dma('pool', wot_s[hc][:], WoT.rearrange("(hc p) o -> hc p o", hc=8)[hc])
    tCp = tC
    tC2 = S.dma('sp', b1c_s[:], b1cd)
    tC2 = S.dma('sp', bo_s[:], bo_rep)
    tC2 = S.dma('sp', br_s[:], br_c)
    tCs = tC2

    STORE_TOKS = []
    mm2_prev = None

    for half in range(2):
        hb = half * 1024
        xo_done = do_ln(u_own[half * 1024:(half + 1) * 1024, :], 8, xn_own)
        lg_toks = []
        for pc in range(2):
            mmsp = []
            for hc in range(8):
                mmsp.append((lambda pc, hc: lambda h: h.matmul(
                    psS[0:8, 0:512], wr_s[:, hc * 8:(hc + 1) * 8],
                    xn_own[hc][:, pc * 512:(pc + 1) * 512],
                    start=(hc == 0), stop=(hc == 7)))(pc, hc))
            tmm = mm_group(psS, mmsp, waits=[xo_done, tCp, tCs])
            ev = ps_read(psS, 'act', (lambda pc: lambda h: h.activation(
                lg_em[:, pc * 512:(pc + 1) * 512], psS[0:8, 0:512], AF.Identity,
                bias=br_s[:], scale=1.0))(pc))
            lg_toks.append(ev)
        gate_done = []
        prev_topp = None
        for ti in range(8):
            ps = psP[2]
            tt = pe_transpose(ps, (lambda ti, ps: lambda h: h.transpose(
                ps[0:128, 0:8], lg_em[:, ti * 128:(ti + 1) * 128], IDF[0:8, 0:8]))(ti, ps),
                waits=lg_toks + [prev_topp])
            c1 = ps_read(ps, 'dve', (lambda ps: lambda h: h.tensor_copy(lg_tm[:], ps[0:128, 0:8]))(ps))
            m1 = S.op('dve', lambda h: h.reduce_max(st_s[:, 0:1], lg_tm[:], axis=mybir.AxisListType.X), waits=[c1])
            m2 = S.op('dve', lambda h: h.tensor_scalar(lg_tm[:], lg_tm[:], st_s[:, 0:1], None, ALU.subtract), waits=[m1])
            m3 = S.op('act', lambda h: h.activation(prb[:], lg_tm[:], AF.Exp), waits=[m2])
            m4 = S.op('dve', lambda h: h.reduce_sum(st_s[:, 1:2], prb[:], axis=mybir.AxisListType.X), waits=[m3])
            m5 = S.op('dve', lambda h: h.reciprocal(st_s[:, 2:3], st_s[:, 1:2]), waits=[m4])
            m6 = S.op('dve', lambda h: h.tensor_scalar(prb[:], prb[:], st_s[:, 2:3], None, ALU.mult), waits=[m5])
            last = m6
            for j in range(8):
                j1 = S.op('dve', (lambda j: lambda h: h.tensor_scalar(tmp8[:], prb[:], prb[:, j:j + 1], None, ALU.is_ge))(j), waits=[last])
                j2 = S.op('dve', lambda h: h.tensor_mul(tmp8[:], tmp8[:], prb[:]), waits=[j1])
                last = S.op('dve', (lambda j: lambda h: h.reduce_sum(Sb[:, j:j + 1], tmp8[:], axis=mybir.AxisListType.X))(j), waits=[j2])
            k1 = S.op('dve', lambda h: h.tensor_scalar(tmp8[:], Sb[:], 0.8, None, ALU.is_lt), waits=[last])
            k2 = S.op('dve', lambda h: h.tensor_tensor(Sb[:], Sb[:], prb[:], ALU.is_equal), waits=[k1])
            k3 = S.op('dve', lambda h: h.tensor_max(tmp8[:], tmp8[:], Sb[:]), waits=[k2])
            k4 = S.op('dve', (lambda ti: lambda h: h.tensor_mul(
                gate_tm[:, ti * 8:(ti + 1) * 8], tmp8[:], prb[:]))(ti), waits=[k3])
            prev_topp = k4
            tt2 = pe_transpose(ps, (lambda ti, ps: lambda h: h.transpose(
                ps[0:8, 0:128], gate_tm[:, ti * 8:(ti + 1) * 8], IDF[:]))(ti, ps), waits=[k4])
            ev2 = ps_read(ps, 'dve', (lambda ti, ps: lambda h: h.tensor_copy(
                gate_cm[:, ti * 128:(ti + 1) * 128], ps[0:8, 0:128]))(ti, ps))
            gate_done.append(ev2)
        yacc_last = [None] * 8
        yinit = []
        for ti in range(8):
            ps = psP[0]
            mmb = []
            for oh in range(2):
                mmb.append((lambda ti, oh, ps: lambda h: h.matmul(
                    ps[:, oh * 512:(oh + 1) * 512], gate_cm[:, ti * 128:(ti + 1) * 128],
                    b2_s[:, oh * 512:(oh + 1) * 512], start=True, stop=True))(ti, oh, ps))
            tmm = mm_group(ps, mmb, waits=[gate_done[ti], tCp, tCs])
            ev = ps_read(ps, 'dve', (lambda ti, ps: lambda h: h.tensor_add(
                yacc[ti][:], ps[:], bo_s[:]))(ti, ps), waits=[yacc_last[ti]])
            yacc_last[ti] = ev
            yinit.append(ev)
        g_done = []
        prev_q = None
        for dc in range(8):
            wl = S.dma('pool', wq_s[:].rearrange("p (hc d) -> p hc d", hc=8),
                       WqT.rearrange("(hc p) d -> p hc d", hc=8)[:, :, dc * 128:(dc + 1) * 128],
                       waits=[prev_q])
            for pc in range(2):
                ps = psP[pc % 2]
                mmq = []
                for hc in range(8):
                    mmq.append((lambda pc, hc, ps: lambda h: h.matmul(
                        ps[:, 0:512], wq_s[:, hc * 128:(hc + 1) * 128],
                        xn_own[hc][:, pc * 512:(pc + 1) * 512],
                        start=(hc == 0), stop=(hc == 7)))(pc, hc, ps))
                tmm = mm_group(ps, mmq, waits=[wl, xo_done, ycm_written[0]])
                ev = ps_read(ps, 'dve', (lambda dc, pc, ps, hb: lambda h: h.scalar_tensor_tensor(
                    ycm[dc][:, hb + pc * 512: hb + pc * 512 + 512], ps[:, 0:512],
                    bq_s[:, dc:dc + 1],
                    ycm[dc][:, hb + pc * 512: hb + pc * 512 + 512],
                    ALU.add, ALU.mult))(dc, pc, ps, hb))
                g_done.append(ev)
                prev_q = ev
        g_last = g_done[-1]
        w_use = {'w1': [None, None], 'w2': [None, None]}
        ld = {}

        def issue_w1(gfs, e, fs):
            pb = gfs % 2
            lt = S.dma('pool', w1s[pb][:].rearrange("p (hc f) -> p hc f", hc=8),
                       W1[e].rearrange("(hc p) f -> p hc f", hc=8)[:, :, fs * 512:(fs + 1) * 512],
                       waits=[w_use['w1'][pb]])
            ld[('w1', gfs)] = lt

        def issue_w2(gfs, e, fs):
            pb = gfs % 2
            lt = S.dma('pool', w2s[pb][:].rearrange("p (fc o) -> p fc o", fc=4),
                       W2[e].rearrange("(fc p) o -> p fc o", fc=32)[:, fs * 4:(fs + 1) * 4, :],
                       waits=[w_use['w2'][pb]])
            ld[('w2', gfs)] = lt

        issue_w1(0, 0, 0); issue_w2(0, 0, 0)
        issue_w1(1, 0, 1); issue_w2(1, 0, 1)
        h_done = [None] * 8
        for e in range(8):
            for pairi in range(4):
                for sub in range(2):
                    fs = pairi * 2 + sub
                    gfs = e * 8 + fs
                    pb = gfs % 2
                    lt1 = ld[('w1', gfs)]
                    mm1_last = None
                    for fc in range(4):
                        psm = psP[fc % 2]
                        mms = []
                        for hc in range(8):
                            for tq in range(2):
                                mms.append((lambda fc, hc, tq, psm, pb: lambda h: h.matmul(
                                    psm[:, tq * 512:(tq + 1) * 512],
                                    w1s[pb][:, hc * 512 + fc * 128: hc * 512 + fc * 128 + 128],
                                    xn_own[hc][:, tq * 512:(tq + 1) * 512],
                                    start=(hc == 0), stop=(hc == 7)))(fc, hc, tq, psm, pb))
                        tmm = mm_group(psm, mms, waits=[lt1, xo_done])
                        mm1_last = tmm
                        bcol = b1c_s[:, e * 32 + fs * 4 + fc: e * 32 + fs * 4 + fc + 1]
                        hidx = sub * 4 + fc
                        ev = ps_read(psm, 'act', (lambda psm, bcol, hidx: lambda h: h.activation(
                            hbuf[hidx][:], psm[:], AF.Relu, bias=bcol, scale=1.0))(psm, bcol, hidx))
                        h_done[hidx] = ev
                    w_use['w1'][pb] = mm1_last
                    if gfs + 2 < 64:
                        e2_, fs2 = divmod(gfs + 2, 8)
                        issue_w1(gfs + 2, e2_, fs2)
                mm2_tok = None
                for tt_ in range(8):
                    psm2 = psP[2]
                    mms2 = []
                    for fc8 in range(8):
                        sub2 = fc8 // 4; fc = fc8 % 4
                        pb2 = (e * 8 + pairi * 2 + sub2) % 2
                        for oh in range(2):
                            mms2.append((lambda fc8, fc, sub2, pb2, oh, tt_, psm2: lambda h: h.matmul(
                                psm2[:, oh * 512:(oh + 1) * 512],
                                hbuf[sub2 * 4 + fc][:, tt_ * 128:(tt_ + 1) * 128],
                                w2s[pb2][:, fc * 1024 + oh * 512: fc * 1024 + oh * 512 + 512],
                                start=(fc8 == 0), stop=(fc8 == 7)))(fc8, fc, sub2, pb2, oh, tt_, psm2))
                    wdep = _maxper(h_done) + [ld[('w2', e * 8 + pairi * 2)], ld[('w2', e * 8 + pairi * 2 + 1)]]
                    tmm2 = mm_group(psm2, mms2, waits=wdep)
                    mm2_tok = tmm2
                    gcol = gate_tm[:, tt_ * 8 + e: tt_ * 8 + e + 1]
                    ev2 = ps_read(psm2, 'dve', (lambda psm2, gcol, tt_: lambda h: h.scalar_tensor_tensor(
                        yacc[tt_][:], psm2[:], gcol, yacc[tt_][:], ALU.mult, ALU.add))(psm2, gcol, tt_),
                        waits=[yacc_last[tt_], gate_done[tt_]])
                    yacc_last[tt_] = ev2
                    mm2_prev = ev2
                for q_ in range(2):
                    gfs = e * 8 + pairi * 2 + q_
                    w_use['w2'][gfs % 2] = mm2_tok
                    if gfs + 2 < 64:
                        e2_, fs2 = divmod(gfs + 2, 8)
                        issue_w2(gfs + 2, e2_, fs2)
        if debug and half == 0:
            S.dma('sp', dbg_prb, prb[:], waits=[gate_done[-1]])
            S.dma('sp', dbg_Sb, Sb[:], waits=[gate_done[-1]])
            for hc in range(8):
                S.dma('sp', dbg_xno[hc], xn_own[hc][:], waits=[xo_done])
            S.dma('sp', dbg_lg, lg_em[:], waits=lg_toks)
            S.dma('sp', dbg_gate, gate_tm[:], waits=[gate_done[-1]])
            for ti in range(8):
                S.dma('sp', dbg_yacc[ti], yacc[ti][:], waits=[mm2_prev])
        for ti in range(8):
            ub = uot[ti % 2]
            ul = S.dma('sp', ub[:], u_own[half * 1024 + ti * 128: half * 1024 + (ti + 1) * 128, :],
                       waits=[STORE_TOKS[-2] if len(STORE_TOKS) >= 2 else None])
            psm = psP[ti % 2]
            mmo = []
            for chc in range(8):
                for oh in range(2):
                    mmo.append((lambda chc, oh, ti, psm, hb: lambda h: h.matmul(
                        psm[:, oh * 512:(oh + 1) * 512],
                        ycm[chc][:, hb + ti * 128: hb + ti * 128 + 128],
                        wot_s[chc][:, oh * 512:(oh + 1) * 512],
                        start=(chc == 0), stop=(chc == 7)))(chc, oh, ti, psm, hb))
            tmm = mm_group(psm, mmo, waits=[g_last, tCp, mm2_prev])
            ob = outt[ti % 2]
            e1 = ps_read(psm, 'dve', (lambda psm, ti, ob: lambda h: h.tensor_add(
                ob[:], psm[:], yacc[ti][:]))(psm, ti, ob),
                waits=[yacc_last[ti], STORE_TOKS[-2] if len(STORE_TOKS) >= 2 else None])
            e2 = S.op('dve', (lambda ob, ub: lambda h: h.tensor_add(ob[:], ob[:], ub[:]))(ob, ub),
                      waits=[e1, ul])
            stt = S.dma('sp', out_own[half * 1024 + ti * 128: half * 1024 + (ti + 1) * 128, :],
                        ob[:], waits=[e2])
            STORE_TOKS.append(stt)

    def make_runner(name):
        def run(h):
            for kind, waits, payload, tok in S.q[name]:
                for (s, v) in waits:
                    h.wait_ge(sems[s], v)
                if kind == 'op':
                    r = payload(h)
                    if tok is not None:
                        r.then_inc(sems[tok[0]], 1)
                else:
                    out_ap, in_ap = payload
                    h.dma_start(out_ap, in_ap).then_inc(sems[tok[0]], 16)
        return run

    final_sp = S.cnt['sp16']
    with nc.Block() as block:
        @block.tensor
        def _(h):
            make_runner('pe')(h)

        @block.vector
        def _(h):
            make_runner('dve')(h)

        @block.scalar
        def _(h):
            make_runner('act')(h)

        @block.gpsimd
        def _(h):
            make_runner('pool')(h)

        @block.sync
        def _(h):
            make_runner('sp')(h)
            h.wait_ge(sems['sp16'], final_sp)

    stack.close()
    return nc


def _host_arrays(ins):
    T = _tables()
    f32 = lambda x: np.ascontiguousarray(np.asarray(x), np.float32) if np.asarray(x).dtype != np.float32 else np.ascontiguousarray(x)
    shared = {}
    shared['WqT'] = np.ascontiguousarray(np.asarray(ins['L0_attn_Wq']).T)
    shared['WkT'] = np.ascontiguousarray(np.asarray(ins['L0_attn_Wk']).T)
    shared['WvT'] = np.ascontiguousarray(np.asarray(ins['L0_attn_Wv']).T)
    shared['WoT'] = np.ascontiguousarray(np.asarray(ins['L0_attn_Wo']).T)
    shared['kker'] = np.ascontiguousarray(ins['L0_attn_k_kernel'])
    shared['sker'] = np.ascontiguousarray(ins['L0_attn_ssm_kernel'])
    shared['Wr'] = np.ascontiguousarray(ins['L0_ffn_Wr'])
    shared['W1'] = np.ascontiguousarray(ins['L0_ffn_W1'])
    shared['W2'] = np.ascontiguousarray(ins['L0_ffn_W2'])
    shared['b2d'] = np.ascontiguousarray(ins['L0_ffn_b2'])
    for nm, key in [('bq_c', 'L0_attn_bq'), ('bk_c', 'L0_attn_bk'), ('bv_c', 'L0_attn_bv')]:
        shared[nm] = np.ascontiguousarray(np.asarray(ins[key]).reshape(8, 128).T)
    shared['br_c'] = np.ascontiguousarray(np.asarray(ins['L0_ffn_br']).reshape(8, 1))
    shared['bo_rep'] = np.ascontiguousarray(np.tile(np.asarray(ins['L0_attn_bo'])[None, :], (128, 1)))
    b1 = np.asarray(ins['L0_ffn_b1'], np.float32)
    b1c = np.zeros((128, 256), np.float32)
    for e in range(8):
        for fcg in range(32):
            b1c[:, e * 32 + fcg] = b1[e, fcg * 128:(fcg + 1) * 128]
    shared['b1cd'] = b1c
    kD = np.asarray(ins['L0_attn_k_D'], np.float32)
    Dv = np.asarray(ins['L0_attn_D'], np.float32)

    def mk_pairs(vec):
        a = np.zeros((128, 512), np.float32)
        for ck in range(16):
            for pr in range(32):
                a[0:64, ck * 32 + pr] = vec[ck * 64 + 2 * pr]
                a[64:128, ck * 32 + pr] = vec[ck * 64 + 2 * pr + 1]
        return a.astype(BF)
    shared['kd_pr'] = mk_pairs(kD)
    shared['d_pr'] = mk_pairs(Dv)
    for k, v in T.items():
        shared['tab_' + k] = v
    u = np.asarray(ins['L0_attn_u'], np.float32)
    per_core = []
    for c in range(8):
        b, h = c // 2, c % 2
        d = dict(shared)
        d['u_b'] = np.ascontiguousarray(u[b])
        d['u_own'] = np.ascontiguousarray(u[b, h * 2048:(h + 1) * 2048])
        d['i128c_own'] = T['i128c_h1'] if h else T['i128c_h0']
        d['i128sn_own'] = T['i128sn_h1'] if h else T['i128sn_h0']
        per_core.append(d)
    return per_core


def kernel(**ins):
    nc = build_nc()
    in_maps = _host_arrays(ins)
    res = run_bass_kernel_spmd(nc, in_maps, core_ids=list(range(8)))
    out = np.zeros((4, 4096, 1024), np.float32)
    for c in range(8):
        b, h = c // 2, c % 2
        out[b, h * 2048:(h + 1) * 2048] = res.results[c]["out_own"]
    return out


